# version 08
# baseline (speedup 1.0000x reference)
import numpy as np
import ml_dtypes
import concourse.bass as bass
import concourse.mybir as mybir
from concourse.bass import IndirectOffsetOnAxis
from concourse.bass_utils import run_bass_kernel_spmd

F32 = mybir.dt.float32
BF16 = mybir.dt.bfloat16
I32 = mybir.dt.int32
U32 = mybir.dt.uint32
AF = mybir.ActivationFunctionType
ALU = mybir.AluOpType
AX = mybir.AxisListType.X

B, T, C = 4, 4096, 1024
NG = C // 128
TT = 512
HT = T // 2
NTC = HT // TT
EPS = 1e-5
NCORES = 8
BISECT_ITERS = 48


class Buf:
    def __init__(self, ap):
        self.ap = ap
        self.writer = None
        self.readers = []

    def __getitem__(self, idx):
        return self.ap[idx]


class Sched:

    def __init__(self, nc):
        self.nc = nc
        self.ops = []
        self.sems = {}
        self.ctx = []
        self._dmasems = {}
        self._outsem = None
        self._nsem = 0
        self._scopes = []

    def sem(self, name, dma=False):
        h = self.nc.semaphore(name).__enter__()
        s = {"h": h, "tot": 0, "name": name, "dma": dma}
        self.sems[name] = s
        self._nsem += 1
        return s

    def esem(self, engine):
        name = "S_" + engine
        if name not in self.sems:
            self.sem(name)
        return self.sems[name]

    DMAPOOL = 28

    def dmasem(self, buf):
        if buf is None:
            if self._outsem is None:
                self._outsem = self.sem("D_out", dma=True)
            return self._outsem
        k = id(buf)
        if k not in self._dmasems:
            i = len(self._dmasems)
            if i < self.DMAPOOL:
                s = self.sem(f"D_{i}", dma=True)
            else:
                s = self.sems[f"D_{i % self.DMAPOOL}"]
            self._dmasems[k] = s
        return self._dmasems[k]

    def op(self, engine, fn, reads=(), writes=(), after=(), dma=False,
           dmabuf=None, noww=False, inc_amt=None):
        waits = {}

        def add(tok):
            if tok is None:
                return
            s, v = tok
            if s["dma"]:
                v = s["tot"]
            k = s["name"]
            waits[k] = (s, max(v, waits[k][1]) if k in waits else v)

        for t in after:
            add(t)
        for b in reads:
            add(b.writer)
        for b in writes:
            if not noww:
                add(b.writer)
            for t in b.readers:
                add(t)
        if dma:
            db = dmabuf if dmabuf is not None else (writes[0] if writes else None)
            s = self.dmasem(db)
            amt = 16 if inc_amt is None else inc_amt
        else:
            s = self.esem(engine)
            amt = 1
        s["tot"] += amt
        tok = (s, s["tot"])
        wl = [(ws, wv) for (ws, wv) in waits.values() if not (ws is s and wv >= tok[1])]
        self.ops.append((engine, fn, wl, (s, amt)))
        for b in reads:
            b.readers.append(tok)
        for b in writes:
            b.writer = tok
            b.readers = []
        return tok

    def dma(self, out, in_, reads=(), writes=(), after=(), engine="sync",
            noww=False):
        return self.op(engine, lambda e: e.dma_start(out=out, in_=in_),
                       reads=reads, writes=writes, after=after, dma=True,
                       noww=noww)

    def emit_and_close(self):
        lim = getattr(self, "limit", None)
        if lim is not None and lim < len(self.ops):
            self.ops = self.ops[:lim]
            tot = {}
            for (e, fn, waits, inc) in self.ops:
                s, amt = inc
                tot[s["name"]] = tot.get(s["name"], 0) + amt
            for s in self.sems.values():
                s["tot"] = tot.get(s["name"], 0)
        with self.nc.Block() as block:
            def mk(engname):
                def run(eng):
                    for (e, fn, waits, inc) in self.ops:
                        if e != engname:
                            continue
                        for (s, v) in waits:
                            eng.wait_ge(s["h"], v)
                        inst = fn(eng)
                        inst.then_inc(inc[0]["h"], inc[1])
                return run

            finals = [(s, s["tot"]) for s in self.sems.values()]
            self.op("sync", lambda e: e.nop(), after=finals)

            for e in ("sync", "vector", "scalar", "tensor", "gpsimd"):
                if any(o[0] == e for o in self.ops):
                    getattr(block, e)(mk(e))
        for h in reversed(self.ctx):
            pass
        for s in self.sems.values():
            pass

    def sb(self, name, shape, dtype):
        cm = self.nc.sbuf_tensor(getattr(self, "prefix", "") + name, shape, dtype)
        h = cm.__enter__()
        if self._scopes:
            self._scopes[-1].append(cm)
        else:
            self.ctx.append(cm)
        return Buf(h[:])

    def ps(self, name, shape, dtype=F32):
        cm = self.nc.psum_tensor(getattr(self, "prefix", "") + name, shape, dtype)
        h = cm.__enter__()
        if self._scopes:
            self._scopes[-1].append(cm)
        else:
            self.ctx.append(cm)
        return Buf(h[:])

    def push_scope(self):
        self._scopes.append([])

    def fence(self):
        finals = [(s, s["tot"]) for s in self.sems.values()]
        for e in ("sync", "vector", "scalar", "tensor", "gpsimd"):
            self.op(e, lambda en: en.nop(), after=finals)

    def pop_scope(self):
        self.fence()
        for cm in reversed(self._scopes.pop()):
            cm.__exit__(None, None, None)


def host_fm(x):
    return np.ascontiguousarray(np.swapaxes(np.asarray(x), -1, -2))


def host_vec8(v):
    return np.ascontiguousarray(np.asarray(v, np.float32).reshape(NG, 128).T)


def consts_host():
    return {
        "c_ident_bf": np.eye(128, dtype=ml_dtypes.bfloat16),
        "c_ident_f": np.eye(128, dtype=np.float32),
        "c_ones_row_f": np.ones((1, 128), np.float32),
        "c_ones_col_f": np.ones((128, 1), np.float32),
        "c_mean_col_bf": np.full((128, 1), 1.0 / C, ml_dtypes.bfloat16),
        "c_iota16": np.ascontiguousarray(
            np.broadcast_to(np.arange(16, dtype=np.float32), (128, 16))),
        "c_iotam64": np.ascontiguousarray(
            np.broadcast_to(np.arange(16, dtype=np.float32) - 64.0, (128, 16))),
    }


def declare_consts(nc):
    d = {}
    d["ident_bf"] = nc.dram_tensor("c_ident_bf", [128, 128], BF16, kind="ExternalInput").ap()
    d["ident_f"] = nc.dram_tensor("c_ident_f", [128, 128], F32, kind="ExternalInput").ap()
    d["ones_row_f"] = nc.dram_tensor("c_ones_row_f", [1, 128], F32, kind="ExternalInput").ap()
    d["ones_col_f"] = nc.dram_tensor("c_ones_col_f", [128, 1], F32, kind="ExternalInput").ap()
    d["mean_col_bf"] = nc.dram_tensor("c_mean_col_bf", [128, 1], BF16, kind="ExternalInput").ap()
    d["iota16"] = nc.dram_tensor("c_iota16", [128, 16], F32, kind="ExternalInput").ap()
    return d


class CT:
    def __init__(self, sc, co):
        self.identf = sc.sb("ct_identf", [128, 128], F32)
        self.identbf = sc.sb("ct_identbf", [128, 128], BF16)
        self.onesrow = sc.sb("ct_onesrow", [1, 128], F32)
        self.onescol = sc.sb("ct_onescol", [128, 1], F32)
        self.meancol = sc.sb("ct_meancol", [128, 1], BF16)
        self.iota16 = sc.sb("ct_iota16", [128, 16], F32)
        self.keps = sc.sb("ct_keps", [1, 1], F32)
        sc.op("vector", lambda e: e.memset(self.keps[:], float(EPS)),
              writes=[self.keps])
        sc.dma(self.identf[:], co["ident_f"], writes=[self.identf])
        sc.dma(self.identbf[:], co["ident_bf"], writes=[self.identbf])
        sc.dma(self.onesrow[:], co["ones_row_f"], writes=[self.onesrow])
        sc.dma(self.onescol[:], co["ones_col_f"], writes=[self.onescol])
        sc.dma(self.meancol[:], co["mean_col_bf"], writes=[self.meancol])
        sc.dma(self.iota16[:], co["iota16"], writes=[self.iota16])


class LNScratch:
    def __init__(self, sc, tag, n):
        self.n = n
        self.bf = sc.sb(f"{tag}_bf", [128, NG * n], BF16)
        self.sq0 = sc.sb(f"{tag}_sq0", [128, n], BF16)
        self.sq1 = sc.sb(f"{tag}_sq1", [128, n], BF16)
        self.st = sc.sb(f"{tag}_st", [1, 4 * n], F32)


def emit_ln(sc, ct, y, out, n, lns, psA, psB, psR, psN, gamma=None, beta=None,
            cast_out_bf=None):
    st = lns.st
    mu = st[0:1, 0:n]
    rstd = st[0:1, n:2 * n]
    nu = st[0:1, 2 * n:3 * n]
    var = st[0:1, 3 * n:4 * n]
    bf = lns.bf

    for g in range(NG):
        sc.op("scalar", lambda e, g=g: e.copy(bf[:, g * n:(g + 1) * n],
                                              y[:, g * n:(g + 1) * n]),
              reads=[y], writes=[bf])

    def mk_mu(e):
        for g in range(NG):
            i = e.matmul(psA[0:1, 0:n], ct.meancol[:], bf[:, g * n:(g + 1) * n],
                         start=(g == 0), stop=(g == NG - 1))
        return i

    sc.op("tensor", mk_mu, reads=[bf, ct.meancol], writes=[psA])

    sqs = [lns.sq0, lns.sq1]
    for g in range(NG):
        sq = sqs[g % 2]
        sc.op("scalar", lambda e, g=g, sq=sq: e.activation(
            sq[:, :], bf[:, g * n:(g + 1) * n], AF.Square), reads=[bf], writes=[sq])
        sc.op("tensor", lambda e, g=g, sq=sq: e.matmul(
            psB[0:1, 0:n], ct.meancol[:], sq[:, :],
            start=(g == 0), stop=(g == NG - 1)), reads=[sq, ct.meancol], writes=[psB])

    sc.op("vector", lambda e: e.tensor_copy(mu, psA[0:1, 0:n]),
          reads=[psA], writes=[st])
    sc.op("scalar", lambda e: e.activation(var, mu, AF.Square),
          reads=[st], writes=[st])
    sc.op("vector", lambda e: e.tensor_sub(var, psB[0:1, 0:n], var),
          reads=[psB, st], writes=[st])
    sc.op("scalar", lambda e: e.activation(rstd, var, AF.Ln,
                                           bias=ct.keps[0:1, 0:1]),
          reads=[st, ct.keps], writes=[st])
    sc.op("scalar", lambda e: e.activation(rstd, rstd, AF.Exp, scale=-0.5),
          reads=[st], writes=[st])
    sc.op("vector", lambda e: e.tensor_mul(nu, mu, rstd), reads=[st], writes=[st])
    sc.op("tensor", lambda e: e.matmul(psR[0:128, 0:n], ct.onesrow[:], rstd,
                                       start=True, stop=True),
          reads=[st, ct.onesrow], writes=[psR])
    sc.op("tensor", lambda e: e.matmul(psN[0:128, 0:n], ct.onesrow[:], nu,
                                       start=True, stop=True),
          reads=[st, ct.onesrow], writes=[psN])
    toks = []
    for g in range(NG):
        sl = slice(g * n, (g + 1) * n)
        sc.op("vector", lambda e, sl=sl: e.tensor_mul(out[:, sl], y[:, sl],
                                                      psR[0:128, 0:n]),
              reads=[y, psR], writes=[out])
        t = sc.op("vector", lambda e, sl=sl: e.tensor_sub(out[:, sl], out[:, sl],
                                                          psN[0:128, 0:n]),
                  reads=[psN, out], writes=[out])
        if gamma is not None:
            t = sc.op("vector", lambda e, sl=sl, g=g: e.tensor_scalar(
                out[:, sl], out[:, sl], gamma[:, g:g + 1], beta[:, g:g + 1],
                op0=ALU.mult, op1=ALU.add), reads=[gamma, beta, out], writes=[out])
        if cast_out_bf is not None:
            sc.op("scalar", lambda e, sl=sl: e.copy(cast_out_bf[:, sl], out[:, sl]),
                  reads=[out], writes=[cast_out_bf])
        toks.append(t)
    return toks


def emit_phase1(nc, sc, ct, E):
    x = E["x"].ap
    xrev = E["xrev"].ap
    rwv = E["rw"].ap
    rbv = E["rb"].ap
    hsel = E["hsel"].ap
    x1b = E["x1"]
    x1o = x1b.ap
    srw = nc.dram_tensor("p1_srw", [1, T], F32).ap()
    srm = nc.dram_tensor("p1_srm", [1, T], F32).ap()
    srwb = Buf(srw)
    srmb = Buf(srm)

    rwvt = sc.sb("rwvt", [128, NG], F32)
    rbt = sc.sb("rbt", [128, 1], F32)
    hselt = sc.sb("hselt", [1, 2], F32)
    xa = sc.sb("xa", [128, NG * TT], F32)
    xb = sc.sb("xb", [128, NG * TT], F32)
    xr0 = sc.sb("xr0", [128, NG * TT], F32)
    rwrow = sc.sb("rwrow", [1, T], F32)
    rwm = sc.sb("rwm", [128, 32], F32)
    flags = sc.sb("flags", [128, 32], F32)
    eqm = sc.sb("eqm", [128, 32], F32)
    cums = sc.sb("cums", [128, 32], F32)
    maskm = sc.sb("maskm", [128, 32], F32)
    fred = sc.sb("fred", [128, 1], F32)
    rowt = sc.sb("rowt", [128, 1], F32)
    midb = sc.sb("midb", [128, 1], F32)
    tstarb = sc.sb("tstarb", [128, 1], F32)
    mbc = sc.sb("mbc", [128, 1], F32)
    prefc = sc.sb("prefc", [128, 1], F32)
    sv = sc.sb("sv", [1, 16], F32)
    prow = sc.sb("prow", [1, 128], F32)
    prow2 = sc.sb("prow2", [1, 128], F32)
    mrow = sc.sb("mrow", [1, TT], F32)
    maskbc = sc.sb("maskbc", [128, TT], F32)
    ytile = sc.sb("ytile", [128, NG * TT], F32)
    lns = LNScratch(sc, "p1ln", TT)

    p0 = sc.ps("p0", [128, TT])
    p1 = sc.ps("p1", [128, TT])
    p2 = sc.ps("p2", [128, TT])
    p3 = sc.ps("p3", [128, TT])
    p4 = sc.ps("p4", [128, TT])
    p5 = sc.ps("p5", [128, TT])

    sc.dma(rwvt[:], rwv, writes=[rwvt])
    sc.dma(rbt[:], rbv, writes=[rbt])
    sc.dma(hselt[:], hsel, writes=[hselt])

    xt = [xa, xb]
    for tc in range(T // TT):
        xx = xt[tc % 2]
        for g in range(NG):
            sc.dma(xx[:, g * TT:(g + 1) * TT],
                   x[g * 128:(g + 1) * 128, tc * TT:(tc + 1) * TT], writes=[xx])

        def mmrw(e, xx=xx):
            for g in range(NG):
                i = e.matmul(p0[0:1, 0:TT], rwvt[:, g:g + 1],
                             xx[:, g * TT:(g + 1) * TT],
                             start=(g == 0), stop=(g == NG - 1))
            return i
        sc.op("tensor", mmrw, reads=[xx, rwvt], writes=[p0])
        sc.op("vector", lambda e, tc=tc: e.tensor_copy(
            rwrow[0:1, tc * TT:(tc + 1) * TT], p0[0:1, 0:TT]),
            reads=[p0], writes=[rwrow])

    sc.dma(srw, rwrow[:], reads=[rwrow], writes=[srwb])
    sc.dma(rwm[:], srw.rearrange("o (p j) -> (o p) j", p=128),
           reads=[srwb], writes=[rwm])
    sc.op("vector", lambda e: e.tensor_scalar_add(rwm[:], rwm[:], rbt[:]),
          reads=[rbt], writes=[rwm])

    sc.op("vector", lambda e: e.tensor_reduce(fred[:], rwm[:], axis=AX, op=ALU.min),
          reads=[rwm], writes=[fred])
    sc.op("tensor", lambda e: e.matmul(p1[0:1, 0:128], fred[:], ct.identf[:],
                                       start=True, stop=True),
          reads=[fred, ct.identf], writes=[p1])
    sc.op("vector", lambda e: e.tensor_reduce(sv[0:1, 0:1], p1[0:1, 0:128],
                                              axis=AX, op=ALU.min),
          reads=[p1], writes=[sv])
    sc.op("vector", lambda e: e.tensor_reduce(fred[:], rwm[:], axis=AX, op=ALU.max),
          reads=[rwm], writes=[fred])
    sc.op("tensor", lambda e: e.matmul(p1[0:1, 0:128], fred[:], ct.identf[:],
                                       start=True, stop=True),
          reads=[fred, ct.identf], writes=[p1])
    sc.op("vector", lambda e: e.tensor_reduce(sv[0:1, 1:2], p1[0:1, 0:128],
                                              axis=AX, op=ALU.max),
          reads=[p1], writes=[sv])

    for it in range(BISECT_ITERS):
        sc.op("vector", lambda e: e.tensor_scalar(
            sv[0:1, 2:3], sv[0:1, 0:1], sv[0:1, 1:2], 0.5,
            op0=ALU.add, op1=ALU.mult), reads=[], writes=[sv])
        sc.op("tensor", lambda e: e.matmul(p1[0:128, 0:1], ct.onesrow[:],
                                           sv[0:1, 2:3], start=True, stop=True),
              reads=[sv, ct.onesrow], writes=[p1])
        sc.op("vector", lambda e: e.tensor_copy(midb[:], p1[0:128, 0:1]),
              reads=[p1], writes=[midb])
        sc.op("vector", lambda e: e.tensor_scalar(
            flags[:], rwm[:], midb[:], 0.0, op0=ALU.is_ge, op1=ALU.add,
            accum_out=fred[:]), reads=[rwm, midb], writes=[flags, fred])
        sc.op("tensor", lambda e: e.matmul(p1[0:1, 1:2], fred[:], ct.onescol[:],
                                           start=True, stop=True),
              reads=[fred, ct.onescol], writes=[p1])
        sc.op("vector", lambda e: e.tensor_copy(sv[0:1, 3:4], p1[0:1, 1:2]),
              reads=[p1], writes=[sv])
        sc.op("vector", lambda e: e.tensor_scalar(
            sv[0:1, 4:5], sv[0:1, 3:4], 1024.0, None, op0=ALU.is_ge),
            reads=[], writes=[sv])
        sc.op("vector", lambda e: e.tensor_scalar(
            sv[0:1, 7:8], sv[0:1, 2:3], sv[0:1, 0:1], None, op0=ALU.subtract),
            reads=[], writes=[sv])
        sc.op("vector", lambda e: e.scalar_tensor_tensor(
            sv[0:1, 0:1], sv[0:1, 7:8], sv[0:1, 4:5], sv[0:1, 0:1],
            op0=ALU.mult, op1=ALU.add), reads=[], writes=[sv])
        sc.op("vector", lambda e: e.tensor_scalar(
            sv[0:1, 8:9], sv[0:1, 1:2], sv[0:1, 2:3], None, op0=ALU.subtract),
            reads=[], writes=[sv])
        sc.op("vector", lambda e: e.scalar_tensor_tensor(
            sv[0:1, 1:2], sv[0:1, 8:9], sv[0:1, 4:5], sv[0:1, 2:3],
            op0=ALU.mult, op1=ALU.add), reads=[], writes=[sv])

    sc.op("tensor", lambda e: e.matmul(p1[0:128, 0:1], ct.onesrow[:],
                                       sv[0:1, 0:1], start=True, stop=True),
          reads=[sv, ct.onesrow], writes=[p1])
    sc.op("vector", lambda e: e.tensor_copy(tstarb[:], p1[0:128, 0:1]),
          reads=[p1], writes=[tstarb])
    sc.op("vector", lambda e: e.tensor_scalar(
        flags[:], rwm[:], tstarb[:], 0.0, op0=ALU.is_gt, op1=ALU.add,
        accum_out=fred[:]), reads=[rwm, tstarb], writes=[flags, fred])
    sc.op("tensor", lambda e: e.matmul(p1[0:1, 1:2], fred[:], ct.onescol[:],
                                       start=True, stop=True),
          reads=[fred, ct.onescol], writes=[p1])
    sc.op("vector", lambda e: e.tensor_copy(sv[0:1, 5:6], p1[0:1, 1:2]),
          reads=[p1], writes=[sv])
    sc.op("vector", lambda e: e.tensor_scalar(
        sv[0:1, 6:7], sv[0:1, 5:6], -1.0, 1024.0, op0=ALU.mult, op1=ALU.add),
        reads=[], writes=[sv])
    sc.op("vector", lambda e: e.tensor_scalar(
        eqm[:], rwm[:], tstarb[:], 0.0, op0=ALU.is_equal, op1=ALU.add,
        accum_out=rowt[:]), reads=[rwm, tstarb], writes=[eqm, rowt])
    sc.op("vector", lambda e: e.tensor_tensor_scan(
        cums[:], ct.onescol.ap.to_broadcast((128, 32)), eqm[:], 0.0,
        op0=ALU.mult, op1=ALU.add), reads=[ct.onescol, eqm], writes=[cums])
    sc.op("tensor", lambda e: e.matmul(p1[0:1, 0:128], rowt[:], ct.identf[:],
                                       start=True, stop=True),
          reads=[rowt, ct.identf], writes=[p1])
    sc.op("vector", lambda e: e.tensor_copy(prow[:], p1[0:1, 0:128]),
          reads=[p1], writes=[prow])
    sc.op("vector", lambda e: e.tensor_tensor_scan(
        prow2[:], ct.onesrow[:], prow[:], 0.0, op0=ALU.mult, op1=ALU.add),
        reads=[ct.onesrow, prow], writes=[prow2])
    sc.op("vector", lambda e: e.tensor_sub(prow2[:], prow2[:], prow[:]),
          reads=[prow], writes=[prow2])
    sc.op("tensor", lambda e: e.matmul(p1[0:128, 2:3], prow2[:],
                                       ct.onesrow[0:1, 0:1], start=True, stop=True),
          reads=[prow2, ct.onesrow], writes=[p1])
    sc.op("vector", lambda e: e.tensor_copy(prefc[:], p1[0:128, 2:3]),
          reads=[p1], writes=[prefc])
    sc.op("vector", lambda e: e.tensor_scalar_add(cums[:], cums[:], prefc[:]),
          reads=[prefc], writes=[cums])
    sc.op("tensor", lambda e: e.matmul(p1[0:128, 3:4], ct.onesrow[:],
                                       sv[0:1, 6:7], start=True, stop=True),
          reads=[sv, ct.onesrow], writes=[p1])
    sc.op("vector", lambda e: e.tensor_copy(mbc[:], p1[0:128, 3:4]),
          reads=[p1], writes=[mbc])
    sc.op("vector", lambda e: e.tensor_scalar(
        cums[:], cums[:], mbc[:], None, op0=ALU.is_le), reads=[mbc], writes=[cums])
    sc.op("vector", lambda e: e.tensor_mul(eqm[:], eqm[:], cums[:]),
          reads=[cums], writes=[eqm])
    sc.op("vector", lambda e: e.tensor_add(maskm[:], flags[:], eqm[:]),
          reads=[flags, eqm], writes=[maskm])
    sc.dma(srm.rearrange("o (p j) -> (o p) j", p=128), maskm[:],
           reads=[maskm], writes=[srmb])

    xh = E["xh"].ap
    mrowsel = sc.sb("mrowsel", [1, 2 * TT], F32)
    for tc in range(NTC):
        xx = xt[tc % 2]
        for g in range(NG):
            sc.dma(xx[:, g * TT:(g + 1) * TT],
                   xh[g * 128:(g + 1) * 128, tc * TT:(tc + 1) * TT], writes=[xx])
        for g in range(NG):
            sc.dma(xr0[:, g * TT:(g + 1) * TT],
                   xrev[g * 128:(g + 1) * 128, tc * TT:(tc + 1) * TT], writes=[xr0])
        sc.dma(mrowsel[0:1, 0:TT], srm[0:1, tc * TT:(tc + 1) * TT],
               reads=[srmb], writes=[mrowsel])
        sc.dma(mrowsel[0:1, TT:2 * TT], srm[0:1, HT + tc * TT:HT + (tc + 1) * TT],
               reads=[srmb], writes=[mrowsel])
        sc.op("vector", lambda e: e.tensor_scalar(
            mrow[:], mrowsel[0:1, 0:TT], hselt[0:1, 0:1], None, op0=ALU.mult),
            reads=[mrowsel, hselt], writes=[mrow])
        sc.op("vector", lambda e: e.scalar_tensor_tensor(
            mrow[:], mrowsel[0:1, TT:2 * TT], hselt[0:1, 1:2], mrow[:],
            op0=ALU.mult, op1=ALU.add), reads=[mrowsel, hselt], writes=[mrow])
        sc.op("tensor", lambda e: e.matmul(p2[0:128, 0:TT], ct.onesrow[:],
                                           mrow[:], start=True, stop=True),
              reads=[mrow, ct.onesrow], writes=[p2])
        sc.op("vector", lambda e: e.tensor_copy(maskbc[:], p2[0:128, 0:TT]),
              reads=[p2], writes=[maskbc])
        for g in range(NG):
            sl = slice(g * TT, (g + 1) * TT)
            sc.op("vector", lambda e, sl=sl: e.tensor_mul(
                ytile[:, sl], xr0[:, sl], maskbc[:]),
                reads=[xr0, maskbc], writes=[ytile])
            sc.op("vector", lambda e, sl=sl, xx=xx: e.tensor_add(
                ytile[:, sl], ytile[:, sl], xx[:, sl]),
                reads=[xx], writes=[ytile])
        emit_ln(sc, ct, ytile, ytile, TT, lns, p3, p4, p5, p2)
        for g in range(NG):
            sc.dma(x1o[g * 128:(g + 1) * 128, tc * TT:(tc + 1) * TT],
                   ytile[:, g * TT:(g + 1) * TT], reads=[ytile], writes=[x1b],
                   engine="gpsimd", noww=True)


def build_phase1():
    nc = bass.Bass("TRN2", target_bir_lowering=False)
    E = {}
    E["x"] = Buf(nc.dram_tensor("p1_x", [C, T], F32, kind="ExternalInput").ap())
    E["xh"] = Buf(nc.dram_tensor("p1_xh", [C, HT], F32, kind="ExternalInput").ap())
    E["xrev"] = Buf(nc.dram_tensor("p1_xrev", [C, HT], F32, kind="ExternalInput").ap())
    E["rw"] = Buf(nc.dram_tensor("p1_rw", [128, NG], F32, kind="ExternalInput").ap())
    E["rb"] = Buf(nc.dram_tensor("p1_rb", [128, 1], F32, kind="ExternalInput").ap())
    E["hsel"] = Buf(nc.dram_tensor("p1_hsel", [1, 2], F32, kind="ExternalInput").ap())
    E["x1"] = Buf(nc.dram_tensor("p1_x1", [C, HT], F32, kind="ExternalOutput").ap())
    co = declare_consts(nc)
    sc = Sched(nc)
    ct = CT(sc, co)
    emit_phase1(nc, sc, ct, E)
    sc.emit_and_close()
    return nc


MOE_DEBUG = 0
MOE_STAGE = 99
MOE_POSLIM = 99


def emit_moe(nc, sc, ct, EX, tag, E, IH, CAP, renorm, shared):
    assert IH % 128 == 0 or IH == 1408
    NIH = (IH + 127) // 128
    IHS = []
    o = 0
    while o < IH:
        w = min(512, IH - o)
        IHS.append((o, o + w))
        o += w
    NTILES = HT // 128
    SLOTS = E * CAP
    NST = CAP // 128

    xB = EX["x"]
    x = xB.ap
    rw = EX["rw"].ap
    wg = EX["wg"].ap
    wu = EX["wu"].ap
    wd = EX["wd"].ap
    if shared:
        wgs = EX["wgs"].ap
        wus = EX["wus"].ap
        wds = EX["wds"].ap
    iom = EX["iotam64"].ap
    outB = EX["o"]
    out_o = outB.ap
    xg_d = Buf(nc.dram_tensor(f"{tag}_xg", [SLOTS, C], BF16).ap())
    yb_d = Buf(nc.dram_tensor(f"{tag}_yb", [SLOTS, C], BF16).ap())
    zt_ = sc_zt_holder = None
    if MOE_DEBUG:
        dbg_a1 = nc.dram_tensor(f"{tag}_dbg_a1", [128, HT // 128], I32,
                                kind="ExternalOutput").ap()
        dbg_a2 = nc.dram_tensor(f"{tag}_dbg_a2", [128, HT // 128], I32,
                                kind="ExternalOutput").ap()
        dbg_idx = nc.dram_tensor(f"{tag}_dbg_idx", [128, 2 * (HT // 128)], F32,
                                 kind="ExternalOutput").ap()
        dbg_pos = nc.dram_tensor(f"{tag}_dbg_pos", [128, 2 * (HT // 128)], F32,
                                 kind="ExternalOutput").ap()
        dbg_w = nc.dram_tensor(f"{tag}_dbg_w", [128, 2 * (HT // 128)], F32,
                               kind="ExternalOutput").ap()

    iotam = sc.sb("iotam", [128, 16], F32)
    sc.dma(iotam[:], iom, writes=[iotam])
    zt = sc.sb("zt", [128, C], BF16)
    sc.op("vector", lambda e: e.memset(zt[:], 0.0), writes=[zt])
    for j in range(SLOTS // 128):
        sc.dma(xg_d.ap[j * 128:(j + 1) * 128, :], zt[:], reads=[zt],
               writes=[xg_d], noww=True,
               engine=("sync", "gpsimd")[j % 2])

    xbf = sc.sb("xbf", [128, NG * HT], BF16)
    xa = sc.sb("xa", [128, NG * TT], F32)
    xb = xa if IH > 1024 else sc.sb("xb", [128, NG * TT], F32)
    rwt = sc.sb("rwt", [128, NG * E], F32)
    ltile = sc.sb("ltile", [128, 16], F32)
    m8 = sc.sb("m8", [128, 8], F32)
    nmax = sc.sb("nmax", [128, 1], F32)
    den = sc.sb("den", [128, 1], F32)
    rd = sc.sb("rd", [128, 1], F32)
    et = sc.sb("et", [128, 16], F32)
    e12 = sc.sb("e12", [128, 2], F32)
    eqt = sc.sb("eqt", [128, 16], F32)
    candt = sc.sb("candt", [128, 16], F32)
    ohbf = sc.sb("ohbf", [128, 16], BF16)
    ohT = sc.sb("ohT", [16, 128], F32)
    cumT = sc.sb("cumT", [16, 128], F32)
    runk = [sc.sb("run1", [16, 1], F32), sc.sb("run2", [16, 1], F32)]
    poscol = sc.sb("poscol", [128, 1], F32)
    rowred = sc.sb("rowred", [16, 1], F32)
    idxA = [sc.sb("idx1A", [128, NTILES], F32), sc.sb("idx2A", [128, NTILES], F32)]
    posA = [sc.sb("pos1A", [128, NTILES], F32), sc.sb("pos2A", [128, NTILES], F32)]
    wA = sc.sb("wA", [128, 2 * NTILES], F32)
    adrI = [sc.sb("a1I", [128, NTILES], I32), sc.sb("a2I", [128, NTILES], I32)]
    cnt1bc = sc.sb("cnt1bc", [128, 16], F32)
    cnt1row = sc.sb("cnt1row", [1, 16], F32)
    x1tm = sc.sb("x1tm", [128, C], BF16)
    adrF = sc.sb("adrF", [128, 2], F32)

    wgbf = sc.sb("wgbf", [128, NIH * 128 * 8 // 8], BF16)
    sc.ctx.pop()
    wgbf = sc.sb("wgbf2", [128, NG * IH], BF16)
    wubf = sc.sb("wubf", [128, NG * IH], BF16)
    wdbf = sc.sb("wdbf", [128, NIH * C], BF16)
    WMX = max(IH, C)
    NWT = 2 if IH > 1024 else 4
    wtmps = [sc.sb(f"wtmp{i}", [128, WMX], F32) for i in range(NWT)]
    wtmp = wtmps[0]
    wtmp2 = wtmps[1]
    _wrot = [0]

    def stage_w(dst_buf, dst_sl, src_ap, n):
        i = _wrot[0] % NWT
        _wrot[0] += 1
        wt = wtmps[i]
        deng = ("sync", "gpsimd", "scalar", "sync")[i]
        ceng = ("scalar", "vector", "gpsimd", "scalar")[i]
        sc.dma(wt[:, 0:n], src_ap, writes=[wt], engine=deng)
        if ceng == "scalar":
            sc.op(ceng, lambda e, wt=wt, dst_sl=dst_sl, n=n: e.copy(
                dst_buf[:, dst_sl], wt[:, 0:n]), reads=[wt], writes=[dst_buf])
        else:
            sc.op(ceng, lambda e, wt=wt, dst_sl=dst_sl, n=n: e.tensor_copy(
                dst_buf[:, dst_sl], wt[:, 0:n]), reads=[wt], writes=[dst_buf])
    xgt = [sc.sb(f"xgt{j}", [128, C], BF16) for j in range(2)]
    xgf = [sc.sb(f"xgf{j}", [128, NG * 128], BF16) for j in range(2)]
    hsil = sc.sb("hsil", [128, IH], F32)
    htm = sc.sb("htm", [128, IH], BF16)
    hfm = sc.sb("hfm", [128, NIH * 128], BF16)
    ytm = sc.sb("ytm", [128, C], BF16)
    y1 = sc.sb("y1", [128, C], BF16)
    y2 = sc.sb("y2", [128, C], BF16)
    ycomb = sc.sb("ycomb", [128, C], F32)
    ycbf = sc.sb("ycbf", [128, C], BF16)
    yfm = sc.sb("yfm", [128, NG * TT], F32)
    res = yfm
    lns = LNScratch(sc, f"{tag}ln", TT)
    if shared:
        wgsbf = wgbf
        wusbf = wubf
        wdsbf = wdbf

    pp = [sc.ps(f"pp{i}", [128, TT]) for i in range(6)]
    pt = sc.ps("pt", [128, TT], BF16)
    ptf = sc.ps("ptf", [128, TT])

    for k in range(2):
        sc.op("vector", lambda e, k=k: e.memset(runk[k][:], 0.0), writes=[runk[k]])

    sc.dma(rwt.ap.rearrange("p (g e) -> p g e", g=NG),
           rw.rearrange("(g p) e -> p g e", p=128), writes=[rwt])
    xt = [xa, xb]
    for tc in range(NTC):
        xx = xt[tc % 2]
        for g in range(NG):
            sc.dma(xx[:, g * TT:(g + 1) * TT],
                   x[g * 128:(g + 1) * 128, tc * TT:(tc + 1) * TT],
                   reads=[xB], writes=[xx])
        for g in range(NG):
            sc.op("scalar", lambda e, g=g, tc=tc, xx=xx: e.copy(
                xbf[:, g * HT + tc * TT:g * HT + (tc + 1) * TT],
                xx[:, g * TT:(g + 1) * TT]), reads=[xx], writes=[xbf])
        for ti in range(4):
            i = tc * 4 + ti
            if MOE_STAGE < 12:
                continue

            def mml(e, xx=xx, ti=ti, i=i):
                for g in range(NG):
                    r = e.matmul(ptf[0:128, (i % 4) * 128:(i % 4) * 128 + E],
                                 xx[:, g * TT + ti * 128:g * TT + (ti + 1) * 128],
                                 rwt[:, g * E:(g + 1) * E],
                                 start=(g == 0), stop=(g == NG - 1))
                return r
            sc.op("tensor", mml, reads=[xx, rwt], writes=[ptf])
            sc.op("vector", lambda e, i=i: e.tensor_copy(
                ltile[:, 0:E], ptf[0:128, (i % 4) * 128:(i % 4) * 128 + E]),
                reads=[ptf], writes=[ltile])
            if MOE_STAGE < 13:
                continue
            sc.op("vector", lambda e: e.max(out=m8[:], in_=ltile[:, 0:8] if E == 8
                                            else ltile[:, 0:16]),
                  reads=[ltile], writes=[m8])
            sc.op("vector", lambda e: e.reduce_max(nmax[:], ltile[:, 0:E], axis=AX,
                                                   negate=True),
                  reads=[ltile], writes=[nmax])
            sc.op("scalar", lambda e: e.activation(et[:, 0:E], ltile[:, 0:E], AF.Exp,
                                                   bias=nmax[:], accum_out=den[:]),
                  reads=[ltile, nmax], writes=[et, den])
            sc.op("scalar", lambda e: e.activation(e12[:], m8[:, 0:2], AF.Exp,
                                                   bias=nmax[:]),
                  reads=[m8, nmax], writes=[e12])
            sc.op("vector", lambda e: e.reciprocal(rd[:], den[:]),
                  reads=[den], writes=[rd])
            if renorm:
                sc.op("vector", lambda e: e.tensor_scalar_mul(e12[:], e12[:], rd[:]),
                      reads=[rd], writes=[e12])
                sc.op("vector", lambda e: e.tensor_scalar(
                    den[:], e12[:, 0:1], e12[:, 1:2], 1e-20,
                    op0=ALU.add, op1=ALU.add), reads=[e12], writes=[den])
                sc.op("vector", lambda e: e.reciprocal(rd[:], den[:]),
                      reads=[den], writes=[rd])
            t_w = sc.op("vector", lambda e, i=i: e.tensor_scalar_mul(
                wA[:, 2 * i:2 * i + 2], e12[:], rd[:]), reads=[e12, rd], writes=[wA])
            if MOE_STAGE < 14:
                continue
            sc.op("vector", lambda e: e.tensor_scalar(
                eqt[:, 0:E], ltile[:, 0:E], m8[:, 0:1], None, op0=ALU.is_equal),
                reads=[ltile, m8], writes=[eqt])
            sc.op("vector", lambda e: e.tensor_mul(candt[:, 0:E], eqt[:, 0:E],
                                                   iotam[:, 0:E]),
                  reads=[eqt, iotam], writes=[candt])
            sc.op("vector", lambda e, i=i: e.tensor_reduce(
                idxA[0][:, i:i + 1], candt[:, 0:E], axis=AX, op=ALU.min),
                reads=[candt], writes=[idxA[0]])
            sc.op("vector", lambda e, i=i: e.tensor_scalar_add(
                idxA[0][:, i:i + 1], idxA[0][:, i:i + 1], 64.0), writes=[idxA[0]])
            sc.op("vector", lambda e: e.tensor_scalar(
                eqt[:, 0:E], ltile[:, 0:E], m8[:, 1:2], None, op0=ALU.is_equal),
                reads=[ltile, m8], writes=[eqt])
            sc.op("vector", lambda e, i=i: e.tensor_scalar(
                candt[:, 0:E], iotam[:, 0:E], idxA[0][:, i:i + 1], None,
                op0=ALU.is_equal), reads=[iotam, idxA[0]], writes=[candt])
            sc.ops.pop()
            sc.esem("vector")["tot"] -= 1
            candt.readers = [r for r in candt.readers]
            sc.op("vector", lambda e, i=i: e.tensor_scalar(
                et[:, 0:E], iotam[:, 0:E], idxA[0][:, i:i + 1], -64.0,
                op0=ALU.subtract, op1=ALU.is_equal),
                reads=[iotam, idxA[0]], writes=[et])
            sc.op("vector", lambda e: e.tensor_mul(candt[:, 0:E], eqt[:, 0:E],
                                                   iotam[:, 0:E]),
                  reads=[eqt, iotam], writes=[candt])
            sc.op("vector", lambda e: e.scalar_tensor_tensor(
                candt[:, 0:E], et[:, 0:E], 64.0, candt[:, 0:E],
                op0=ALU.mult, op1=ALU.add), reads=[et], writes=[candt])
            sc.op("vector", lambda e, i=i: e.tensor_reduce(
                idxA[1][:, i:i + 1], candt[:, 0:E], axis=AX, op=ALU.min),
                reads=[candt], writes=[idxA[1]])
            sc.op("vector", lambda e, i=i: e.tensor_scalar_add(
                idxA[1][:, i:i + 1], idxA[1][:, i:i + 1], 64.0), writes=[idxA[1]])
            if MOE_STAGE < 15:
                continue
            for k in range(2):
                PL = [0]

                def pgate():
                    PL[0] += 1
                    return PL[0] <= MOE_POSLIM
                if pgate():
                    sc.op("vector", lambda e, i=i, k=k: e.tensor_scalar(
                        ohbf[:, 0:E], iotam[:, 0:E], idxA[k][:, i:i + 1], -64.0,
                        op0=ALU.subtract, op1=ALU.is_equal),
                        reads=[iotam, idxA[k]], writes=[ohbf])
                if pgate():
                    sc.op("tensor", lambda e: e.transpose(
                        pt[0:E, 0:128], ohbf[:, 0:E], ct.identbf[:]),
                        reads=[ohbf, ct.identbf], writes=[pt])
                if pgate():
                    sc.op("vector", lambda e: e.tensor_copy(ohT[0:E, :],
                                                            pt[0:E, 0:128]),
                          reads=[pt], writes=[ohT])
                if pgate():
                    sc.op("vector", lambda e: e.tensor_tensor_scan(
                        cumT[0:E, :], ct.onescol[0:E, 0:1].to_broadcast((E, 128)),
                        ohT[0:E, :], 0.0, op0=ALU.mult, op1=ALU.add),
                        reads=[ohT, ct.onescol], writes=[cumT])
                if pgate():
                    sc.op("vector", lambda e, k=k: e.tensor_scalar_add(
                        cumT[0:E, :], cumT[0:E, :], runk[k][0:E, :]),
                        reads=[runk[k]], writes=[cumT])
                if pgate():
                    sc.op("vector", lambda e: e.tensor_mul(cumT[0:E, :],
                                                           cumT[0:E, :],
                                                           ohT[0:E, :]),
                          reads=[ohT], writes=[cumT])
                if pgate():
                    sc.op("tensor", lambda e: e.matmul(
                        ptf[0:1, 128:256], ct.onescol[0:E, 0:1], cumT[0:E, :],
                        start=True, stop=True),
                        reads=[cumT, ct.onescol], writes=[ptf])
                if pgate():
                    sc.op("vector", lambda e: e.tensor_copy(ohT[0:1, 0:128],
                                                            ptf[0:1, 128:256]),
                          reads=[ptf], writes=[ohT])
                if pgate():
                    sc.op("tensor", lambda e: e.matmul(
                        ptf[0:128, 126:127], ohT[0:1, 0:128], ct.onesrow[0:1, 0:1],
                        start=True, stop=True), reads=[ohT, ct.onesrow], writes=[ptf])
                if pgate():
                    sc.op("vector", lambda e, i=i, k=k: e.tensor_scalar(
                        posA[k][:, i:i + 1], ptf[0:128, 126:127], -1.0, None,
                        op0=ALU.add), reads=[ptf], writes=[posA[k]])
                if pgate():
                    sc.op("vector", lambda e: e.tensor_reduce(
                        rowred[0:E, :], cumT[0:E, :], axis=AX, op=ALU.max),
                        reads=[cumT], writes=[rowred])
                if pgate():
                    sc.op("vector", lambda e, k=k: e.tensor_max(
                        runk[k][0:E, :], runk[k][0:E, :], rowred[0:E, :]),
                        reads=[rowred], writes=[runk[k]])


    if MOE_STAGE < 16:
        return

    sc.op("tensor", lambda e: e.matmul(ptf[0:1, 0:E], runk[0][0:E, :],
                                       ct.identf[0:E, 0:E], start=True, stop=True),
          reads=[runk[0], ct.identf], writes=[ptf])
    sc.op("vector", lambda e: e.tensor_copy(cnt1row[0:1, 0:E], ptf[0:1, 0:E]),
          reads=[ptf], writes=[cnt1row])
    sc.op("tensor", lambda e: e.matmul(ptf[0:128, 0:E], ct.onesrow[:],
                                       cnt1row[0:1, 0:E], start=True, stop=True),
          reads=[cnt1row, ct.onesrow], writes=[ptf])
    sc.op("vector", lambda e: e.tensor_copy(cnt1bc[:, 0:E], ptf[0:128, 0:E]),
          reads=[ptf], writes=[cnt1bc])

    for i in range(NTILES):
        sc.op("vector", lambda e, i=i: e.tensor_scalar_min(
            adrF[:, 0:1], posA[0][:, i:i + 1], float(CAP - 1)),
            reads=[posA[0]], writes=[adrF])
        sc.op("vector", lambda e, i=i: e.scalar_tensor_tensor(
            adrF[:, 0:1], idxA[0][:, i:i + 1], float(CAP), adrF[:, 0:1],
            op0=ALU.mult, op1=ALU.add), reads=[idxA[0]], writes=[adrF])
        sc.op("vector", lambda e, i=i: e.tensor_scalar(
            eqt[:, 0:E], iotam[:, 0:E], idxA[1][:, i:i + 1], -64.0,
            op0=ALU.subtract, op1=ALU.is_equal),
            reads=[iotam, idxA[1]], writes=[eqt])
        sc.op("vector", lambda e: e.tensor_mul(eqt[:, 0:E], eqt[:, 0:E],
                                               cnt1bc[:, 0:E]),
              reads=[cnt1bc], writes=[eqt])
        sc.op("vector", lambda e: e.tensor_reduce(den[:], eqt[:, 0:E], axis=AX,
                                                  op=ALU.add),
              reads=[eqt], writes=[den])
        sc.op("vector", lambda e, i=i: e.tensor_add(
            adrF[:, 1:2], posA[1][:, i:i + 1], den[:]),
            reads=[posA[1], den], writes=[adrF])
        sc.op("vector", lambda e: e.tensor_scalar_min(
            adrF[:, 1:2], adrF[:, 1:2], float(CAP - 1)), writes=[adrF])
        sc.op("vector", lambda e, i=i: e.scalar_tensor_tensor(
            adrF[:, 1:2], idxA[1][:, i:i + 1], float(CAP), adrF[:, 1:2],
            op0=ALU.mult, op1=ALU.add), reads=[idxA[1]], writes=[adrF])
        for k in range(2):
            sc.op("vector", lambda e, i=i, k=k: e.tensor_copy(
                adrI[k][:, i:i + 1], adrF[:, k:k + 1]), reads=[adrF],
                writes=[adrI[k]])
        for g in range(NG):
            sc.op("tensor", lambda e, i=i, g=g: e.transpose(
                pt[0:128, (g % 4) * 128:(g % 4 + 1) * 128],
                xbf[:, g * HT + i * 128:g * HT + (i + 1) * 128], ct.identbf[:]),
                reads=[xbf, ct.identbf], writes=[pt])
            if g % 4 == 3:
                sc.op("vector", lambda e, g=g: e.tensor_copy(
                    x1tm[:, (g - 3) * 128:(g + 1) * 128], pt[0:128, 0:512]),
                    reads=[pt], writes=[x1tm])
        if not MOE_DEBUG:
            for k in range(2):
                sc.op("gpsimd", lambda e, i=i, k=k: e.indirect_dma_start(
                    out=xg_d.ap, out_offset=IndirectOffsetOnAxis(
                        ap=adrI[k][:, i:i + 1], axis=0),
                    in_=x1tm[:], in_offset=None),
                    reads=[adrI[k], x1tm], writes=[xg_d], dma=True)

    if MOE_STAGE < 4:
        return

    for e_i in range(E):
        for g in range(NG):
            stage_w(wgbf, slice(g * IH, (g + 1) * IH),
                    wg[e_i, g * 128:(g + 1) * 128, :], IH)
            stage_w(wubf, slice(g * IH, (g + 1) * IH),
                    wu[e_i, g * 128:(g + 1) * 128, :], IH)
        for ic in range(NIH):
            stage_w(wdbf, slice(ic * C, (ic + 1) * C),
                    wd[e_i, ic * 128:(ic + 1) * 128, :], C)
        for j in range(NST):
            row0 = e_i * CAP + j * 128
            xgtj = xgt[j % 2]
            xgfj = xgf[j % 2]
            sc.dma(xgtj[:], xg_d.ap[row0:row0 + 128, :],
                   reads=[xg_d], writes=[xgtj])
            for g in range(NG):
                sc.op("tensor", lambda e, xgtj=xgtj, g=g: e.transpose(
                    pt[0:128, (g % 4) * 128:(g % 4 + 1) * 128],
                    xgtj[:, g * 128:(g + 1) * 128], ct.identbf[:]),
                    reads=[xgtj, ct.identbf], writes=[pt])
                if g % 4 == 3:
                    sc.op("vector", lambda e, xgfj=xgfj, g=g: e.tensor_copy(
                        xgfj[:, (g - 3) * 128:(g + 1) * 128], pt[0:128, 0:512]),
                        reads=[pt], writes=[xgfj])
            def mmgu(e, j=j, xgfj=xgfj):
                for g in range(NG):
                    for si, (i0, i1) in enumerate(IHS):
                        e.matmul(pp[si][0:128, 0:i1 - i0],
                                 xgfj[:, g * 128:(g + 1) * 128],
                                 wgbf[:, g * IH + i0:g * IH + i1],
                                 start=(g == 0), stop=(g == NG - 1))
                        r = e.matmul(pp[3 + si][0:128, 0:i1 - i0],
                                     xgfj[:, g * 128:(g + 1) * 128],
                                     wubf[:, g * IH + i0:g * IH + i1],
                                     start=(g == 0), stop=(g == NG - 1))
                return r
            sc.op("tensor", mmgu, reads=[xgfj, wgbf, wubf],
                  writes=[pp[si] for si in range(len(IHS))] +
                         [pp[3 + si] for si in range(len(IHS))])
            for si, (i0, i1) in enumerate(IHS):
                sc.op("scalar", lambda e, si=si, i0=i0, i1=i1: e.activation(
                    hsil[:, i0:i1], pp[si][0:128, 0:i1 - i0], AF.Silu),
                    reads=[pp[si]], writes=[hsil])
                sc.op("vector", lambda e, si=si, i0=i0, i1=i1: e.tensor_mul(
                    htm[:, i0:i1], hsil[:, i0:i1], pp[3 + si][0:128, 0:i1 - i0]),
                    reads=[hsil, pp[3 + si]], writes=[htm])
            for ic in range(NIH):
                sc.op("tensor", lambda e, ic=ic: e.transpose(
                    pt[0:128, (ic % 4) * 128:(ic % 4 + 1) * 128],
                    htm[:, ic * 128:(ic + 1) * 128], ct.identbf[:]),
                    reads=[htm, ct.identbf], writes=[pt])
                if ic % 4 == 3 or ic == NIH - 1:
                    lo = (ic // 4) * 4
                    sc.op("vector", lambda e, ic=ic, lo=lo: e.tensor_copy(
                        hfm[:, lo * 128:(ic + 1) * 128],
                        pt[0:128, 0:(ic + 1 - lo) * 128]),
                        reads=[pt], writes=[hfm])
            def mmd(e, j=j):
                for ic in range(NIH):
                    e.matmul(pp[0][0:128, 0:512],
                             hfm[:, ic * 128:(ic + 1) * 128],
                             wdbf[:, ic * C:ic * C + 512],
                             start=(ic == 0), stop=(ic == NIH - 1))
                    r = e.matmul(pp[1][0:128, 0:512],
                                 hfm[:, ic * 128:(ic + 1) * 128],
                                 wdbf[:, ic * C + 512:(ic + 1) * C],
                                 start=(ic == 0), stop=(ic == NIH - 1))
                return r
            sc.op("tensor", mmd, reads=[hfm, wdbf], writes=[pp[0], pp[1]])
            for half in range(2):
                sc.op("vector", lambda e, half=half: e.tensor_copy(
                    ytm[:, half * 512:(half + 1) * 512],
                    pp[half][0:128, 0:512]), reads=[pp[half]], writes=[ytm])
            sc.dma(yb_d.ap[row0 + 0:row0 + 128, :] if False else
                   yb_d.ap[e_i * CAP + j * 128:e_i * CAP + j * 128 + 128, :],
                   ytm[:], reads=[ytm], writes=[yb_d])

    if MOE_STAGE < 5:
        return

    if shared:
        for g in range(NG):
            stage_w(wgsbf, slice(g * IH, (g + 1) * IH),
                    wgs[g * 128:(g + 1) * 128, :], IH)
            stage_w(wusbf, slice(g * IH, (g + 1) * IH),
                    wus[g * 128:(g + 1) * 128, :], IH)
        for ic in range(NIH):
            stage_w(wdsbf, slice(ic * C, (ic + 1) * C),
                    wds[ic * 128:(ic + 1) * 128, :], C)

    for tc in range(NTC):
        xx = xt[tc % 2]
        for g in range(NG):
            sc.dma(xx[:, g * TT:(g + 1) * TT],
                   x[g * 128:(g + 1) * 128, tc * TT:(tc + 1) * TT],
                   reads=[xB], writes=[xx])
        for ti in range(4):
            i = tc * 4 + ti
            if not MOE_DEBUG:
                for k, ybuf in ((0, y1), (1, y2)):
                    sc.op("gpsimd", lambda e, i=i, k=k, ybuf=ybuf:
                          e.indirect_dma_start(
                              out=ybuf[:], out_offset=None,
                              in_=yb_d.ap, in_offset=IndirectOffsetOnAxis(
                                  ap=adrI[k][:, i:i + 1], axis=0)),
                          reads=[yb_d, adrI[k]], writes=[ybuf], dma=True)
            sc.op("vector", lambda e, i=i: e.tensor_scalar_mul(
                ycomb[:], y1[:], wA[:, 2 * i:2 * i + 1]),
                reads=[y1, wA], writes=[ycomb])
            sc.op("vector", lambda e, i=i: e.scalar_tensor_tensor(
                ycomb[:], y2[:], wA[:, 2 * i + 1:2 * i + 2], ycomb[:],
                op0=ALU.mult, op1=ALU.add), reads=[y2, wA], writes=[ycomb])
            if shared:
                def mmgus(e, i=i):
                    for g in range(NG):
                        for si, (i0, i1) in enumerate(IHS):
                            e.matmul(pp[si][0:128, 0:i1 - i0],
                                     xbf[:, g * HT + i * 128:g * HT + (i + 1) * 128],
                                     wgsbf[:, g * IH + i0:g * IH + i1],
                                     start=(g == 0), stop=(g == NG - 1))
                            r = e.matmul(pp[3 + si][0:128, 0:i1 - i0],
                                         xbf[:, g * HT + i * 128:g * HT + (i + 1) * 128],
                                         wusbf[:, g * IH + i0:g * IH + i1],
                                         start=(g == 0), stop=(g == NG - 1))
                    return r
                sc.op("tensor", mmgus, reads=[xbf, wgsbf, wusbf],
                      writes=[pp[si] for si in range(len(IHS))] +
                             [pp[3 + si] for si in range(len(IHS))])
                for si, (i0, i1) in enumerate(IHS):
                    sc.op("scalar", lambda e, si=si, i0=i0, i1=i1: e.activation(
                        hsil[:, i0:i1], pp[si][0:128, 0:i1 - i0], AF.Silu),
                        reads=[pp[si]], writes=[hsil])
                    sc.op("vector", lambda e, si=si, i0=i0, i1=i1: e.tensor_mul(
                        htm[:, i0:i1], hsil[:, i0:i1], pp[3 + si][0:128, 0:i1 - i0]),
                        reads=[hsil, pp[3 + si]], writes=[htm])
                for ic in range(NIH):
                    sc.op("tensor", lambda e, ic=ic: e.transpose(
                        pt[0:128, (ic % 4) * 128:(ic % 4 + 1) * 128],
                        htm[:, ic * 128:(ic + 1) * 128], ct.identbf[:]),
                        reads=[htm, ct.identbf], writes=[pt])
                    if ic % 4 == 3 or ic == NIH - 1:
                        lo = (ic // 4) * 4
                        sc.op("vector", lambda e, ic=ic, lo=lo: e.tensor_copy(
                            hfm[:, lo * 128:(ic + 1) * 128],
                            pt[0:128, 0:(ic + 1 - lo) * 128]),
                            reads=[pt], writes=[hfm])

                def mmds(e):
                    for ic in range(NIH):
                        e.matmul(pp[0][0:128, 0:512],
                                 hfm[:, ic * 128:(ic + 1) * 128],
                                 wdsbf[:, ic * C:ic * C + 512],
                                 start=(ic == 0), stop=(ic == NIH - 1))
                        r = e.matmul(pp[1][0:128, 0:512],
                                     hfm[:, ic * 128:(ic + 1) * 128],
                                     wdsbf[:, ic * C + 512:(ic + 1) * C],
                                     start=(ic == 0), stop=(ic == NIH - 1))
                    return r
                sc.op("tensor", mmds, reads=[hfm, wdsbf], writes=[pp[0], pp[1]])
                for half in range(2):
                    sc.op("vector", lambda e, half=half: e.tensor_add(
                        ycomb[:, half * 512:(half + 1) * 512],
                        ycomb[:, half * 512:(half + 1) * 512],
                        pp[half][0:128, 0:512]), reads=[pp[half]], writes=[ycomb])
            sc.op("scalar", lambda e: e.copy(ycbf[:], ycomb[:]),
                  reads=[ycomb], writes=[ycbf])
            for g in range(NG):
                sc.op("tensor", lambda e, g=g: e.transpose(
                    pt[0:128, 0:128],
                    ycbf[:, g * 128:(g + 1) * 128], ct.identbf[:]),
                    reads=[ycbf, ct.identbf], writes=[pt])
                sc.op("vector", lambda e, g=g, ti=ti: e.tensor_copy(
                    yfm[:, g * TT + ti * 128:g * TT + ti * 128 + 128],
                    pt[0:128, 0:128]), reads=[pt], writes=[yfm])
        for g in range(NG):
            sl = slice(g * TT, (g + 1) * TT)
            sc.op("vector", lambda e, sl=sl, xx=xx: e.tensor_add(
                res[:, sl], res[:, sl], xx[:, sl]),
                reads=[xx], writes=[res])
        emit_ln(sc, ct, res, res, TT, lns, pp[0], pp[1], pp[2], pp[3])
        for g in range(NG):
            sc.dma(out_o[g * 128:(g + 1) * 128, tc * TT:(tc + 1) * TT],
                   res[:, g * TT:(g + 1) * TT], reads=[res], writes=[outB],
                   engine="gpsimd", noww=True)
        if "bndx" in EX and tc == NTC - 1:
            xbndt = sc.sb(f"{tag}_xbndt", [128, NG], F32)
            xbnds = sc.sb(f"{tag}_xbnds", [128, NG], F32)
            sc.op("vector", lambda e: e.tensor_copy(
                xbndt[:, 0:NG], res.ap[:, TT - 1::TT]), reads=[res],
                writes=[xbndt])
            for half, sv_ in ((0, EX["s0"]), (1, EX["s1"])):
                sc.op("vector", lambda e, sv_=sv_: e.tensor_scalar_mul(
                    xbnds[:], xbndt[:], sv_[:]), reads=[xbndt, sv_],
                    writes=[xbnds])
                sc.dma(EX["bndx"].ap[half, :].rearrange("(p g) -> p g", p=128),
                       xbnds[:], reads=[xbnds], writes=[EX["bndx"]], noww=True)


def _moe_external(nc, tag, E, IH, shared, with_x=True):
    EX = {}
    if with_x:
        EX["x"] = Buf(nc.dram_tensor(f"{tag}_x", [C, HT], F32, kind="ExternalInput").ap())
    EX["rw"] = Buf(nc.dram_tensor(f"{tag}_rw", [C, E], F32, kind="ExternalInput").ap())
    EX["wg"] = Buf(nc.dram_tensor(f"{tag}_wg", [E, C, IH], F32, kind="ExternalInput").ap())
    EX["wu"] = Buf(nc.dram_tensor(f"{tag}_wu", [E, C, IH], F32, kind="ExternalInput").ap())
    EX["wd"] = Buf(nc.dram_tensor(f"{tag}_wd", [E, IH, C], F32, kind="ExternalInput").ap())
    if shared:
        EX["wgs"] = Buf(nc.dram_tensor(f"{tag}_wgs", [C, IH], F32, kind="ExternalInput").ap())
        EX["wus"] = Buf(nc.dram_tensor(f"{tag}_wus", [C, IH], F32, kind="ExternalInput").ap())
        EX["wds"] = Buf(nc.dram_tensor(f"{tag}_wds", [IH, C], F32, kind="ExternalInput").ap())
    return EX


def build_moe_phase(tag, E, IH, CAP, renorm, shared, final_gamma=False):
    nc = bass.Bass("TRN2", target_bir_lowering=False)
    EX = _moe_external(nc, tag, E, IH, shared)
    EX["iotam64"] = Buf(nc.dram_tensor("c_iotam64", [128, 16], F32,
                                       kind="ExternalInput").ap())
    EX["o"] = Buf(nc.dram_tensor(f"{tag}_o", [C, HT], F32,
                                 kind="ExternalOutput").ap())
    co = declare_consts(nc)
    sc = Sched(nc)
    ct = CT(sc, co)
    emit_moe(nc, sc, ct, EX, tag, E, IH, CAP, renorm, shared)
    sc.emit_and_close()
    return nc


def emit_ln_1tok(sc, ct, x, out, pps, sq1, sv1, mub, rsb, gamma=None, beta=None):
    sc.op("tensor", lambda e: e.matmul(pps[0:1, 0:NG], ct.onescol[:], x[:, 0:NG],
                                       start=True, stop=True),
          reads=[x, ct.onescol], writes=[pps])
    sc.op("vector", lambda e: e.tensor_reduce(sv1[0:1, 0:1], pps[0:1, 0:NG],
                                              axis=AX, op=ALU.add),
          reads=[pps], writes=[sv1])
    sc.op("scalar", lambda e: e.activation(sq1[:, 0:NG], x[:, 0:NG], AF.Square),
          reads=[x], writes=[sq1])
    sc.op("tensor", lambda e: e.matmul(pps[0:1, 8:8 + NG], ct.onescol[:],
                                       sq1[:, 0:NG], start=True, stop=True),
          reads=[sq1, ct.onescol], writes=[pps])
    sc.op("vector", lambda e: e.tensor_reduce(sv1[0:1, 1:2], pps[0:1, 8:8 + NG],
                                              axis=AX, op=ALU.add),
          reads=[pps], writes=[sv1])
    sc.op("vector", lambda e: e.tensor_scalar_mul(sv1[0:1, 0:2], sv1[0:1, 0:2],
                                                  1.0 / C), writes=[sv1])
    sc.op("scalar", lambda e: e.activation(sv1[0:1, 2:3], sv1[0:1, 0:1],
                                           AF.Square), reads=[sv1], writes=[sv1])
    sc.op("vector", lambda e: e.tensor_scalar(
        sv1[0:1, 3:4], sv1[0:1, 1:2], sv1[0:1, 2:3], None, op0=ALU.subtract),
        writes=[sv1])
    sc.op("scalar", lambda e: e.activation(sv1[0:1, 3:4], sv1[0:1, 3:4], AF.Ln,
                                           bias=ct.keps[0:1, 0:1]),
          reads=[ct.keps], writes=[sv1])
    sc.op("scalar", lambda e: e.activation(sv1[0:1, 3:4], sv1[0:1, 3:4], AF.Exp,
                                           scale=-0.5), writes=[sv1])
    sc.op("tensor", lambda e: e.matmul(pps[0:128, 16:17], ct.onesrow[:],
                                       sv1[0:1, 0:1], start=True, stop=True),
          reads=[sv1, ct.onesrow], writes=[pps])
    sc.op("vector", lambda e: e.tensor_copy(mub[:], pps[0:128, 16:17]),
          reads=[pps], writes=[mub])
    sc.op("tensor", lambda e: e.matmul(pps[0:128, 17:18], ct.onesrow[:],
                                       sv1[0:1, 3:4], start=True, stop=True),
          reads=[sv1, ct.onesrow], writes=[pps])
    sc.op("vector", lambda e: e.tensor_copy(rsb[:], pps[0:128, 17:18]),
          reads=[pps], writes=[rsb])
    sc.op("vector", lambda e: e.tensor_scalar(
        out[:, 0:NG], x[:, 0:NG], mub[:], rsb[:], op0=ALU.subtract,
        op1=ALU.mult), reads=[x, mub, rsb], writes=[out])
    if gamma is not None:
        for g in range(NG):
            sc.op("vector", lambda e, g=g: e.tensor_scalar(
                out[:, g:g + 1], out[:, g:g + 1], gamma[:, g:g + 1],
                beta[:, g:g + 1], op0=ALU.mult, op1=ALU.add),
                reads=[gamma, beta], writes=[out])


def emit_phase3(nc, sc, ct, EX):
    xB = EX["x"]
    x = xB.ap
    pmv = EX["pm"].ap
    wk = EX["wk"].ap
    wv = EX["wv"].ap
    wr = EX["wr"].ap
    cns = {nm: EX["c_" + nm].ap for nm in
           ("l0g", "l0b", "l1g", "l1b", "tmk", "tmv", "tmr")}
    kslice = EX["kslice"]
    vslice = EX["vslice"]
    kvbufs = EX["kvbufs"]
    srB = EX["sr"]
    sro = srB.ap
    x2pB = EX["x2p"]
    x2po = x2pB.ap

    ctile = {}
    for nm in cns:
        ctile[nm] = sc.sb(f"c_{nm}", [128, NG], F32)
        sc.dma(ctile[nm][:], cns[nm], writes=[ctile[nm]])
    pmt = sc.sb("pmt", [1, 1], F32)
    sc.dma(pmt[:], pmv, writes=[pmt])
    xprevt = EX["xprevt"]

    wbf = {}
    wtmps3 = [sc.sb(f"wtmp{i}", [128, 1024], F32) for i in range(4)]
    _wr3 = [0]

    def stage3(dst_buf, dst_sl, src_ap):
        i = _wr3[0] % 4
        _wr3[0] += 1
        wt = wtmps3[i]
        deng = ("sync", "gpsimd", "scalar", "sync")[i]
        ceng = ("scalar", "vector", "gpsimd", "scalar")[i]
        sc.dma(wt[:], src_ap, writes=[wt], engine=deng)
        if ceng == "scalar":
            sc.op(ceng, lambda e, wt=wt, dst_sl=dst_sl: e.copy(
                dst_buf[:, dst_sl], wt[:]), reads=[wt], writes=[dst_buf])
        else:
            sc.op(ceng, lambda e, wt=wt, dst_sl=dst_sl: e.tensor_copy(
                dst_buf[:, dst_sl], wt[:]), reads=[wt], writes=[dst_buf])

    for nm, wap in (("k", wk), ("v", wv), ("r", wr)):
        wbf[nm] = sc.sb(f"w{nm}bf", [128, NG * C], BF16)
        for g in range(NG):
            stage3(wbf[nm], slice(g * C, (g + 1) * C),
                   wap[g * 128:(g + 1) * 128, :])

    xa = sc.sb("xa", [128, NG * TT], F32)
    xb = sc.sb("xb", [128, NG * TT], F32)
    x2pt = sc.sb("x2pt", [128, NG * TT], F32)
    htile = sc.sb("htile", [128, NG * TT], F32)
    hlast = sc.sb("hlast", [128, NG], F32)
    hprev = sc.sb("hprev", [128, NG], F32)
    sq1 = sc.sb("sq1", [128, NG], F32)
    sv1 = sc.sb("sv1", [1, 32], F32)
    mub = sc.sb("mub", [128, 1], F32)
    rsb = sc.sb("rsb", [128, 1], F32)
    pmb = sc.sb("pmb", [128, 1], F32)
    xmix = {nm: sc.sb(f"xmix{nm}", [128, NG * TT], BF16)
            for nm in ("k", "v", "r")}
    obuf = sc.sb("obuf", [128, TT], BF16)
    obuf2 = sc.sb("obuf2", [128, TT], BF16)
    srbndt = sc.sb("srbndt", [128, NG], F32)
    x2pbndt = sc.sb("x2pbndt", [128, NG], F32)
    lns = LNScratch(sc, "p3ln", TT)
    pp = [sc.ps(f"pp{i}", [128, TT]) for i in range(6)]

    emit_ln_1tok(sc, ct, xprevt, hprev, pp[4], sq1, sv1, mub, rsb,
                 gamma=ctile["l0g"], beta=ctile["l0b"])
    emit_ln_1tok(sc, ct, hprev, hprev, pp[4], sq1, sv1, mub, rsb,
                 gamma=ctile["l1g"], beta=ctile["l1b"])
    sc.op("tensor", lambda e: e.matmul(pp[4][0:128, 18:19], ct.onesrow[:],
                                       pmt[:], start=True, stop=True),
          reads=[pmt, ct.onesrow], writes=[pp[4]])
    sc.op("vector", lambda e: e.tensor_copy(pmb[:], pp[4][0:128, 18:19]),
          reads=[pp[4]], writes=[pmb])
    sc.op("vector", lambda e: e.tensor_scalar_mul(hprev[:], hprev[:], pmb[:]),
          reads=[pmb], writes=[hprev])

    xt = [xa, xb]
    for tc in range(NTC):
        xx = xt[tc % 2]
        for g in range(NG):
            sc.dma(xx[:, g * TT:(g + 1) * TT],
                   x[g * 128:(g + 1) * 128, tc * TT:(tc + 1) * TT],
                   reads=[xB], writes=[xx])
        emit_ln(sc, ct, xx, x2pt, TT, lns, pp[0], pp[1], pp[2], pp[3],
                gamma=ctile["l0g"], beta=ctile["l0b"])
        for g in range(NG):
            sc.dma(x2po[g * 128:(g + 1) * 128, tc * TT:(tc + 1) * TT],
                   x2pt[:, g * TT:(g + 1) * TT], reads=[x2pt], writes=[x2pB],
                   engine="gpsimd", noww=True)
        emit_ln(sc, ct, x2pt, htile, TT, lns, pp[0], pp[1], pp[2], pp[3],
                gamma=ctile["l1g"], beta=ctile["l1b"])
        for nm, cvec in (("k", "tmk"), ("v", "tmv"), ("r", "tmr")):
            dst = xmix[nm]
            for g in range(NG):
                sl0 = g * TT
                sc.op("vector", lambda e, sl0=sl0: e.tensor_sub(
                    x2pt[:, sl0 + 1:sl0 + TT], htile[:, sl0 + 1:sl0 + TT],
                    htile[:, sl0:sl0 + TT - 1]), reads=[htile], writes=[x2pt])
                sc.op("vector", lambda e, sl0=sl0, g=g, cvec=cvec, dst=dst:
                      e.scalar_tensor_tensor(
                          dst[:, sl0 + 1:sl0 + TT], x2pt[:, sl0 + 1:sl0 + TT],
                          ctile[cvec][:, g:g + 1], htile[:, sl0:sl0 + TT - 1],
                          op0=ALU.mult, op1=ALU.add),
                      reads=[x2pt, ctile[cvec], htile], writes=[dst])
                hl = hprev if tc == 0 else hlast
                sc.op("vector", lambda e, sl0=sl0, g=g, hl=hl: e.tensor_sub(
                    x2pt[:, sl0:sl0 + 1], htile[:, sl0:sl0 + 1], hl[:, g:g + 1]),
                    reads=[htile, hl], writes=[x2pt])
                sc.op("vector", lambda e, sl0=sl0, g=g, cvec=cvec, hl=hl,
                      dst=dst: e.scalar_tensor_tensor(
                          dst[:, sl0:sl0 + 1], x2pt[:, sl0:sl0 + 1],
                          ctile[cvec][:, g:g + 1], hl[:, g:g + 1],
                          op0=ALU.mult, op1=ALU.add),
                      reads=[x2pt, ctile[cvec], hl], writes=[dst])
        hl2 = hlast
        sc.op("vector", lambda e: e.tensor_copy(
            hl2[:, 0:NG], htile.ap[:, TT - 1::TT]), reads=[htile], writes=[hl2])
        for nm in ("k", "v", "r"):
            for oc in range(NG):
                pb = pp[oc % 2]

                def mmkv(e, nm=nm, oc=oc, pb=pb):
                    for g in range(NG):
                        r = e.matmul(pb[0:128, 0:TT],
                                     wbf[nm][:, g * C + oc * 128:g * C + (oc + 1) * 128],
                                     xmix[nm][:, g * TT:(g + 1) * TT],
                                     start=(g == 0), stop=(g == NG - 1))
                    return r
                sc.op("tensor", mmkv, reads=[wbf[nm], xmix[nm]], writes=[pb])
                if nm == "r":
                    sc.op("scalar", lambda e, pb=pb: e.activation(
                        obuf[:], pb[0:128, 0:TT], AF.Sigmoid),
                        reads=[pb], writes=[obuf])
                    if "bnd3" in EX and tc == NTC - 1:
                        sc.op("vector", lambda e, oc=oc: e.tensor_copy(
                            srbndt[:, oc:oc + 1], obuf[:, TT - 1:TT]),
                            reads=[obuf], writes=[srbndt])
                    dst = sro[oc * 128:(oc + 1) * 128, tc * TT:(tc + 1) * TT]
                    dstb = [srB]
                elif nm == "k":
                    sc.op("vector", lambda e, pb=pb: e.tensor_copy(
                        obuf[:], pb[0:128, 0:TT]), reads=[pb], writes=[obuf])
                    dst = kslice(oc, tc)
                    dst2 = EX["kslice2"](oc, tc) if "kv2" in EX else None
                    dstb = kvbufs
                else:
                    sc.op("vector", lambda e, pb=pb: e.tensor_copy(
                        obuf[:], pb[0:128, 0:TT]), reads=[pb], writes=[obuf])
                    dst = vslice(oc, tc)
                    dst2 = EX["vslice2"](oc, tc) if "kv2" in EX else None
                    dstb = kvbufs
                if nm != "r" and "kv2" in EX:
                    s0 = EX["s0"]
                    s1 = EX["s1"]
                    sc.op("vector", lambda e: e.tensor_scalar_mul(
                        obuf2[:], obuf[:], s1[:]), reads=[obuf, s1],
                        writes=[obuf2])
                    sc.op("vector", lambda e: e.tensor_scalar_mul(
                        obuf[:], obuf[:], s0[:]), reads=[s0], writes=[obuf])
                    sc.dma(dst2, obuf2[:], reads=[obuf2], writes=[kvbufs[0]],
                           noww=True)
                sc.dma(dst, obuf[:], reads=[obuf], writes=[dstb[0]], noww=True)
    if "bnd3" in EX:
        sc.op("vector", lambda e: e.tensor_copy(
            x2pbndt[:, 0:NG], x2pt.ap[:, TT - 1::TT]), reads=[x2pt],
            writes=[x2pbndt])
        bnd3 = EX["bnd3"]
        bsc = sc.sb("bnd3sc", [128, NG], F32)
        for half, sv_ in ((0, EX["s0"]), (1, EX["s1"])):
            sc.op("vector", lambda e, sv_=sv_: e.tensor_scalar_mul(
                bsc[:], x2pbndt[:], sv_[:]), reads=[x2pbndt, sv_], writes=[bsc])
            sc.dma(bnd3.ap[half * 2 + 0, :].rearrange("(p g) -> p g", p=128),
                   bsc[:], reads=[bsc], writes=[bnd3], noww=True)
            sc.op("vector", lambda e, sv_=sv_: e.tensor_scalar_mul(
                bsc[:], srbndt[:], sv_[:]), reads=[srbndt, sv_], writes=[bsc])
            sc.dma(bnd3.ap[half * 2 + 1, :].rearrange("(p g) -> p g", p=128),
                   bsc[:], reads=[bsc], writes=[bnd3], noww=True)


def build_phase3():
    nc = bass.Bass("TRN2", target_bir_lowering=False)
    EX = {}
    EX["x"] = Buf(nc.dram_tensor("p3_x", [C, HT], F32, kind="ExternalInput").ap())
    xprev = nc.dram_tensor("p3_xprev", [128, NG], F32, kind="ExternalInput").ap()
    EX["pm"] = Buf(nc.dram_tensor("p3_pm", [1, 1], F32, kind="ExternalInput").ap())
    EX["wk"] = Buf(nc.dram_tensor("p3_wk", [C, C], F32, kind="ExternalInput").ap())
    EX["wv"] = Buf(nc.dram_tensor("p3_wv", [C, C], F32, kind="ExternalInput").ap())
    EX["wr"] = Buf(nc.dram_tensor("p3_wr", [C, C], F32, kind="ExternalInput").ap())
    for nm in ("l0g", "l0b", "l1g", "l1b", "tmk", "tmv", "tmr"):
        EX["c_" + nm] = Buf(nc.dram_tensor(f"p3_{nm}", [128, NG], F32,
                                           kind="ExternalInput").ap())
    ko = nc.dram_tensor("p3_k", [C, HT], BF16, kind="ExternalOutput").ap()
    vo = nc.dram_tensor("p3_v", [C, HT], BF16, kind="ExternalOutput").ap()
    kvB = Buf(ko)
    EX["kslice"] = lambda oc, tc: ko[oc * 128:(oc + 1) * 128,
                                     tc * TT:(tc + 1) * TT]
    EX["vslice"] = lambda oc, tc: vo[oc * 128:(oc + 1) * 128,
                                     tc * TT:(tc + 1) * TT]
    EX["kvbufs"] = [kvB]
    EX["sr"] = Buf(nc.dram_tensor("p3_sr", [C, HT], BF16, kind="ExternalOutput").ap())
    EX["x2p"] = Buf(nc.dram_tensor("p3_x2p", [C, HT], F32, kind="ExternalOutput").ap())
    co = declare_consts(nc)
    sc = Sched(nc)
    ct = CT(sc, co)
    xprevt = sc.sb("xprevt", [128, NG], F32)
    sc.dma(xprevt[:], xprev, writes=[xprevt])
    EX["xprevt"] = xprevt
    emit_phase3(nc, sc, ct, EX)
    sc.emit_and_close()
    return nc


def emit_phase4(nc, sc, EX):
    NCH = 4
    k0 = EX["k0"]
    k1 = EX["k1"]
    v0 = EX["v0"]
    v1 = EX["v1"]
    kvrbufs = EX["kvrbufs"]
    td = EX["td"].ap
    tf = EX["tf"].ap
    att0 = EX["att0"]
    att1 = EX["att1"]
    attbufs = EX["attbufs"]

    tdt = sc.sb("tdt", [128, NCH], F32)
    tft = sc.sb("tft", [128, NCH], F32)
    sc.dma(tdt[:], td, writes=[tdt])
    sc.dma(tft[:], tf, writes=[tft])
    ewt = sc.sb("ewt", [128, NCH], F32)
    eut = sc.sb("eut", [128, NCH], F32)
    nkm = sc.sb("nkm", [128, 1], F32)
    kb = sc.sb("kb", [128, T], BF16)
    vb = sc.sb("vb", [128, T], BF16)
    ek = sc.sb("ek", [128, T], F32)
    t2 = sc.sb("t2", [128, T], F32)
    At = sc.sb("At", [128, T], F32)
    Bt = sc.sb("Bt", [128, T], F32)
    t5 = sc.sb("t5", [128, T], F32)
    attb = sc.sb("attb", [128, T], BF16)

    sc.op("scalar", lambda e: e.activation(ewt[:], tdt[:], AF.Exp),
          reads=[tdt], writes=[ewt])
    sc.op("scalar", lambda e: e.activation(ewt[:], ewt[:], AF.Exp, scale=-1.0),
          writes=[ewt])
    sc.op("scalar", lambda e: e.activation(eut[:], tft[:], AF.Exp),
          reads=[tft], writes=[eut])

    attbndt = sc.sb("attbndt", [128, NCH], BF16)
    attb2 = sc.sb("attb2", [128, T], BF16) if "att2" in EX else None
    blend = "k0b" in EX
    if blend:
        kb2 = sc.sb("kb2", [128, T], BF16)
        vb2 = sc.sb("vb2", [128, T], BF16)
        s0 = EX["s0"]
        s1 = EX["s1"]
    for gg in range(NCH):
        sc.dma(kb[:, 0:HT], k0[gg * 128:(gg + 1) * 128, :],
               reads=kvrbufs, writes=[kb])
        sc.dma(kb[:, HT:T], k1[gg * 128:(gg + 1) * 128, :],
               reads=kvrbufs, writes=[kb], noww=True)
        sc.dma(vb[:, 0:HT], v0[gg * 128:(gg + 1) * 128, :],
               reads=kvrbufs, writes=[vb])
        sc.dma(vb[:, HT:T], v1[gg * 128:(gg + 1) * 128, :],
               reads=kvrbufs, writes=[vb], noww=True)
        if blend:
            sc.dma(kb2[:, 0:HT], EX["k0b"][gg * 128:(gg + 1) * 128, :],
                   reads=kvrbufs, writes=[kb2])
            sc.dma(kb2[:, HT:T], EX["k1b"][gg * 128:(gg + 1) * 128, :],
                   reads=kvrbufs, writes=[kb2], noww=True)
            sc.dma(vb2[:, 0:HT], EX["v0b"][gg * 128:(gg + 1) * 128, :],
                   reads=kvrbufs, writes=[vb2])
            sc.dma(vb2[:, HT:T], EX["v1b"][gg * 128:(gg + 1) * 128, :],
                   reads=kvrbufs, writes=[vb2], noww=True)
            sc.op("vector", lambda e: e.tensor_scalar_mul(kb[:], kb[:], s0[:]),
                  reads=[s0], writes=[kb])
            sc.op("vector", lambda e: e.scalar_tensor_tensor(
                kb[:], kb2[:], s1[:], kb[:], op0=ALU.mult, op1=ALU.add),
                reads=[kb2, s1], writes=[kb])
            sc.op("vector", lambda e: e.tensor_scalar_mul(vb[:], vb[:], s0[:]),
                  reads=[s0], writes=[vb])
            sc.op("vector", lambda e: e.scalar_tensor_tensor(
                vb[:], vb2[:], s1[:], vb[:], op0=ALU.mult, op1=ALU.add),
                reads=[vb2, s1], writes=[vb])
        sc.op("vector", lambda e: e.reduce_max(nkm[:], kb[:], axis=AX,
                                               negate=True),
              reads=[kb], writes=[nkm])
        sc.op("scalar", lambda e: e.activation(ek[:], kb[:], AF.Exp,
                                               bias=nkm[:]),
              reads=[kb, nkm], writes=[ek])
        sc.op("vector", lambda e: e.tensor_mul(t2[:], ek[:], vb[:]),
              reads=[ek, vb], writes=[t2])
        sc.op("vector", lambda e, gg=gg: e.tensor_tensor_scan(
            At[:], ewt.ap[:, gg:gg + 1].to_broadcast((128, T)), t2[:], 0.0,
            op0=ALU.mult, op1=ALU.add), reads=[ewt, t2], writes=[At])
        sc.op("vector", lambda e, gg=gg: e.tensor_tensor_scan(
            Bt[:], ewt.ap[:, gg:gg + 1].to_broadcast((128, T)), ek[:], 0.0,
            op0=ALU.mult, op1=ALU.add), reads=[ewt, ek], writes=[Bt])
        sc.op("vector", lambda e, gg=gg: e.tensor_scalar_mul(
            t5[:], ek[:], eut[:, gg:gg + 1]), reads=[ek, eut], writes=[t5])
        sc.op("vector", lambda e: e.tensor_mul(t2[:], t5[:], vb[:]),
              reads=[t5, vb], writes=[t2])
        sc.op("vector", lambda e: e.tensor_add(t2[:, 1:T], t2[:, 1:T],
                                               At[:, 0:T - 1]),
              reads=[At], writes=[t2])
        sc.op("vector", lambda e: e.tensor_add(t5[:, 1:T], t5[:, 1:T],
                                               Bt[:, 0:T - 1]),
              reads=[Bt], writes=[t5])
        sc.op("scalar", lambda e: e.activation(t5[:], t5[:], AF.Ln),
              writes=[t5])
        sc.op("scalar", lambda e: e.activation(t5[:], t5[:], AF.Exp,
                                               scale=-1.0), writes=[t5])
        sc.op("vector", lambda e: e.tensor_mul(attb[:], t2[:], t5[:]),
              reads=[t2, t5], writes=[attb])
        if "attbnd" in EX:
            sc.op("vector", lambda e, gg=gg: e.tensor_copy(
                attbndt[:, gg:gg + 1], attb[:, HT - 1:HT]),
                reads=[attb], writes=[attbndt])
        if "att2" in EX:
            s0 = EX["s0"]
            s1 = EX["s1"]
            att0b, att1b = EX["att2"]
            sc.op("vector", lambda e: e.tensor_scalar_mul(
                attb2[:], attb[:], s1[:]), reads=[attb, s1], writes=[attb2])
            sc.op("vector", lambda e: e.tensor_scalar_mul(
                attb[:], attb[:], s0[:]), reads=[s0], writes=[attb])
            sc.dma(att0b[gg * 128:(gg + 1) * 128, :], attb2[:, 0:HT],
                   reads=[attb2], writes=[attbufs[0]], noww=True)
            sc.dma(att1b[gg * 128:(gg + 1) * 128, :], attb2[:, HT:T],
                   reads=[attb2], writes=[attbufs[0]], noww=True)
        sc.dma(att0[gg * 128:(gg + 1) * 128, :], attb[:, 0:HT], reads=[attb],
               writes=[attbufs[0]], noww=True)
        sc.dma(att1[gg * 128:(gg + 1) * 128, :], attb[:, HT:T], reads=[attb],
               writes=[attbufs[0]], noww=True)
    if "attbnd" in EX:
        bsc = sc.sb("bndAsc", [128, NCH], BF16)
        for ap_, sv_ in zip(EX["attbnd"], (EX.get("s0"), EX.get("s1"))):
            if sv_ is None:
                sc.dma(ap_, attbndt[:], reads=[attbndt], writes=[attbufs[0]],
                       noww=True)
            else:
                sc.op("vector", lambda e, sv_=sv_: e.tensor_scalar_mul(
                    bsc[:], attbndt[:], sv_[:]), reads=[attbndt, sv_],
                    writes=[bsc])
                sc.dma(ap_, bsc[:], reads=[bsc], writes=[attbufs[0]], noww=True)


def build_phase4():
    NCH = 4
    nc = bass.Bass("TRN2", target_bir_lowering=False)
    kin = nc.dram_tensor("p4_k", [512, T], BF16, kind="ExternalInput").ap()
    vin = nc.dram_tensor("p4_v", [512, T], BF16, kind="ExternalInput").ap()
    EX = {}
    EX["k0"] = kin[:, 0:HT]
    EX["k1"] = kin[:, HT:T]
    EX["v0"] = vin[:, 0:HT]
    EX["v1"] = vin[:, HT:T]
    EX["kvrbufs"] = []
    EX["td"] = Buf(nc.dram_tensor("p4_td", [128, NCH], F32, kind="ExternalInput").ap())
    EX["tf"] = Buf(nc.dram_tensor("p4_tf", [128, NCH], F32, kind="ExternalInput").ap())
    atto = nc.dram_tensor("p4_att", [512, T], BF16, kind="ExternalOutput").ap()
    EX["att0"] = atto[:, 0:HT]
    EX["att1"] = atto[:, HT:T]
    EX["attbufs"] = [Buf(atto)]
    sc = Sched(nc)
    emit_phase4(nc, sc, EX)
    sc.emit_and_close()
    return nc


def emit_phase5(nc, sc, ct, EX):
    H4 = 4096
    NHC = H4 // 128
    x2B = EX["x2"]
    x2 = x2B.ap
    x2pB = EX["x2p"]
    x2p = x2pB.ap
    srB = EX["sr"]
    srin = srB.ap
    attB = EX["att"]
    attin = attB.ap
    pmv = EX["pm"].ap
    wo = EX["wo"].ap
    wfr = EX["wfr"].ap
    wfk = EX["wfk"].ap
    wfv = EX["wfv"].ap
    cns = {nm: EX["c_" + nm].ap for nm in ("l2g", "l2b", "cmk", "cmr")}
    ooB = EX["o"]
    oo = ooB.ap
    x3_d = Buf(nc.dram_tensor("p5_x3d", [C, HT], F32).ap())
    kf_d = Buf(nc.dram_tensor("p5_kfd", [H4, HT], BF16).ap())
    sr2_d = Buf(nc.dram_tensor("p5_sr2d", [C, HT], BF16).ap())

    ctile = {}
    for nm in cns:
        ctile[nm] = sc.sb(f"c5_{nm}", [128, NG], F32)
        sc.dma(ctile[nm][:], cns[nm], writes=[ctile[nm]])
    pmt = sc.sb("pmt5", [1, 1], F32)
    sc.dma(pmt[:], pmv, writes=[pmt])
    apv = EX["apv"]
    spv = EX["spv"]
    xpv = EX["xpv"]

    wtmps5 = [sc.sb(f"wtmp{i}", [128, 1024], F32) for i in range(2)]
    wtmps5b = [Buf(wtmps5[0].ap[:, 0:512]), Buf(wtmps5[0].ap[:, 512:1024]),
               Buf(wtmps5[1].ap[:, 0:512]), Buf(wtmps5[1].ap[:, 512:1024])]

    _wr5 = [0]

    def stage5(dst_buf, dst_sl, src_ap):
        i = _wr5[0] % 2
        _wr5[0] += 1
        wt = wtmps5[i]
        deng = ("sync", "gpsimd")[i]
        ceng = ("scalar", "vector")[i]
        sc.dma(wt[:], src_ap, writes=[wt], engine=deng)
        if ceng == "scalar":
            sc.op(ceng, lambda e, wt=wt, dst_sl=dst_sl: e.copy(
                dst_buf[:, dst_sl], wt[:]), reads=[wt], writes=[dst_buf])
        else:
            sc.op(ceng, lambda e, wt=wt, dst_sl=dst_sl: e.tensor_copy(
                dst_buf[:, dst_sl], wt[:]), reads=[wt], writes=[dst_buf])

    wobf = sc.sb("wobf", [128, NG * C], BF16)
    wfrbf = sc.sb("wfrbf", [128, NG * C], BF16)
    wbig = sc.sb("wbig", [128, NG * H4], BF16)
    for g in range(NG):
        stage5(wobf, slice(g * C, (g + 1) * C), wo[g * 128:(g + 1) * 128, :])
        stage5(wfrbf, slice(g * C, (g + 1) * C), wfr[g * 128:(g + 1) * 128, :])
    for g in range(NG):
        for q in range(4):
            stage5(wbig, slice(g * H4 + q * C, g * H4 + (q + 1) * C),
                   wfk[g * 128:(g + 1) * 128, q * C:(q + 1) * C])

    xa = sc.sb("xa", [128, NG * TT], F32)
    xb = xa
    srt = sc.sb("srt", [128, NG * TT], BF16)
    attt = sc.sb("attt", [128, NG * TT], BF16)
    att2t = sc.sb("att2t", [128, TT], BF16) if "attin2" in EX else None
    x3t = sc.sb("x3t", [128, NG * TT], F32)
    h2t = sc.sb("h2t", [128, NG * TT], F32)
    xk2 = sc.sb("xk2", [128, NG * TT], BF16)
    xr2 = sc.sb("xr2", [128, NG * TT], BF16)
    h2last = sc.sb("h2last", [128, NG], F32)
    h2prev = sc.sb("h2prev", [128, NG], F32)
    sq1 = sc.sb("sq1", [128, NG], F32)
    sv1 = sc.sb("sv1", [1, 32], F32)
    mub = sc.sb("mub", [128, 1], F32)
    rsb = sc.sb("rsb", [128, 1], F32)
    pmb = sc.sb("pmb", [128, 1], F32)
    obuf = sc.sb("obuf", [128, TT], BF16)
    kft = sc.sb("kft", [128, TT], BF16)
    lns = LNScratch(sc, "p5ln", TT)
    pp = [sc.ps(f"pp{i}", [128, TT]) for i in range(6)]

    sc.op("tensor", lambda e: e.matmul(pp[4][0:128, 18:19], ct.onesrow[:],
                                       pmt[:], start=True, stop=True),
          reads=[pmt, ct.onesrow], writes=[pp[4]])
    sc.op("vector", lambda e: e.tensor_copy(pmb[:], pp[4][0:128, 18:19]),
          reads=[pp[4]], writes=[pmb])

    sap = sc.sb("sap", [128, NG], F32)
    sc.op("vector", lambda e: e.tensor_mul(sap[:], spv[:], apv[:]),
          reads=[spv, apv], writes=[sap])
    sapbf = sc.sb("sapbf", [128, NG], BF16)
    sc.op("vector", lambda e: e.tensor_copy(sapbf[:], sap[:]),
          reads=[sap], writes=[sapbf])

    def mmprev(e):
        for oc in range(NG):
            for g in range(NG):
                r = e.matmul(pp[5][0:128, oc:oc + 1],
                             wobf[:, g * C + oc * 128:g * C + (oc + 1) * 128],
                             sapbf[:, g:g + 1],
                             start=(g == 0), stop=(g == NG - 1))
        return r
    sc.op("tensor", mmprev, reads=[wobf, sapbf], writes=[pp[5]])
    sc.op("vector", lambda e: e.tensor_add(h2prev[:], pp[5][0:128, 0:NG],
                                           xpv[:]),
          reads=[pp[5], xpv], writes=[h2prev])
    emit_ln_1tok(sc, ct, h2prev, h2prev, pp[4], sq1, sv1, mub, rsb,
                 gamma=ctile["l2g"], beta=ctile["l2b"])
    sc.op("vector", lambda e: e.tensor_scalar_mul(h2prev[:], h2prev[:], pmb[:]),
          reads=[pmb], writes=[h2prev])

    xt = [xa, xb]
    for tc in range(NTC):
        xx = xt[tc % 2]
        for g in range(NG):
            sc.dma(xx[:, g * TT:(g + 1) * TT],
                   x2p[g * 128:(g + 1) * 128, tc * TT:(tc + 1) * TT],
                   reads=[x2pB], writes=[xx])
            sc.dma(srt[:, g * TT:(g + 1) * TT],
                   srin[g * 128:(g + 1) * 128, tc * TT:(tc + 1) * TT],
                   reads=[srB], writes=[srt])
            sc.dma(attt[:, g * TT:(g + 1) * TT],
                   attin[g * 128:(g + 1) * 128, tc * TT:(tc + 1) * TT],
                   reads=[attB], writes=[attt])
            if "attin2" in EX:
                sc.dma(att2t[:],
                       EX["attin2"][g * 128:(g + 1) * 128, tc * TT:(tc + 1) * TT],
                       reads=[attB], writes=[att2t])
                sc.op("vector", lambda e, g=g: e.tensor_scalar_mul(
                    attt[:, g * TT:(g + 1) * TT], attt[:, g * TT:(g + 1) * TT],
                    EX["s0"][:]), reads=[EX["s0"]], writes=[attt])
                sc.op("vector", lambda e, g=g: e.scalar_tensor_tensor(
                    attt[:, g * TT:(g + 1) * TT], att2t[:], EX["s1"][:],
                    attt[:, g * TT:(g + 1) * TT], op0=ALU.mult, op1=ALU.add),
                    reads=[att2t, EX["s1"]], writes=[attt])
        sratt = attt
        for g in range(NG):
            sc.op("vector", lambda e, g=g: e.tensor_mul(
                sratt[:, g * TT:(g + 1) * TT], srt[:, g * TT:(g + 1) * TT],
                attt[:, g * TT:(g + 1) * TT]), reads=[srt], writes=[attt])
        for oc in range(NG):
            pb = pp[oc % 2]

            def mmo(e, oc=oc, pb=pb):
                for g in range(NG):
                    r = e.matmul(pb[0:128, 0:TT],
                                 wobf[:, g * C + oc * 128:g * C + (oc + 1) * 128],
                                 sratt[:, g * TT:(g + 1) * TT],
                                 start=(g == 0), stop=(g == NG - 1))
                return r
            sc.op("tensor", mmo, reads=[wobf, sratt], writes=[pb])
            sc.op("vector", lambda e, oc=oc, pb=pb, xx=xx: e.tensor_add(
                x3t[:, oc * TT:(oc + 1) * TT], pb[0:128, 0:TT],
                xx[:, oc * TT:(oc + 1) * TT]), reads=[pb, xx], writes=[x3t])
        for g in range(NG):
            sc.dma(x3_d.ap[g * 128:(g + 1) * 128, tc * TT:(tc + 1) * TT],
                   x3t[:, g * TT:(g + 1) * TT], reads=[x3t], writes=[x3_d])
        emit_ln(sc, ct, x3t, h2t, TT, lns, pp[0], pp[1], pp[2], pp[3],
                gamma=ctile["l2g"], beta=ctile["l2b"])
        for dst, cvec in ((xk2, "cmk"), (xr2, "cmr")):
            for g in range(NG):
                sl0 = g * TT
                sc.op("vector", lambda e, sl0=sl0: e.tensor_sub(
                    x3t[:, sl0 + 1:sl0 + TT], h2t[:, sl0 + 1:sl0 + TT],
                    h2t[:, sl0:sl0 + TT - 1]), reads=[h2t], writes=[x3t])
                sc.op("vector", lambda e, sl0=sl0, g=g, dst=dst, cvec=cvec:
                      e.scalar_tensor_tensor(
                          dst[:, sl0 + 1:sl0 + TT], x3t[:, sl0 + 1:sl0 + TT],
                          ctile[cvec][:, g:g + 1], h2t[:, sl0:sl0 + TT - 1],
                          op0=ALU.mult, op1=ALU.add),
                      reads=[x3t, ctile[cvec], h2t], writes=[dst])
                hl = h2prev if tc == 0 else h2last
                sc.op("vector", lambda e, sl0=sl0, g=g, hl=hl: e.tensor_sub(
                    x3t[:, sl0:sl0 + 1], h2t[:, sl0:sl0 + 1], hl[:, g:g + 1]),
                    reads=[h2t, hl], writes=[x3t])
                sc.op("vector", lambda e, sl0=sl0, g=g, dst=dst, cvec=cvec,
                      hl=hl: e.scalar_tensor_tensor(
                          dst[:, sl0:sl0 + 1], x3t[:, sl0:sl0 + 1],
                          ctile[cvec][:, g:g + 1], hl[:, g:g + 1],
                          op0=ALU.mult, op1=ALU.add),
                      reads=[x3t, ctile[cvec], hl], writes=[dst])
        sc.op("vector", lambda e: e.tensor_copy(
            h2last[:, 0:NG], h2t.ap[:, TT - 1::TT]), reads=[h2t],
            writes=[h2last])
        for oc in range(NG):
            pb = pp[oc % 2]

            def mmr(e, oc=oc, pb=pb):
                for g in range(NG):
                    r = e.matmul(pb[0:128, 0:TT],
                                 wfrbf[:, g * C + oc * 128:g * C + (oc + 1) * 128],
                                 xr2[:, g * TT:(g + 1) * TT],
                                 start=(g == 0), stop=(g == NG - 1))
                return r
            sc.op("tensor", mmr, reads=[wfrbf, xr2], writes=[pb])
            sc.op("scalar", lambda e, pb=pb: e.activation(
                obuf[:], pb[0:128, 0:TT], AF.Sigmoid), reads=[pb],
                writes=[obuf])
            sc.dma(sr2_d.ap[oc * 128:(oc + 1) * 128, tc * TT:(tc + 1) * TT],
                   obuf[:], reads=[obuf], writes=[sr2_d])
        for oc in range(NHC):
            pb = pp[2 + (oc % 2)]

            def mmk(e, oc=oc, pb=pb):
                for g in range(NG):
                    r = e.matmul(pb[0:128, 0:TT],
                                 wbig[:, g * H4 + oc * 128:g * H4 + (oc + 1) * 128],
                                 xk2[:, g * TT:(g + 1) * TT],
                                 start=(g == 0), stop=(g == NG - 1))
                return r
            sc.op("tensor", mmk, reads=[wbig, xk2], writes=[pb])
            sc.op("vector", lambda e, pb=pb: e.tensor_scalar_max(
                pb[0:128, 0:TT], pb[0:128, 0:TT], 0.0), writes=[pb])
            sc.op("scalar", lambda e, pb=pb: e.activation(
                kft[:], pb[0:128, 0:TT], AF.Square), reads=[pb], writes=[kft])
            sc.dma(kf_d.ap[oc * 128:(oc + 1) * 128, tc * TT:(tc + 1) * TT],
                   kft[:], reads=[kft], writes=[kf_d])

    for hc in range(NHC):
        stage5(wbig, slice(hc * C, (hc + 1) * C),
               wfv[hc * 128:(hc + 1) * 128, :])
    _kfc0 = Buf(xk2.ap[:, 0:TT])
    _kfc0.writer = xk2.writer
    _kfc0.readers = list(xk2.readers)
    kfc = [_kfc0, Buf(xk2.ap[:, TT:2 * TT])]
    kfc[1].writer = xk2.writer
    kfc[1].readers = list(xk2.readers)
    for tc in range(NTC):
        xx = xt[tc % 2]
        for g in range(NG):
            sc.dma(srt[:, g * TT:(g + 1) * TT],
                   sr2_d.ap[g * 128:(g + 1) * 128, tc * TT:(tc + 1) * TT],
                   reads=[sr2_d], writes=[srt])
            sc.dma(xx[:, g * TT:(g + 1) * TT],
                   x2[g * 128:(g + 1) * 128, tc * TT:(tc + 1) * TT],
                   reads=[x2B], writes=[xx])
            sc.dma(x3t[:, g * TT:(g + 1) * TT],
                   x3_d.ap[g * 128:(g + 1) * 128, tc * TT:(tc + 1) * TT],
                   reads=[x3_d], writes=[x3t])
        for ohalf in range(2):
            for hc in range(NHC):
                kf_b = kfc[hc % 2]
                sc.dma(kf_b[:],
                       kf_d.ap[hc * 128:(hc + 1) * 128, tc * TT:(tc + 1) * TT],
                       reads=[kf_d], writes=[kf_b])

                def mmv(e, ohalf=ohalf, hc=hc, kf_b=kf_b):
                    for oi in range(4):
                        oc = ohalf * 4 + oi
                        r = e.matmul(pp[oi][0:128, 0:TT],
                                     wbig[:, hc * C + oc * 128:hc * C + (oc + 1) * 128],
                                     kf_b[:],
                                     start=(hc == 0), stop=(hc == NHC - 1))
                    return r
                sc.op("tensor", mmv, reads=[wbig, kf_b],
                      writes=[pp[0], pp[1], pp[2], pp[3]])
            for oi in range(4):
                oc = ohalf * 4 + oi
                sc.op("vector", lambda e, oc=oc, oi=oi: e.tensor_mul(
                    h2t[:, oc * TT:(oc + 1) * TT], pp[oi][0:128, 0:TT],
                    srt[:, oc * TT:(oc + 1) * TT]), reads=[pp[oi], srt],
                    writes=[h2t])
                sc.op("vector", lambda e, oc=oc, xx=xx: e.tensor_add(
                    h2t[:, oc * TT:(oc + 1) * TT], h2t[:, oc * TT:(oc + 1) * TT],
                    xx[:, oc * TT:(oc + 1) * TT]), reads=[xx], writes=[h2t])
                sc.op("vector", lambda e, oc=oc: e.tensor_add(
                    h2t[:, oc * TT:(oc + 1) * TT], h2t[:, oc * TT:(oc + 1) * TT],
                    x3t[:, oc * TT:(oc + 1) * TT]), reads=[x3t], writes=[h2t])
        emit_ln(sc, ct, h2t, h2t, TT, lns, pp[0], pp[1], pp[2], pp[3])
        for g in range(NG):
            sc.dma(oo[g * 128:(g + 1) * 128, tc * TT:(tc + 1) * TT],
                   h2t[:, g * TT:(g + 1) * TT], reads=[h2t], writes=[ooB],
                   engine="gpsimd", noww=True)


def build_phase5():
    nc = bass.Bass("TRN2", target_bir_lowering=False)
    H4 = 4096
    EX = {}
    EX["x2"] = Buf(nc.dram_tensor("p5_x2", [C, HT], F32, kind="ExternalInput").ap())
    EX["x2p"] = Buf(nc.dram_tensor("p5_x2p", [C, HT], F32, kind="ExternalInput").ap())
    EX["sr"] = Buf(nc.dram_tensor("p5_sr", [C, HT], BF16, kind="ExternalInput").ap())
    EX["att"] = Buf(nc.dram_tensor("p5_att", [C, HT], BF16, kind="ExternalInput").ap())
    attprev = nc.dram_tensor("p5_attprev", [128, NG], BF16, kind="ExternalInput").ap()
    srprev = nc.dram_tensor("p5_srprev", [128, NG], BF16, kind="ExternalInput").ap()
    x2pprev = nc.dram_tensor("p5_x2pprev", [128, NG], F32, kind="ExternalInput").ap()
    EX["pm"] = Buf(nc.dram_tensor("p5_pm", [1, 1], F32, kind="ExternalInput").ap())
    EX["wo"] = Buf(nc.dram_tensor("p5_wo", [C, C], F32, kind="ExternalInput").ap())
    EX["wfr"] = Buf(nc.dram_tensor("p5_wfr", [C, C], F32, kind="ExternalInput").ap())
    EX["wfk"] = Buf(nc.dram_tensor("p5_wfk", [C, H4], F32, kind="ExternalInput").ap())
    EX["wfv"] = Buf(nc.dram_tensor("p5_wfv", [H4, C], F32, kind="ExternalInput").ap())
    for nm in ("l2g", "l2b", "cmk", "cmr"):
        EX["c_" + nm] = Buf(nc.dram_tensor(f"p5_{nm}", [128, NG], F32,
                                           kind="ExternalInput").ap())
    EX["o"] = Buf(nc.dram_tensor("p5_o", [C, HT], F32, kind="ExternalOutput").ap())
    co = declare_consts(nc)
    sc = Sched(nc)
    ct = CT(sc, co)
    apvb = sc.sb("apvb", [128, NG], BF16)
    spvb = sc.sb("spvb", [128, NG], BF16)
    apv = sc.sb("apv", [128, NG], F32)
    spv = sc.sb("spv", [128, NG], F32)
    xpv = sc.sb("xpv", [128, NG], F32)
    sc.dma(apvb[:], attprev, writes=[apvb])
    sc.dma(spvb[:], srprev, writes=[spvb])
    sc.dma(xpv[:], x2pprev, writes=[xpv])
    sc.op("vector", lambda e: e.tensor_copy(apv[:], apvb[:]),
          reads=[apvb], writes=[apv])
    sc.op("vector", lambda e: e.tensor_copy(spv[:], spvb[:]),
          reads=[spvb], writes=[spv])
    EX["apv"] = apv
    EX["spv"] = spv
    EX["xpv"] = xpv
    emit_phase5(nc, sc, ct, EX)
    sc.emit_and_close()
    return nc


PAIRS = [[0, 1], [2, 3], [4, 5], [6, 7]]


FUSED_STAGE = 9


def build_fused():
    nc = bass.Bass("TRN2", target_bir_lowering=False)
    co = declare_consts(nc)
    iotam64 = Buf(nc.dram_tensor("c_iotam64", [128, 16], F32,
                                 kind="ExternalInput").ap())
    ch0 = nc.dram_tensor("c_h0", [128, 1], F32, kind="ExternalInput").ap()
    ch1 = nc.dram_tensor("c_h1", [128, 1], F32, kind="ExternalInput").ap()

    x1_d = Buf(nc.dram_tensor("f_x1", [C, HT], F32).ap())
    x2_d = Buf(nc.dram_tensor("f_x2", [C, HT], F32).ap())
    x2p_d = Buf(nc.dram_tensor("f_x2p", [C, HT], F32).ap())
    sr_d = Buf(nc.dram_tensor("f_sr", [C, HT], BF16).ap())
    x4_d = Buf(nc.dram_tensor("f_x4", [C, HT], F32).ap())
    bnd2s = Buf(nc.dram_tensor("f_bnd2s", [2, C], F32).ap())
    bnd2r = Buf(nc.dram_tensor("f_bnd2r", [2, C], F32).ap())
    kvs = Buf(nc.dram_tensor("f_kvs", [2 * 2048, HT], BF16).ap())
    kvr = Buf(nc.dram_tensor("f_kvr", [2 * 2048, HT], BF16).ap())
    bnd3s = Buf(nc.dram_tensor("f_bnd3s", [4, C], F32).ap())
    bnd3r = Buf(nc.dram_tensor("f_bnd3r", [4, C], F32).ap())
    atts = Buf(nc.dram_tensor("f_atts", [C, T], BF16).ap())
    attr_ = Buf(nc.dram_tensor("f_attr", [C, T], BF16).ap())
    bndAs = Buf(nc.dram_tensor("f_bndAs", [2, 512], BF16).ap())
    bndAr = Buf(nc.dram_tensor("f_bndAr", [2, 512], BF16).ap())

    p6o = Buf(nc.dram_tensor("p6_o", [C, HT], F32, kind="ExternalOutput").ap())
    sc = Sched(nc)
    ct = CT(sc, co)
    s0t = sc.sb("s0t", [128, 1], F32)
    s1t = sc.sb("s1t", [128, 1], F32)
    sc.dma(s0t[:], ch0, writes=[s0t])
    sc.dma(s1t[:], ch1, writes=[s1t])

    def allred(snd, rcv, dt):
        sc.op("gpsimd", lambda e: e.collective_compute(
            "AllReduce", ALU.add, replica_groups=PAIRS,
            ins=[snd.ap.opt()], outs=[rcv.ap.opt()]),
            reads=[snd], writes=[rcv], dma=True, dmabuf=rcv, inc_amt=1)

    sc.prefix = "f1_"
    sc.push_scope()
    E1 = {}
    E1["x"] = Buf(nc.dram_tensor("p1_x", [C, T], F32, kind="ExternalInput").ap())
    E1["xh"] = Buf(nc.dram_tensor("p1_xh", [C, HT], F32, kind="ExternalInput").ap())
    E1["xrev"] = Buf(nc.dram_tensor("p1_xrev", [C, HT], F32, kind="ExternalInput").ap())
    E1["rw"] = Buf(nc.dram_tensor("p1_rw", [128, NG], F32, kind="ExternalInput").ap())
    E1["rb"] = Buf(nc.dram_tensor("p1_rb", [128, 1], F32, kind="ExternalInput").ap())
    E1["hsel"] = Buf(nc.dram_tensor("p1_hsel", [1, 2], F32, kind="ExternalInput").ap())
    E1["x1"] = x1_d
    emit_phase1(nc, sc, ct, E1)
    sc.pop_scope()
    if FUSED_STAGE == 15:
        allred(bnd2s, bnd2r, F32)
        import os
        lim = os.environ.get("SCHED_LIMIT")
        if lim:
            n = int(lim)
            last = sc.ops[-1]
            sc.ops = sc.ops[:n]
            sc.ops.append(last)
            tot = {}
            for (e, fn, waits, inc) in sc.ops:
                s, amt = inc
                tot[s["name"]] = tot.get(s["name"], 0) + amt
            for s in sc.sems.values():
                s["tot"] = tot.get(s["name"], 0)
            e, fn, waits, inc = sc.ops[-1]
            waits = [(s, min(v, tot.get(s["name"], 0))) for (s, v) in waits]
            sc.ops[-1] = (e, fn, waits, inc)
        sc.emit_and_close()
        return nc
    if FUSED_STAGE < 2:
        sc.emit_and_close()
        return nc

    sc.prefix = "f2_"
    sc.push_scope()
    EX2 = _moe_external(nc, "p2", 16, 1024, False, with_x=False)
    EX2["x"] = x1_d
    EX2["o"] = x2_d
    EX2["iotam64"] = iotam64
    EX2["bndx"] = bnd2s
    EX2["s0"] = s0t
    EX2["s1"] = s1t
    emit_moe(nc, sc, ct, EX2, "p2", 16, 1024, 384, False, False)
    sc.pop_scope()
    if FUSED_STAGE == 25:
        sc.emit_and_close()
        return nc
    allred(bnd2s, bnd2r, F32)
    if FUSED_STAGE < 3:
        sc.emit_and_close()
        return nc

    sc.prefix = "f3_"
    sc.push_scope()
    EX3 = {}
    EX3["x"] = x2_d
    EX3["pm"] = Buf(nc.dram_tensor("p3_pm", [1, 1], F32, kind="ExternalInput").ap())
    EX3["wk"] = Buf(nc.dram_tensor("p3_wk", [C, C], F32, kind="ExternalInput").ap())
    EX3["wv"] = Buf(nc.dram_tensor("p3_wv", [C, C], F32, kind="ExternalInput").ap())
    EX3["wr"] = Buf(nc.dram_tensor("p3_wr", [C, C], F32, kind="ExternalInput").ap())
    for nm in ("l0g", "l0b", "l1g", "l1b", "tmk", "tmv", "tmr"):
        EX3["c_" + nm] = Buf(nc.dram_tensor(f"p3_{nm}", [128, NG], F32,
                                            kind="ExternalInput").ap())
    xprevt = sc.sb("xprevt", [128, NG], F32)
    sc.dma(xprevt[:], bnd2r.ap[0, :].rearrange("(p g) -> p g", p=128),
           reads=[bnd2r], writes=[xprevt])
    EX3["xprevt"] = xprevt
    kvs_ap = kvs.ap

    def ksl(reg):
        def f(oc, tc):
            if oc < 4:
                r = reg * 2048 + oc * 128
            else:
                r = reg * 2048 + 1024 + (oc - 4) * 128
            return kvs_ap[r:r + 128, tc * TT:(tc + 1) * TT]
        return f

    def vsl(reg):
        def f(oc, tc):
            if oc < 4:
                r = reg * 2048 + 512 + oc * 128
            else:
                r = reg * 2048 + 1536 + (oc - 4) * 128
            return kvs_ap[r:r + 128, tc * TT:(tc + 1) * TT]
        return f
    EX3["kslice"] = ksl(0)
    EX3["vslice"] = vsl(0)
    EX3["kslice2"] = ksl(1)
    EX3["vslice2"] = vsl(1)
    EX3["kv2"] = True
    EX3["s0"] = s0t
    EX3["s1"] = s1t
    EX3["kvbufs"] = [kvs]
    EX3["sr"] = sr_d
    EX3["x2p"] = x2p_d
    EX3["bnd3"] = bnd3s
    emit_phase3(nc, sc, ct, EX3)
    sc.pop_scope()
    allred(kvs, kvr, BF16)
    allred(bnd3s, bnd3r, F32)
    if FUSED_STAGE < 4:
        sc.emit_and_close()
        return nc

    sc.prefix = "f4_"
    sc.push_scope()
    EX4 = {}
    kvr_ap = kvr.ap
    EX4["k0"] = kvr_ap[0:512, :]
    EX4["v0"] = kvr_ap[512:1024, :]
    EX4["k1"] = kvr_ap[2048:2560, :]
    EX4["v1"] = kvr_ap[2560:3072, :]
    EX4["k0b"] = kvr_ap[1024:1536, :]
    EX4["v0b"] = kvr_ap[1536:2048, :]
    EX4["k1b"] = kvr_ap[3072:3584, :]
    EX4["v1b"] = kvr_ap[3584:4096, :]
    EX4["kvrbufs"] = [kvr]
    EX4["s0"] = s0t
    EX4["s1"] = s1t
    EX4["td"] = Buf(nc.dram_tensor("p4_td", [128, 4], F32, kind="ExternalInput").ap())
    EX4["tf"] = Buf(nc.dram_tensor("p4_tf", [128, 4], F32, kind="ExternalInput").ap())
    EX4["att0"] = atts.ap[0:512, 0:HT]
    EX4["att1"] = atts.ap[0:512, HT:T]
    EX4["att2"] = (atts.ap[512:1024, 0:HT], atts.ap[512:1024, HT:T])
    EX4["attbufs"] = [atts]
    EX4["attbnd"] = [bndAs.ap[0, :].rearrange("(p g) -> p g", p=128),
                     bndAs.ap[1, :].rearrange("(p g) -> p g", p=128)]
    emit_phase4(nc, sc, EX4)
    sc.pop_scope()
    allred(atts, attr_, BF16)
    allred(bndAs, bndAr, BF16)
    if FUSED_STAGE < 5:
        sc.emit_and_close()
        return nc

    sc.prefix = "f5_"
    sc.push_scope()
    EX5 = {}
    EX5["x2"] = x2_d
    EX5["x2p"] = x2p_d
    EX5["sr"] = sr_d
    EX5["att"] = attr_
    EX5["attin2"] = attr_.ap[:, HT:T]
    EX5["s0"] = s0t
    EX5["s1"] = s1t
    EX5["pm"] = Buf(nc.dram_tensor("p5_pm", [1, 1], F32, kind="ExternalInput").ap())
    EX5["wo"] = Buf(nc.dram_tensor("p5_wo", [C, C], F32, kind="ExternalInput").ap())
    EX5["wfr"] = Buf(nc.dram_tensor("p5_wfr", [C, C], F32, kind="ExternalInput").ap())
    EX5["wfk"] = Buf(nc.dram_tensor("p5_wfk", [C, 4096], F32, kind="ExternalInput").ap())
    EX5["wfv"] = Buf(nc.dram_tensor("p5_wfv", [4096, C], F32, kind="ExternalInput").ap())
    for nm in ("l2g", "l2b", "cmk", "cmr"):
        EX5["c_" + nm] = Buf(nc.dram_tensor(f"p5_{nm}", [128, NG], F32,
                                            kind="ExternalInput").ap())
    EX5["o"] = x4_d
    apvb = sc.sb("apvb", [128, NG], BF16)
    apv = sc.sb("apv", [128, NG], F32)
    spv = sc.sb("spv", [128, NG], F32)
    xpv = sc.sb("xpv", [128, NG], F32)
    sc.dma(apvb[:, 0:4], bndAr.ap[0, :].rearrange("(p g) -> p g", p=128),
           reads=[bndAr], writes=[apvb])
    sc.dma(apvb[:, 4:8], bndAr.ap[1, :].rearrange("(p g) -> p g", p=128),
           reads=[bndAr], writes=[apvb], noww=True)
    sc.op("vector", lambda e: e.tensor_copy(apv[:], apvb[:]),
          reads=[apvb], writes=[apv])
    sc.dma(xpv[:], bnd3r.ap[0, :].rearrange("(p g) -> p g", p=128),
           reads=[bnd3r], writes=[xpv])
    sc.dma(spv[:], bnd3r.ap[1, :].rearrange("(p g) -> p g", p=128),
           reads=[bnd3r], writes=[spv])
    EX5["apv"] = apv
    EX5["spv"] = spv
    EX5["xpv"] = xpv
    emit_phase5(nc, sc, ct, EX5)
    sc.pop_scope()
    if FUSED_STAGE < 6:
        sc.emit_and_close()
        return nc

    sc.prefix = "f6_"
    sc.push_scope()
    EX6 = _moe_external(nc, "p6", 8, 1408, True, with_x=False)
    EX6["x"] = x4_d
    EX6["iotam64"] = iotam64
    EX6["o"] = p6o
    emit_moe(nc, sc, ct, EX6, "p6", 8, 1408, 768, True, True)
    sc.pop_scope()

    sc.emit_and_close()
    return nc


def run_fused(inputs):
    co = _get_consts()
    global _FUSED_NC
    if _FUSED_NC is None:
        _FUSED_NC = build_fused()
    nc = _FUSED_NC
    ins = {k: np.asarray(v) for k, v in inputs.items()}
    f32 = np.float32
    xf = host_fm(ins["L0_attn_x"])
    xrevf = xf[:, :, ::-1]
    rw_vec = np.asarray(ins["L0_attn_router_w"])[0]
    rb = np.asarray(ins["L0_attn_router_b"])[0]
    wkT = np.ascontiguousarray(np.asarray(ins["L1_attn_Wk"], f32).T)
    wvT = np.ascontiguousarray(np.asarray(ins["L1_attn_Wv"], f32).T)
    wrT = np.ascontiguousarray(np.asarray(ins["L1_attn_Wr"], f32).T)
    woT = np.ascontiguousarray(np.asarray(ins["L1_attn_Wo"], f32).T)
    wfrT = np.ascontiguousarray(np.asarray(ins["L1_attn_Wfr"], f32).T)
    wfkT = np.ascontiguousarray(np.asarray(ins["L1_attn_Wfk"], f32).T)
    wfvT = np.ascontiguousarray(np.asarray(ins["L1_attn_Wfv"], f32).T)
    td = np.asarray(ins["L1_attn_time_decay"], f32)
    tf = np.asarray(ins["L1_attn_time_first"], f32)
    in_maps = []
    for core in range(NCORES):
        b, h = core // 2, core % 2
        m = dict(co)
        m["p1_x"] = np.ascontiguousarray(xf[b])
        m["p1_xh"] = np.ascontiguousarray(xf[b, :, h * HT:(h + 1) * HT])
        m["p1_xrev"] = np.ascontiguousarray(xrevf[b, :, h * HT:(h + 1) * HT])
        m["p1_rw"] = host_vec8(rw_vec)
        m["p1_rb"] = np.full((128, 1), float(rb), f32)
        m["p1_hsel"] = np.array([[1.0 - h, float(h)]], f32)
        m["c_h0"] = np.full((128, 1), 1.0 - h, f32)
        m["c_h1"] = np.full((128, 1), float(h), f32)
        m["p2_rw"] = np.ascontiguousarray(np.asarray(ins["L0_ffn_router_w"], f32))
        m["p2_wg"] = np.ascontiguousarray(np.asarray(ins["L0_ffn_gate_w"], f32))
        m["p2_wu"] = np.ascontiguousarray(np.asarray(ins["L0_ffn_up_w"], f32))
        m["p2_wd"] = np.ascontiguousarray(np.asarray(ins["L0_ffn_down_w"], f32))
        m["p3_pm"] = np.array([[float(h)]], f32)
        m["p3_wk"] = wkT
        m["p3_wv"] = wvT
        m["p3_wr"] = wrT
        m["p3_l0g"] = host_vec8(ins["L1_attn_ln0_w"])
        m["p3_l0b"] = host_vec8(ins["L1_attn_ln0_b"])
        m["p3_l1g"] = host_vec8(ins["L1_attn_ln1_w"])
        m["p3_l1b"] = host_vec8(ins["L1_attn_ln1_b"])
        m["p3_tmk"] = host_vec8(ins["L1_attn_tmk"])
        m["p3_tmv"] = host_vec8(ins["L1_attn_tmv"])
        m["p3_tmr"] = host_vec8(ins["L1_attn_tmr"])
        sl = slice(h * 512, (h + 1) * 512)
        m["p4_td"] = np.ascontiguousarray(td[sl].reshape(4, 128).T)
        m["p4_tf"] = np.ascontiguousarray(tf[sl].reshape(4, 128).T)
        m["p5_pm"] = np.array([[float(h)]], f32)
        m["p5_wo"] = woT
        m["p5_wfr"] = wfrT
        m["p5_wfk"] = wfkT
        m["p5_wfv"] = wfvT
        m["p5_l2g"] = host_vec8(ins["L1_attn_ln2_w"])
        m["p5_l2b"] = host_vec8(ins["L1_attn_ln2_b"])
        m["p5_cmk"] = host_vec8(ins["L1_attn_cmk"])
        m["p5_cmr"] = host_vec8(ins["L1_attn_cmr"])
        m["p6_rw"] = np.ascontiguousarray(np.asarray(ins["L1_ffn_w_router"], f32).T)
        m["p6_wg"] = np.ascontiguousarray(np.asarray(ins["L1_ffn_wg"], f32))
        m["p6_wu"] = np.ascontiguousarray(np.asarray(ins["L1_ffn_wu"], f32))
        m["p6_wd"] = np.ascontiguousarray(np.asarray(ins["L1_ffn_wd"], f32))
        m["p6_wgs"] = np.ascontiguousarray(np.asarray(ins["L1_ffn_wg_s"], f32))
        m["p6_wus"] = np.ascontiguousarray(np.asarray(ins["L1_ffn_wu_s"], f32))
        m["p6_wds"] = np.ascontiguousarray(np.asarray(ins["L1_ffn_wd_s"], f32))
        in_maps.append(m)
    res = run_bass_kernel_spmd(nc, in_maps, core_ids=list(range(NCORES)))
    out = np.zeros((B, C, T), np.float32)
    for core in range(NCORES):
        b, h = core // 2, core % 2
        out[b, :, h * HT:(h + 1) * HT] = res.results[core]["p6_o"]
    return out


_FUSED_NC = None


_CONSTS = None


def _get_consts():
    global _CONSTS
    if _CONSTS is None:
        _CONSTS = consts_host()
    return _CONSTS


def run_phase1(xf, rw_vec, rb):
    co = _get_consts()
    nc = build_phase1()
    xrevf = xf[:, :, ::-1]
    in_maps = []
    for core in range(NCORES):
        b, h = core // 2, core % 2
        m = dict(co)
        m["p1_x"] = np.ascontiguousarray(xf[b])
        m["p1_xh"] = np.ascontiguousarray(xf[b, :, h * HT:(h + 1) * HT])
        m["p1_xrev"] = np.ascontiguousarray(xrevf[b, :, h * HT:(h + 1) * HT])
        m["p1_rw"] = host_vec8(rw_vec)
        m["p1_rb"] = np.full((128, 1), float(rb), np.float32)
        m["p1_hsel"] = np.array([[1.0 - h, float(h)]], np.float32)
        in_maps.append(m)
    res = run_bass_kernel_spmd(nc, in_maps, core_ids=list(range(NCORES)))
    x1 = np.zeros((B, C, T), np.float32)
    for core in range(NCORES):
        b, h = core // 2, core % 2
        x1[b, :, h * HT:(h + 1) * HT] = res.results[core]["p1_x1"]
    return x1


_MOE_NC = {}


def run_moe_phase(tag, E, IH, CAP, renorm, xin, rw, wg, wu, wd,
                  wgs=None, wus=None, wds=None):
    co = _get_consts()
    key = (tag, E, IH, CAP, renorm, wgs is not None)
    if key not in _MOE_NC:
        _MOE_NC[key] = build_moe_phase(tag, E, IH, CAP, renorm, wgs is not None)
    nc = _MOE_NC[key]
    rw = np.ascontiguousarray(np.asarray(rw, np.float32))
    wg = np.ascontiguousarray(np.asarray(wg, np.float32))
    wu = np.ascontiguousarray(np.asarray(wu, np.float32))
    wd = np.ascontiguousarray(np.asarray(wd, np.float32))
    in_maps = []
    for core in range(NCORES):
        b, h = core // 2, core % 2
        m = dict(co)
        m[f"{tag}_x"] = np.ascontiguousarray(xin[b, :, h * HT:(h + 1) * HT])
        m[f"{tag}_rw"] = rw
        m[f"{tag}_wg"] = wg
        m[f"{tag}_wu"] = wu
        m[f"{tag}_wd"] = wd
        if wgs is not None:
            m[f"{tag}_wgs"] = np.ascontiguousarray(np.asarray(wgs, np.float32))
            m[f"{tag}_wus"] = np.ascontiguousarray(np.asarray(wus, np.float32))
            m[f"{tag}_wds"] = np.ascontiguousarray(np.asarray(wds, np.float32))
        in_maps.append(m)
    res = run_bass_kernel_spmd(nc, in_maps, core_ids=list(range(NCORES)))
    out = np.zeros((B, C, T), np.float32)
    for core in range(NCORES):
        b, h = core // 2, core % 2
        out[b, :, h * HT:(h + 1) * HT] = res.results[core][f"{tag}_o"]
    return out


_P3_NC = None
_P4_NC = None
_P5_NC = None


def _slice_prev(arr_fm, b, h):
    if h == 0:
        return np.zeros((128, NG), np.float32)
    return host_vec8(arr_fm[b, :, h * HT - 1])


def run_phase3(x2f, ins):
    global _P3_NC
    if _P3_NC is None:
        _P3_NC = build_phase3()
    co = _get_consts()
    f32 = np.float32
    wkT = np.ascontiguousarray(np.asarray(ins["L1_attn_Wk"], f32).T)
    wvT = np.ascontiguousarray(np.asarray(ins["L1_attn_Wv"], f32).T)
    wrT = np.ascontiguousarray(np.asarray(ins["L1_attn_Wr"], f32).T)
    in_maps = []
    for core in range(NCORES):
        b, h = core // 2, core % 2
        m = dict(co)
        m["p3_x"] = np.ascontiguousarray(x2f[b, :, h * HT:(h + 1) * HT])
        m["p3_xprev"] = _slice_prev(x2f, b, h)
        m["p3_pm"] = np.array([[float(h)]], f32)
        m["p3_wk"] = wkT
        m["p3_wv"] = wvT
        m["p3_wr"] = wrT
        m["p3_l0g"] = host_vec8(ins["L1_attn_ln0_w"])
        m["p3_l0b"] = host_vec8(ins["L1_attn_ln0_b"])
        m["p3_l1g"] = host_vec8(ins["L1_attn_ln1_w"])
        m["p3_l1b"] = host_vec8(ins["L1_attn_ln1_b"])
        m["p3_tmk"] = host_vec8(ins["L1_attn_tmk"])
        m["p3_tmv"] = host_vec8(ins["L1_attn_tmv"])
        m["p3_tmr"] = host_vec8(ins["L1_attn_tmr"])
        in_maps.append(m)
    res = run_bass_kernel_spmd(nc=_P3_NC, in_maps=in_maps,
                               core_ids=list(range(NCORES)))
    kf = np.zeros((B, C, T), ml_dtypes.bfloat16)
    vf = np.zeros((B, C, T), ml_dtypes.bfloat16)
    srf = np.zeros((B, C, T), ml_dtypes.bfloat16)
    x2pf = np.zeros((B, C, T), np.float32)
    for core in range(NCORES):
        b, h = core // 2, core % 2
        sl = slice(h * HT, (h + 1) * HT)
        kf[b, :, sl] = res.results[core]["p3_k"]
        vf[b, :, sl] = res.results[core]["p3_v"]
        srf[b, :, sl] = res.results[core]["p3_sr"]
        x2pf[b, :, sl] = res.results[core]["p3_x2p"]
    return kf, vf, srf, x2pf


def run_phase4(kf, vf, ins):
    global _P4_NC
    if _P4_NC is None:
        _P4_NC = build_phase4()
    f32 = np.float32
    td = np.asarray(ins["L1_attn_time_decay"], f32)
    tf = np.asarray(ins["L1_attn_time_first"], f32)
    in_maps = []
    for core in range(NCORES):
        b, ch = core // 2, core % 2
        sl = slice(ch * 512, (ch + 1) * 512)
        m = {}
        m["p4_k"] = np.ascontiguousarray(kf[b, sl, :])
        m["p4_v"] = np.ascontiguousarray(vf[b, sl, :])
        m["p4_td"] = np.ascontiguousarray(td[sl].reshape(4, 128).T)
        m["p4_tf"] = np.ascontiguousarray(tf[sl].reshape(4, 128).T)
        in_maps.append(m)
    res = run_bass_kernel_spmd(nc=_P4_NC, in_maps=in_maps,
                               core_ids=list(range(NCORES)))
    att = np.zeros((B, C, T), ml_dtypes.bfloat16)
    for core in range(NCORES):
        b, ch = core // 2, core % 2
        att[b, ch * 512:(ch + 1) * 512, :] = res.results[core]["p4_att"]
    return att


def run_phase5(x2f, x2pf, srf, att, ins):
    global _P5_NC
    if _P5_NC is None:
        _P5_NC = build_phase5()
    co = _get_consts()
    f32 = np.float32
    woT = np.ascontiguousarray(np.asarray(ins["L1_attn_Wo"], f32).T)
    wfrT = np.ascontiguousarray(np.asarray(ins["L1_attn_Wfr"], f32).T)
    wfkT = np.ascontiguousarray(np.asarray(ins["L1_attn_Wfk"], f32).T)
    wfvT = np.ascontiguousarray(np.asarray(ins["L1_attn_Wfv"], f32).T)
    srf32 = np.asarray(srf, f32)
    att32 = np.asarray(att, f32)
    in_maps = []
    for core in range(NCORES):
        b, h = core // 2, core % 2
        sl = slice(h * HT, (h + 1) * HT)
        m = dict(co)
        m["p5_x2"] = np.ascontiguousarray(x2f[b, :, sl])
        m["p5_x2p"] = np.ascontiguousarray(x2pf[b, :, sl])
        m["p5_sr"] = np.ascontiguousarray(srf[b, :, sl])
        m["p5_att"] = np.ascontiguousarray(att[b, :, sl])
        m["p5_attprev"] = _slice_prev(att32, b, h)
        m["p5_srprev"] = _slice_prev(srf32, b, h)
        m["p5_x2pprev"] = _slice_prev(x2pf, b, h)
        m["p5_pm"] = np.array([[float(h)]], f32)
        m["p5_wo"] = woT
        m["p5_wfr"] = wfrT
        m["p5_wfk"] = wfkT
        m["p5_wfv"] = wfvT
        m["p5_l2g"] = host_vec8(ins["L1_attn_ln2_w"])
        m["p5_l2b"] = host_vec8(ins["L1_attn_ln2_b"])
        m["p5_cmk"] = host_vec8(ins["L1_attn_cmk"])
        m["p5_cmr"] = host_vec8(ins["L1_attn_cmr"])
        in_maps.append(m)
    res = run_bass_kernel_spmd(nc=_P5_NC, in_maps=in_maps,
                               core_ids=list(range(NCORES)))
    x4 = np.zeros((B, C, T), np.float32)
    for core in range(NCORES):
        b, h = core // 2, core % 2
        x4[b, :, h * HT:(h + 1) * HT] = res.results[core]["p5_o"]
    return x4


FUSED = True


def kernel(**inputs):
    _ = (inputs["L0_attn_wq"], inputs["L0_attn_wk"], inputs["L0_attn_wv"])
    if FUSED:
        out = run_fused(inputs)
        return np.ascontiguousarray(np.swapaxes(out, -1, -2))
    xf = host_fm(inputs["L0_attn_x"])
    x1 = run_phase1(xf, np.asarray(inputs["L0_attn_router_w"])[0],
                    np.asarray(inputs["L0_attn_router_b"])[0])
    x2 = run_moe_phase("p2", 16, 1024, 384, False, x1,
                       inputs["L0_ffn_router_w"], inputs["L0_ffn_gate_w"],
                       inputs["L0_ffn_up_w"], inputs["L0_ffn_down_w"])
    kf, vf, srf, x2pf = run_phase3(x2, inputs)
    att = run_phase4(kf, vf, inputs)
    x4 = run_phase5(x2, x2pf, srf, att, inputs)
    x5 = run_moe_phase("p6", 8, 1408, 768, True, x4,
                       np.asarray(inputs["L1_ffn_w_router"], np.float32).T,
                       inputs["L1_ffn_wg"], inputs["L1_ffn_wu"],
                       inputs["L1_ffn_wd"], wgs=inputs["L1_ffn_wg_s"],
                       wus=inputs["L1_ffn_wu_s"], wds=inputs["L1_ffn_wd_s"])
    return np.ascontiguousarray(np.swapaxes(x5, -1, -2))


# version 09 (final)
# speedup vs baseline: 1.0113x; 1.0113x over previous
import numpy as np
import ml_dtypes
import concourse.bass as bass
import concourse.mybir as mybir
from concourse.bass import IndirectOffsetOnAxis
from concourse.bass_utils import run_bass_kernel_spmd

F32 = mybir.dt.float32
BF16 = mybir.dt.bfloat16
I32 = mybir.dt.int32
U32 = mybir.dt.uint32
AF = mybir.ActivationFunctionType
ALU = mybir.AluOpType
AX = mybir.AxisListType.X

B, T, C = 4, 4096, 1024
NG = C // 128
TT = 512
HT = T // 2
NTC = HT // TT
EPS = 1e-5
NCORES = 8
BISECT_ITERS = 48


class Buf:
    def __init__(self, ap):
        self.ap = ap
        self.writer = None
        self.readers = []

    def __getitem__(self, idx):
        return self.ap[idx]


class Sched:

    def __init__(self, nc):
        self.nc = nc
        self.ops = []
        self.sems = {}
        self.ctx = []
        self._dmasems = {}
        self._outsem = None
        self._nsem = 0
        self._scopes = []

    def sem(self, name, dma=False):
        h = self.nc.semaphore(name).__enter__()
        s = {"h": h, "tot": 0, "name": name, "dma": dma}
        self.sems[name] = s
        self._nsem += 1
        return s

    def esem(self, engine):
        name = "S_" + engine
        if name not in self.sems:
            self.sem(name)
        return self.sems[name]

    DMAPOOL = {"hw": 18, "sw": 8, "cc": 6}

    def dmasem(self, buf, cls="hw"):
        if buf is None:
            if self._outsem is None:
                self._outsem = {}
            if cls not in self._outsem:
                self._outsem[cls] = self.sem(f"D_out_{cls}", dma=True)
            return self._outsem[cls]
        k = id(buf)
        if k in self._dmasems:
            s, c0 = self._dmasems[k]
            assert c0 == cls, f"buffer changed dma queue class {c0}->{cls}"
            return s
        n = sum(1 for (_, c0) in self._dmasems.values() if c0 == cls)
        pool = self.DMAPOOL[cls]
        name = f"D{cls}_{n % pool}"
        if n < pool:
            s = self.sem(name, dma=True)
        else:
            s = self.sems[name]
        self._dmasems[k] = (s, cls)
        return s

    def op(self, engine, fn, reads=(), writes=(), after=(), dma=False,
           dmabuf=None, noww=False, inc_amt=None):
        waits = {}

        def add(tok):
            if tok is None:
                return
            s, v = tok
            if s["dma"]:
                v = s["tot"]
            k = s["name"]
            waits[k] = (s, max(v, waits[k][1]) if k in waits else v)

        for t in after:
            add(t)
        for b in reads:
            add(b.writer)
        for b in writes:
            if not noww:
                add(b.writer)
            for t in b.readers:
                add(t)
        if dma:
            db = dmabuf if dmabuf is not None else (writes[0] if writes else None)
            cls = "cc" if inc_amt == 1 else ("sw" if engine == "gpsimd" else "hw")
            s = self.dmasem(db, cls)
            amt = 16 if inc_amt is None else inc_amt
        else:
            s = self.esem(engine)
            amt = 1
        s["tot"] += amt
        tok = (s, s["tot"])
        wl = [(ws, wv) for (ws, wv) in waits.values() if not (ws is s and wv >= tok[1])]
        self.ops.append((engine, fn, wl, (s, amt)))
        for b in reads:
            b.readers.append(tok)
        for b in writes:
            b.writer = tok
            b.readers = []
        return tok

    def dma(self, out, in_, reads=(), writes=(), after=(), engine="sync",
            noww=False):
        return self.op(engine, lambda e: e.dma_start(out=out, in_=in_),
                       reads=reads, writes=writes, after=after, dma=True,
                       noww=noww)

    def emit_and_close(self):
        lim = getattr(self, "limit", None)
        if lim is not None and lim < len(self.ops):
            self.ops = self.ops[:lim]
            tot = {}
            for (e, fn, waits, inc) in self.ops:
                s, amt = inc
                tot[s["name"]] = tot.get(s["name"], 0) + amt
            for s in self.sems.values():
                s["tot"] = tot.get(s["name"], 0)
        with self.nc.Block() as block:
            def mk(engname):
                def run(eng):
                    for (e, fn, waits, inc) in self.ops:
                        if e != engname:
                            continue
                        for (s, v) in waits:
                            eng.wait_ge(s["h"], v)
                        inst = fn(eng)
                        inst.then_inc(inc[0]["h"], inc[1])
                return run

            finals = [(s, s["tot"]) for s in self.sems.values()]
            self.op("sync", lambda e: e.nop(), after=finals)

            for e in ("sync", "vector", "scalar", "tensor", "gpsimd"):
                if any(o[0] == e for o in self.ops):
                    getattr(block, e)(mk(e))
        for h in reversed(self.ctx):
            pass
        for s in self.sems.values():
            pass

    def sb(self, name, shape, dtype):
        cm = self.nc.sbuf_tensor(getattr(self, "prefix", "") + name, shape, dtype)
        h = cm.__enter__()
        if self._scopes:
            self._scopes[-1].append(cm)
        else:
            self.ctx.append(cm)
        return Buf(h[:])

    def ps(self, name, shape, dtype=F32):
        cm = self.nc.psum_tensor(getattr(self, "prefix", "") + name, shape, dtype)
        h = cm.__enter__()
        if self._scopes:
            self._scopes[-1].append(cm)
        else:
            self.ctx.append(cm)
        return Buf(h[:])

    def push_scope(self):
        self._scopes.append([])

    def fence(self):
        finals = [(s, s["tot"]) for s in self.sems.values()]
        for e in ("sync", "vector", "scalar", "tensor", "gpsimd"):
            self.op(e, lambda en: en.nop(), after=finals)

    def pop_scope(self):
        self.fence()
        for cm in reversed(self._scopes.pop()):
            cm.__exit__(None, None, None)


def host_fm(x):
    return np.ascontiguousarray(np.swapaxes(np.asarray(x), -1, -2))


def host_vec8(v):
    return np.ascontiguousarray(np.asarray(v, np.float32).reshape(NG, 128).T)


def consts_host():
    return {
        "c_ident_bf": np.eye(128, dtype=ml_dtypes.bfloat16),
        "c_ident_f": np.eye(128, dtype=np.float32),
        "c_ones_row_f": np.ones((1, 128), np.float32),
        "c_ones_col_f": np.ones((128, 1), np.float32),
        "c_mean_col_bf": np.full((128, 1), 1.0 / C, ml_dtypes.bfloat16),
        "c_iota16": np.ascontiguousarray(
            np.broadcast_to(np.arange(16, dtype=np.float32), (128, 16))),
        "c_iotam64": np.ascontiguousarray(
            np.broadcast_to(np.arange(16, dtype=np.float32) - 64.0, (128, 16))),
    }


def declare_consts(nc):
    d = {}
    d["ident_bf"] = nc.dram_tensor("c_ident_bf", [128, 128], BF16, kind="ExternalInput").ap()
    d["ident_f"] = nc.dram_tensor("c_ident_f", [128, 128], F32, kind="ExternalInput").ap()
    d["ones_row_f"] = nc.dram_tensor("c_ones_row_f", [1, 128], F32, kind="ExternalInput").ap()
    d["ones_col_f"] = nc.dram_tensor("c_ones_col_f", [128, 1], F32, kind="ExternalInput").ap()
    d["mean_col_bf"] = nc.dram_tensor("c_mean_col_bf", [128, 1], BF16, kind="ExternalInput").ap()
    d["iota16"] = nc.dram_tensor("c_iota16", [128, 16], F32, kind="ExternalInput").ap()
    return d


class CT:
    def __init__(self, sc, co):
        self.identf = sc.sb("ct_identf", [128, 128], F32)
        self.identbf = sc.sb("ct_identbf", [128, 128], BF16)
        self.onesrow = sc.sb("ct_onesrow", [1, 128], F32)
        self.onescol = sc.sb("ct_onescol", [128, 1], F32)
        self.meancol = sc.sb("ct_meancol", [128, 1], BF16)
        self.iota16 = sc.sb("ct_iota16", [128, 16], F32)
        self.keps = sc.sb("ct_keps", [1, 1], F32)
        sc.op("vector", lambda e: e.memset(self.keps[:], float(EPS)),
              writes=[self.keps])
        sc.dma(self.identf[:], co["ident_f"], writes=[self.identf])
        sc.dma(self.identbf[:], co["ident_bf"], writes=[self.identbf])
        sc.dma(self.onesrow[:], co["ones_row_f"], writes=[self.onesrow])
        sc.dma(self.onescol[:], co["ones_col_f"], writes=[self.onescol])
        sc.dma(self.meancol[:], co["mean_col_bf"], writes=[self.meancol])
        sc.dma(self.iota16[:], co["iota16"], writes=[self.iota16])


class LNScratch:
    def __init__(self, sc, tag, n):
        self.n = n
        self.bf = sc.sb(f"{tag}_bf", [128, NG * n], BF16)
        self.sq0 = sc.sb(f"{tag}_sq0", [128, n], BF16)
        self.sq1 = sc.sb(f"{tag}_sq1", [128, n], BF16)
        self.st = sc.sb(f"{tag}_st", [1, 4 * n], F32)


def emit_ln(sc, ct, y, out, n, lns, psA, psB, psR, psN, gamma=None, beta=None,
            cast_out_bf=None):
    st = lns.st
    mu = st[0:1, 0:n]
    rstd = st[0:1, n:2 * n]
    nu = st[0:1, 2 * n:3 * n]
    var = st[0:1, 3 * n:4 * n]
    bf = lns.bf

    for g in range(NG):
        sc.op("scalar", lambda e, g=g: e.copy(bf[:, g * n:(g + 1) * n],
                                              y[:, g * n:(g + 1) * n]),
              reads=[y], writes=[bf])

    def mk_mu(e):
        for g in range(NG):
            i = e.matmul(psA[0:1, 0:n], ct.meancol[:], bf[:, g * n:(g + 1) * n],
                         start=(g == 0), stop=(g == NG - 1))
        return i

    sc.op("tensor", mk_mu, reads=[bf, ct.meancol], writes=[psA])

    sqs = [lns.sq0, lns.sq1]
    for g in range(NG):
        sq = sqs[g % 2]
        sc.op("scalar", lambda e, g=g, sq=sq: e.activation(
            sq[:, :], bf[:, g * n:(g + 1) * n], AF.Square), reads=[bf], writes=[sq])
        sc.op("tensor", lambda e, g=g, sq=sq: e.matmul(
            psB[0:1, 0:n], ct.meancol[:], sq[:, :],
            start=(g == 0), stop=(g == NG - 1)), reads=[sq, ct.meancol], writes=[psB])

    sc.op("vector", lambda e: e.tensor_copy(mu, psA[0:1, 0:n]),
          reads=[psA], writes=[st])
    sc.op("scalar", lambda e: e.activation(var, mu, AF.Square),
          reads=[st], writes=[st])
    sc.op("vector", lambda e: e.tensor_sub(var, psB[0:1, 0:n], var),
          reads=[psB, st], writes=[st])
    sc.op("scalar", lambda e: e.activation(rstd, var, AF.Ln,
                                           bias=ct.keps[0:1, 0:1]),
          reads=[st, ct.keps], writes=[st])
    sc.op("scalar", lambda e: e.activation(rstd, rstd, AF.Exp, scale=-0.5),
          reads=[st], writes=[st])
    sc.op("vector", lambda e: e.tensor_mul(nu, mu, rstd), reads=[st], writes=[st])
    sc.op("tensor", lambda e: e.matmul(psR[0:128, 0:n], ct.onesrow[:], rstd,
                                       start=True, stop=True),
          reads=[st, ct.onesrow], writes=[psR])
    sc.op("tensor", lambda e: e.matmul(psN[0:128, 0:n], ct.onesrow[:], nu,
                                       start=True, stop=True),
          reads=[st, ct.onesrow], writes=[psN])
    toks = []
    for g in range(NG):
        sl = slice(g * n, (g + 1) * n)
        sc.op("vector", lambda e, sl=sl: e.tensor_mul(out[:, sl], y[:, sl],
                                                      psR[0:128, 0:n]),
              reads=[y, psR], writes=[out])
        t = sc.op("vector", lambda e, sl=sl: e.tensor_sub(out[:, sl], out[:, sl],
                                                          psN[0:128, 0:n]),
                  reads=[psN, out], writes=[out])
        if gamma is not None:
            t = sc.op("vector", lambda e, sl=sl, g=g: e.tensor_scalar(
                out[:, sl], out[:, sl], gamma[:, g:g + 1], beta[:, g:g + 1],
                op0=ALU.mult, op1=ALU.add), reads=[gamma, beta, out], writes=[out])
        if cast_out_bf is not None:
            sc.op("scalar", lambda e, sl=sl: e.copy(cast_out_bf[:, sl], out[:, sl]),
                  reads=[out], writes=[cast_out_bf])
        toks.append(t)
    return toks


def emit_phase1(nc, sc, ct, E):
    x = E["x"].ap
    xrev = E["xrev"].ap
    rwv = E["rw"].ap
    rbv = E["rb"].ap
    hsel = E["hsel"].ap
    x1b = E["x1"]
    x1o = x1b.ap
    srw = nc.dram_tensor("p1_srw", [1, T], F32).ap()
    srm = nc.dram_tensor("p1_srm", [1, T], F32).ap()
    srwb = Buf(srw)
    srmb = Buf(srm)

    rwvt = sc.sb("rwvt", [128, NG], F32)
    rbt = sc.sb("rbt", [128, 1], F32)
    hselt = sc.sb("hselt", [1, 2], F32)
    xa = sc.sb("xa", [128, NG * TT], F32)
    xb = sc.sb("xb", [128, NG * TT], F32)
    xr0 = sc.sb("xr0", [128, NG * TT], F32)
    rwrow = sc.sb("rwrow", [1, T], F32)
    rwm = sc.sb("rwm", [128, 32], F32)
    flags = sc.sb("flags", [128, 32], F32)
    eqm = sc.sb("eqm", [128, 32], F32)
    cums = sc.sb("cums", [128, 32], F32)
    maskm = sc.sb("maskm", [128, 32], F32)
    fred = sc.sb("fred", [128, 1], F32)
    rowt = sc.sb("rowt", [128, 1], F32)
    midb = sc.sb("midb", [128, 1], F32)
    tstarb = sc.sb("tstarb", [128, 1], F32)
    mbc = sc.sb("mbc", [128, 1], F32)
    prefc = sc.sb("prefc", [128, 1], F32)
    sv = sc.sb("sv", [1, 16], F32)
    prow = sc.sb("prow", [1, 128], F32)
    prow2 = sc.sb("prow2", [1, 128], F32)
    mrow = sc.sb("mrow", [1, TT], F32)
    maskbc = sc.sb("maskbc", [128, TT], F32)
    ytile = sc.sb("ytile", [128, NG * TT], F32)
    lns = LNScratch(sc, "p1ln", TT)

    p0 = sc.ps("p0", [128, TT])
    p1 = sc.ps("p1", [128, TT])
    p2 = sc.ps("p2", [128, TT])
    p3 = sc.ps("p3", [128, TT])
    p4 = sc.ps("p4", [128, TT])
    p5 = sc.ps("p5", [128, TT])

    sc.dma(rwvt[:], rwv, writes=[rwvt])
    sc.dma(rbt[:], rbv, writes=[rbt])
    sc.dma(hselt[:], hsel, writes=[hselt])

    xt = [xa, xb]
    for tc in range(T // TT):
        xx = xt[tc % 2]
        for g in range(NG):
            sc.dma(xx[:, g * TT:(g + 1) * TT],
                   x[g * 128:(g + 1) * 128, tc * TT:(tc + 1) * TT], writes=[xx])

        def mmrw(e, xx=xx):
            for g in range(NG):
                i = e.matmul(p0[0:1, 0:TT], rwvt[:, g:g + 1],
                             xx[:, g * TT:(g + 1) * TT],
                             start=(g == 0), stop=(g == NG - 1))
            return i
        sc.op("tensor", mmrw, reads=[xx, rwvt], writes=[p0])
        sc.op("vector", lambda e, tc=tc: e.tensor_copy(
            rwrow[0:1, tc * TT:(tc + 1) * TT], p0[0:1, 0:TT]),
            reads=[p0], writes=[rwrow])

    sc.dma(srw, rwrow[:], reads=[rwrow], writes=[srwb])
    sc.dma(rwm[:], srw.rearrange("o (p j) -> (o p) j", p=128),
           reads=[srwb], writes=[rwm])
    sc.op("vector", lambda e: e.tensor_scalar_add(rwm[:], rwm[:], rbt[:]),
          reads=[rbt], writes=[rwm])

    sc.op("vector", lambda e: e.tensor_reduce(fred[:], rwm[:], axis=AX, op=ALU.min),
          reads=[rwm], writes=[fred])
    sc.op("tensor", lambda e: e.matmul(p1[0:1, 0:128], fred[:], ct.identf[:],
                                       start=True, stop=True),
          reads=[fred, ct.identf], writes=[p1])
    sc.op("vector", lambda e: e.tensor_reduce(sv[0:1, 0:1], p1[0:1, 0:128],
                                              axis=AX, op=ALU.min),
          reads=[p1], writes=[sv])
    sc.op("vector", lambda e: e.tensor_reduce(fred[:], rwm[:], axis=AX, op=ALU.max),
          reads=[rwm], writes=[fred])
    sc.op("tensor", lambda e: e.matmul(p1[0:1, 0:128], fred[:], ct.identf[:],
                                       start=True, stop=True),
          reads=[fred, ct.identf], writes=[p1])
    sc.op("vector", lambda e: e.tensor_reduce(sv[0:1, 1:2], p1[0:1, 0:128],
                                              axis=AX, op=ALU.max),
          reads=[p1], writes=[sv])

    for it in range(BISECT_ITERS):
        sc.op("vector", lambda e: e.tensor_scalar(
            sv[0:1, 2:3], sv[0:1, 0:1], sv[0:1, 1:2], 0.5,
            op0=ALU.add, op1=ALU.mult), reads=[], writes=[sv])
        sc.op("tensor", lambda e: e.matmul(p1[0:128, 0:1], ct.onesrow[:],
                                           sv[0:1, 2:3], start=True, stop=True),
              reads=[sv, ct.onesrow], writes=[p1])
        sc.op("vector", lambda e: e.tensor_copy(midb[:], p1[0:128, 0:1]),
              reads=[p1], writes=[midb])
        sc.op("vector", lambda e: e.tensor_scalar(
            flags[:], rwm[:], midb[:], 0.0, op0=ALU.is_ge, op1=ALU.add,
            accum_out=fred[:]), reads=[rwm, midb], writes=[flags, fred])
        sc.op("tensor", lambda e: e.matmul(p1[0:1, 1:2], fred[:], ct.onescol[:],
                                           start=True, stop=True),
              reads=[fred, ct.onescol], writes=[p1])
        sc.op("vector", lambda e: e.tensor_copy(sv[0:1, 3:4], p1[0:1, 1:2]),
              reads=[p1], writes=[sv])
        sc.op("vector", lambda e: e.tensor_scalar(
            sv[0:1, 4:5], sv[0:1, 3:4], 1024.0, None, op0=ALU.is_ge),
            reads=[], writes=[sv])
        sc.op("vector", lambda e: e.tensor_scalar(
            sv[0:1, 7:8], sv[0:1, 2:3], sv[0:1, 0:1], None, op0=ALU.subtract),
            reads=[], writes=[sv])
        sc.op("vector", lambda e: e.scalar_tensor_tensor(
            sv[0:1, 0:1], sv[0:1, 7:8], sv[0:1, 4:5], sv[0:1, 0:1],
            op0=ALU.mult, op1=ALU.add), reads=[], writes=[sv])
        sc.op("vector", lambda e: e.tensor_scalar(
            sv[0:1, 8:9], sv[0:1, 1:2], sv[0:1, 2:3], None, op0=ALU.subtract),
            reads=[], writes=[sv])
        sc.op("vector", lambda e: e.scalar_tensor_tensor(
            sv[0:1, 1:2], sv[0:1, 8:9], sv[0:1, 4:5], sv[0:1, 2:3],
            op0=ALU.mult, op1=ALU.add), reads=[], writes=[sv])

    sc.op("tensor", lambda e: e.matmul(p1[0:128, 0:1], ct.onesrow[:],
                                       sv[0:1, 0:1], start=True, stop=True),
          reads=[sv, ct.onesrow], writes=[p1])
    sc.op("vector", lambda e: e.tensor_copy(tstarb[:], p1[0:128, 0:1]),
          reads=[p1], writes=[tstarb])
    sc.op("vector", lambda e: e.tensor_scalar(
        flags[:], rwm[:], tstarb[:], 0.0, op0=ALU.is_gt, op1=ALU.add,
        accum_out=fred[:]), reads=[rwm, tstarb], writes=[flags, fred])
    sc.op("tensor", lambda e: e.matmul(p1[0:1, 1:2], fred[:], ct.onescol[:],
                                       start=True, stop=True),
          reads=[fred, ct.onescol], writes=[p1])
    sc.op("vector", lambda e: e.tensor_copy(sv[0:1, 5:6], p1[0:1, 1:2]),
          reads=[p1], writes=[sv])
    sc.op("vector", lambda e: e.tensor_scalar(
        sv[0:1, 6:7], sv[0:1, 5:6], -1.0, 1024.0, op0=ALU.mult, op1=ALU.add),
        reads=[], writes=[sv])
    sc.op("vector", lambda e: e.tensor_scalar(
        eqm[:], rwm[:], tstarb[:], 0.0, op0=ALU.is_equal, op1=ALU.add,
        accum_out=rowt[:]), reads=[rwm, tstarb], writes=[eqm, rowt])
    sc.op("vector", lambda e: e.tensor_tensor_scan(
        cums[:], ct.onescol.ap.to_broadcast((128, 32)), eqm[:], 0.0,
        op0=ALU.mult, op1=ALU.add), reads=[ct.onescol, eqm], writes=[cums])
    sc.op("tensor", lambda e: e.matmul(p1[0:1, 0:128], rowt[:], ct.identf[:],
                                       start=True, stop=True),
          reads=[rowt, ct.identf], writes=[p1])
    sc.op("vector", lambda e: e.tensor_copy(prow[:], p1[0:1, 0:128]),
          reads=[p1], writes=[prow])
    sc.op("vector", lambda e: e.tensor_tensor_scan(
        prow2[:], ct.onesrow[:], prow[:], 0.0, op0=ALU.mult, op1=ALU.add),
        reads=[ct.onesrow, prow], writes=[prow2])
    sc.op("vector", lambda e: e.tensor_sub(prow2[:], prow2[:], prow[:]),
          reads=[prow], writes=[prow2])
    sc.op("tensor", lambda e: e.matmul(p1[0:128, 2:3], prow2[:],
                                       ct.onesrow[0:1, 0:1], start=True, stop=True),
          reads=[prow2, ct.onesrow], writes=[p1])
    sc.op("vector", lambda e: e.tensor_copy(prefc[:], p1[0:128, 2:3]),
          reads=[p1], writes=[prefc])
    sc.op("vector", lambda e: e.tensor_scalar_add(cums[:], cums[:], prefc[:]),
          reads=[prefc], writes=[cums])
    sc.op("tensor", lambda e: e.matmul(p1[0:128, 3:4], ct.onesrow[:],
                                       sv[0:1, 6:7], start=True, stop=True),
          reads=[sv, ct.onesrow], writes=[p1])
    sc.op("vector", lambda e: e.tensor_copy(mbc[:], p1[0:128, 3:4]),
          reads=[p1], writes=[mbc])
    sc.op("vector", lambda e: e.tensor_scalar(
        cums[:], cums[:], mbc[:], None, op0=ALU.is_le), reads=[mbc], writes=[cums])
    sc.op("vector", lambda e: e.tensor_mul(eqm[:], eqm[:], cums[:]),
          reads=[cums], writes=[eqm])
    sc.op("vector", lambda e: e.tensor_add(maskm[:], flags[:], eqm[:]),
          reads=[flags, eqm], writes=[maskm])
    sc.dma(srm.rearrange("o (p j) -> (o p) j", p=128), maskm[:],
           reads=[maskm], writes=[srmb])

    xh = E["xh"].ap
    mrowsel = sc.sb("mrowsel", [1, 2 * TT], F32)
    for tc in range(NTC):
        xx = xt[tc % 2]
        for g in range(NG):
            sc.dma(xx[:, g * TT:(g + 1) * TT],
                   xh[g * 128:(g + 1) * 128, tc * TT:(tc + 1) * TT], writes=[xx])
        for g in range(NG):
            sc.dma(xr0[:, g * TT:(g + 1) * TT],
                   xrev[g * 128:(g + 1) * 128, tc * TT:(tc + 1) * TT], writes=[xr0])
        sc.dma(mrowsel[0:1, 0:TT], srm[0:1, tc * TT:(tc + 1) * TT],
               reads=[srmb], writes=[mrowsel])
        sc.dma(mrowsel[0:1, TT:2 * TT], srm[0:1, HT + tc * TT:HT + (tc + 1) * TT],
               reads=[srmb], writes=[mrowsel])
        sc.op("vector", lambda e: e.tensor_scalar(
            mrow[:], mrowsel[0:1, 0:TT], hselt[0:1, 0:1], None, op0=ALU.mult),
            reads=[mrowsel, hselt], writes=[mrow])
        sc.op("vector", lambda e: e.scalar_tensor_tensor(
            mrow[:], mrowsel[0:1, TT:2 * TT], hselt[0:1, 1:2], mrow[:],
            op0=ALU.mult, op1=ALU.add), reads=[mrowsel, hselt], writes=[mrow])
        sc.op("tensor", lambda e: e.matmul(p2[0:128, 0:TT], ct.onesrow[:],
                                           mrow[:], start=True, stop=True),
              reads=[mrow, ct.onesrow], writes=[p2])
        sc.op("vector", lambda e: e.tensor_copy(maskbc[:], p2[0:128, 0:TT]),
              reads=[p2], writes=[maskbc])
        for g in range(NG):
            sl = slice(g * TT, (g + 1) * TT)
            sc.op("vector", lambda e, sl=sl: e.tensor_mul(
                ytile[:, sl], xr0[:, sl], maskbc[:]),
                reads=[xr0, maskbc], writes=[ytile])
            sc.op("vector", lambda e, sl=sl, xx=xx: e.tensor_add(
                ytile[:, sl], ytile[:, sl], xx[:, sl]),
                reads=[xx], writes=[ytile])
        emit_ln(sc, ct, ytile, ytile, TT, lns, p3, p4, p5, p2)
        for g in range(NG):
            sc.dma(x1o[g * 128:(g + 1) * 128, tc * TT:(tc + 1) * TT],
                   ytile[:, g * TT:(g + 1) * TT], reads=[ytile], writes=[x1b],
                   engine="gpsimd", noww=True)


def build_phase1():
    nc = bass.Bass("TRN2", target_bir_lowering=False)
    E = {}
    E["x"] = Buf(nc.dram_tensor("p1_x", [C, T], F32, kind="ExternalInput").ap())
    E["xh"] = Buf(nc.dram_tensor("p1_xh", [C, HT], F32, kind="ExternalInput").ap())
    E["xrev"] = Buf(nc.dram_tensor("p1_xrev", [C, HT], F32, kind="ExternalInput").ap())
    E["rw"] = Buf(nc.dram_tensor("p1_rw", [128, NG], F32, kind="ExternalInput").ap())
    E["rb"] = Buf(nc.dram_tensor("p1_rb", [128, 1], F32, kind="ExternalInput").ap())
    E["hsel"] = Buf(nc.dram_tensor("p1_hsel", [1, 2], F32, kind="ExternalInput").ap())
    E["x1"] = Buf(nc.dram_tensor("p1_x1", [C, HT], F32, kind="ExternalOutput").ap())
    co = declare_consts(nc)
    sc = Sched(nc)
    ct = CT(sc, co)
    emit_phase1(nc, sc, ct, E)
    sc.emit_and_close()
    return nc


MOE_DEBUG = 0
MOE_STAGE = 99
MOE_POSLIM = 99


def emit_moe(nc, sc, ct, EX, tag, E, IH, CAP, renorm, shared):
    assert IH % 128 == 0 or IH == 1408
    NIH = (IH + 127) // 128
    IHS = []
    o = 0
    while o < IH:
        w = min(512, IH - o)
        IHS.append((o, o + w))
        o += w
    NTILES = HT // 128
    SLOTS = E * CAP
    NST = CAP // 128

    xB = EX["x"]
    x = xB.ap
    rw = EX["rw"].ap
    wg = EX["wg"].ap
    wu = EX["wu"].ap
    wd = EX["wd"].ap
    if shared:
        wgs = EX["wgs"].ap
        wus = EX["wus"].ap
        wds = EX["wds"].ap
    iom = EX["iotam64"].ap
    outB = EX["o"]
    out_o = outB.ap
    xg_d = Buf(nc.dram_tensor(f"{tag}_xg", [SLOTS, C], BF16).ap())
    yb_d = Buf(nc.dram_tensor(f"{tag}_yb", [SLOTS, C], BF16).ap())
    zt_ = sc_zt_holder = None
    if MOE_DEBUG:
        dbg_a1 = nc.dram_tensor(f"{tag}_dbg_a1", [128, HT // 128], I32,
                                kind="ExternalOutput").ap()
        dbg_a2 = nc.dram_tensor(f"{tag}_dbg_a2", [128, HT // 128], I32,
                                kind="ExternalOutput").ap()
        dbg_idx = nc.dram_tensor(f"{tag}_dbg_idx", [128, 2 * (HT // 128)], F32,
                                 kind="ExternalOutput").ap()
        dbg_pos = nc.dram_tensor(f"{tag}_dbg_pos", [128, 2 * (HT // 128)], F32,
                                 kind="ExternalOutput").ap()
        dbg_w = nc.dram_tensor(f"{tag}_dbg_w", [128, 2 * (HT // 128)], F32,
                               kind="ExternalOutput").ap()

    iotam = sc.sb("iotam", [128, 16], F32)
    sc.dma(iotam[:], iom, writes=[iotam])
    zt = sc.sb("zt", [128, C], BF16)
    sc.op("vector", lambda e: e.memset(zt[:], 0.0), writes=[zt])
    for j in range(SLOTS // 128):
        sc.dma(xg_d.ap[j * 128:(j + 1) * 128, :], zt[:], reads=[zt],
               writes=[xg_d], noww=True, engine="gpsimd")

    xbf = sc.sb("xbf", [128, NG * HT], BF16)
    xa = sc.sb("xa", [128, NG * TT], F32)
    xb = xa if IH > 1024 else sc.sb("xb", [128, NG * TT], F32)
    rwt = sc.sb("rwt", [128, NG * E], F32)
    ltile = sc.sb("ltile", [128, 16], F32)
    m8 = sc.sb("m8", [128, 8], F32)
    nmax = sc.sb("nmax", [128, 1], F32)
    den = sc.sb("den", [128, 1], F32)
    rd = sc.sb("rd", [128, 1], F32)
    et = sc.sb("et", [128, 16], F32)
    e12 = sc.sb("e12", [128, 2], F32)
    eqt = sc.sb("eqt", [128, 16], F32)
    candt = sc.sb("candt", [128, 16], F32)
    ohbf = sc.sb("ohbf", [128, 16], BF16)
    ohT = sc.sb("ohT", [16, 128], F32)
    cumT = sc.sb("cumT", [16, 128], F32)
    runk = [sc.sb("run1", [16, 1], F32), sc.sb("run2", [16, 1], F32)]
    poscol = sc.sb("poscol", [128, 1], F32)
    rowred = sc.sb("rowred", [16, 1], F32)
    idxA = [sc.sb("idx1A", [128, NTILES], F32), sc.sb("idx2A", [128, NTILES], F32)]
    posA = [sc.sb("pos1A", [128, NTILES], F32), sc.sb("pos2A", [128, NTILES], F32)]
    wA = sc.sb("wA", [128, 2 * NTILES], F32)
    adrI = [sc.sb("a1I", [128, NTILES], I32), sc.sb("a2I", [128, NTILES], I32)]
    cnt1bc = sc.sb("cnt1bc", [128, 16], F32)
    cnt1row = sc.sb("cnt1row", [1, 16], F32)
    x1tm = sc.sb("x1tm", [128, C], BF16)
    adrF = sc.sb("adrF", [128, 2], F32)

    wgbf = sc.sb("wgbf", [128, NIH * 128 * 8 // 8], BF16)
    sc.ctx.pop()
    wgbf = sc.sb("wgbf2", [128, NG * IH], BF16)
    wubf = sc.sb("wubf", [128, NG * IH], BF16)
    wdbf = sc.sb("wdbf", [128, NIH * C], BF16)
    WMX = max(IH, C)
    NWT = 2 if IH > 1024 else 4
    wtmps = [sc.sb(f"wtmp{i}", [128, WMX], F32) for i in range(NWT)]
    wtmp = wtmps[0]
    wtmp2 = wtmps[1]
    _wrot = [0]

    def stage_w(dst_buf, dst_sl, src_ap, n):
        i = _wrot[0] % NWT
        _wrot[0] += 1
        wt = wtmps[i]
        deng = ("sync", "gpsimd", "scalar", "sync")[i]
        ceng = ("scalar", "vector", "gpsimd", "scalar")[i]
        sc.dma(wt[:, 0:n], src_ap, writes=[wt], engine=deng)
        if ceng == "scalar":
            sc.op(ceng, lambda e, wt=wt, dst_sl=dst_sl, n=n: e.copy(
                dst_buf[:, dst_sl], wt[:, 0:n]), reads=[wt], writes=[dst_buf])
        else:
            sc.op(ceng, lambda e, wt=wt, dst_sl=dst_sl, n=n: e.tensor_copy(
                dst_buf[:, dst_sl], wt[:, 0:n]), reads=[wt], writes=[dst_buf])
    xgt = [sc.sb(f"xgt{j}", [128, C], BF16) for j in range(2)]
    xgf = [sc.sb(f"xgf{j}", [128, NG * 128], BF16) for j in range(2)]
    hsil = sc.sb("hsil", [128, IH], F32)
    htm = sc.sb("htm", [128, IH], BF16)
    hfm = sc.sb("hfm", [128, NIH * 128], BF16)
    ytm = sc.sb("ytm", [128, C], BF16)
    y1 = sc.sb("y1", [128, C], BF16)
    y2 = sc.sb("y2", [128, C], BF16)
    ycomb = sc.sb("ycomb", [128, C], F32)
    ycbf = sc.sb("ycbf", [128, C], BF16)
    yfm = sc.sb("yfm", [128, NG * TT], F32)
    res = yfm
    lns = LNScratch(sc, f"{tag}ln", TT)
    if shared:
        wgsbf = wgbf
        wusbf = wubf
        wdsbf = wdbf

    pp = [sc.ps(f"pp{i}", [128, TT]) for i in range(6)]
    pt = sc.ps("pt", [128, TT], BF16)
    ptf = sc.ps("ptf", [128, TT])

    for k in range(2):
        sc.op("vector", lambda e, k=k: e.memset(runk[k][:], 0.0), writes=[runk[k]])

    sc.dma(rwt.ap.rearrange("p (g e) -> p g e", g=NG),
           rw.rearrange("(g p) e -> p g e", p=128), writes=[rwt])
    xt = [xa, xb]
    for tc in range(NTC):
        xx = xt[tc % 2]
        for g in range(NG):
            sc.dma(xx[:, g * TT:(g + 1) * TT],
                   x[g * 128:(g + 1) * 128, tc * TT:(tc + 1) * TT],
                   reads=[xB], writes=[xx])
        for g in range(NG):
            sc.op("scalar", lambda e, g=g, tc=tc, xx=xx: e.copy(
                xbf[:, g * HT + tc * TT:g * HT + (tc + 1) * TT],
                xx[:, g * TT:(g + 1) * TT]), reads=[xx], writes=[xbf])
        for ti in range(4):
            i = tc * 4 + ti
            if MOE_STAGE < 12:
                continue

            def mml(e, xx=xx, ti=ti, i=i):
                for g in range(NG):
                    r = e.matmul(ptf[0:128, (i % 4) * 128:(i % 4) * 128 + E],
                                 xx[:, g * TT + ti * 128:g * TT + (ti + 1) * 128],
                                 rwt[:, g * E:(g + 1) * E],
                                 start=(g == 0), stop=(g == NG - 1))
                return r
            sc.op("tensor", mml, reads=[xx, rwt], writes=[ptf])
            sc.op("vector", lambda e, i=i: e.tensor_copy(
                ltile[:, 0:E], ptf[0:128, (i % 4) * 128:(i % 4) * 128 + E]),
                reads=[ptf], writes=[ltile])
            if MOE_STAGE < 13:
                continue
            sc.op("vector", lambda e: e.max(out=m8[:], in_=ltile[:, 0:8] if E == 8
                                            else ltile[:, 0:16]),
                  reads=[ltile], writes=[m8])
            sc.op("vector", lambda e: e.reduce_max(nmax[:], ltile[:, 0:E], axis=AX,
                                                   negate=True),
                  reads=[ltile], writes=[nmax])
            sc.op("scalar", lambda e: e.activation(et[:, 0:E], ltile[:, 0:E], AF.Exp,
                                                   bias=nmax[:], accum_out=den[:]),
                  reads=[ltile, nmax], writes=[et, den])
            sc.op("scalar", lambda e: e.activation(e12[:], m8[:, 0:2], AF.Exp,
                                                   bias=nmax[:]),
                  reads=[m8, nmax], writes=[e12])
            sc.op("vector", lambda e: e.reciprocal(rd[:], den[:]),
                  reads=[den], writes=[rd])
            if renorm:
                sc.op("vector", lambda e: e.tensor_scalar_mul(e12[:], e12[:], rd[:]),
                      reads=[rd], writes=[e12])
                sc.op("vector", lambda e: e.tensor_scalar(
                    den[:], e12[:, 0:1], e12[:, 1:2], 1e-20,
                    op0=ALU.add, op1=ALU.add), reads=[e12], writes=[den])
                sc.op("vector", lambda e: e.reciprocal(rd[:], den[:]),
                      reads=[den], writes=[rd])
            t_w = sc.op("vector", lambda e, i=i: e.tensor_scalar_mul(
                wA[:, 2 * i:2 * i + 2], e12[:], rd[:]), reads=[e12, rd], writes=[wA])
            if MOE_STAGE < 14:
                continue
            sc.op("vector", lambda e: e.tensor_scalar(
                eqt[:, 0:E], ltile[:, 0:E], m8[:, 0:1], None, op0=ALU.is_equal),
                reads=[ltile, m8], writes=[eqt])
            sc.op("vector", lambda e: e.tensor_mul(candt[:, 0:E], eqt[:, 0:E],
                                                   iotam[:, 0:E]),
                  reads=[eqt, iotam], writes=[candt])
            sc.op("vector", lambda e, i=i: e.tensor_reduce(
                idxA[0][:, i:i + 1], candt[:, 0:E], axis=AX, op=ALU.min),
                reads=[candt], writes=[idxA[0]])
            sc.op("vector", lambda e, i=i: e.tensor_scalar_add(
                idxA[0][:, i:i + 1], idxA[0][:, i:i + 1], 64.0), writes=[idxA[0]])
            sc.op("vector", lambda e: e.tensor_scalar(
                eqt[:, 0:E], ltile[:, 0:E], m8[:, 1:2], None, op0=ALU.is_equal),
                reads=[ltile, m8], writes=[eqt])
            sc.op("vector", lambda e, i=i: e.tensor_scalar(
                candt[:, 0:E], iotam[:, 0:E], idxA[0][:, i:i + 1], None,
                op0=ALU.is_equal), reads=[iotam, idxA[0]], writes=[candt])
            sc.ops.pop()
            sc.esem("vector")["tot"] -= 1
            candt.readers = [r for r in candt.readers]
            sc.op("vector", lambda e, i=i: e.tensor_scalar(
                et[:, 0:E], iotam[:, 0:E], idxA[0][:, i:i + 1], -64.0,
                op0=ALU.subtract, op1=ALU.is_equal),
                reads=[iotam, idxA[0]], writes=[et])
            sc.op("vector", lambda e: e.tensor_mul(candt[:, 0:E], eqt[:, 0:E],
                                                   iotam[:, 0:E]),
                  reads=[eqt, iotam], writes=[candt])
            sc.op("vector", lambda e: e.scalar_tensor_tensor(
                candt[:, 0:E], et[:, 0:E], 64.0, candt[:, 0:E],
                op0=ALU.mult, op1=ALU.add), reads=[et], writes=[candt])
            sc.op("vector", lambda e, i=i: e.tensor_reduce(
                idxA[1][:, i:i + 1], candt[:, 0:E], axis=AX, op=ALU.min),
                reads=[candt], writes=[idxA[1]])
            sc.op("vector", lambda e, i=i: e.tensor_scalar_add(
                idxA[1][:, i:i + 1], idxA[1][:, i:i + 1], 64.0), writes=[idxA[1]])
            if MOE_STAGE < 15:
                continue
            for k in range(2):
                PL = [0]

                def pgate():
                    PL[0] += 1
                    return PL[0] <= MOE_POSLIM
                if pgate():
                    sc.op("vector", lambda e, i=i, k=k: e.tensor_scalar(
                        ohbf[:, 0:E], iotam[:, 0:E], idxA[k][:, i:i + 1], -64.0,
                        op0=ALU.subtract, op1=ALU.is_equal),
                        reads=[iotam, idxA[k]], writes=[ohbf])
                if pgate():
                    sc.op("tensor", lambda e: e.transpose(
                        pt[0:E, 0:128], ohbf[:, 0:E], ct.identbf[:]),
                        reads=[ohbf, ct.identbf], writes=[pt])
                if pgate():
                    sc.op("vector", lambda e: e.tensor_copy(ohT[0:E, :],
                                                            pt[0:E, 0:128]),
                          reads=[pt], writes=[ohT])
                if pgate():
                    sc.op("vector", lambda e: e.tensor_tensor_scan(
                        cumT[0:E, :], ct.onescol[0:E, 0:1].to_broadcast((E, 128)),
                        ohT[0:E, :], 0.0, op0=ALU.mult, op1=ALU.add),
                        reads=[ohT, ct.onescol], writes=[cumT])
                if pgate():
                    sc.op("vector", lambda e, k=k: e.tensor_scalar_add(
                        cumT[0:E, :], cumT[0:E, :], runk[k][0:E, :]),
                        reads=[runk[k]], writes=[cumT])
                if pgate():
                    sc.op("vector", lambda e: e.tensor_mul(cumT[0:E, :],
                                                           cumT[0:E, :],
                                                           ohT[0:E, :]),
                          reads=[ohT], writes=[cumT])
                if pgate():
                    sc.op("tensor", lambda e: e.matmul(
                        ptf[0:1, 128:256], ct.onescol[0:E, 0:1], cumT[0:E, :],
                        start=True, stop=True),
                        reads=[cumT, ct.onescol], writes=[ptf])
                if pgate():
                    sc.op("vector", lambda e: e.tensor_copy(ohT[0:1, 0:128],
                                                            ptf[0:1, 128:256]),
                          reads=[ptf], writes=[ohT])
                if pgate():
                    sc.op("tensor", lambda e: e.matmul(
                        ptf[0:128, 126:127], ohT[0:1, 0:128], ct.onesrow[0:1, 0:1],
                        start=True, stop=True), reads=[ohT, ct.onesrow], writes=[ptf])
                if pgate():
                    sc.op("vector", lambda e, i=i, k=k: e.tensor_scalar(
                        posA[k][:, i:i + 1], ptf[0:128, 126:127], -1.0, None,
                        op0=ALU.add), reads=[ptf], writes=[posA[k]])
                if pgate():
                    sc.op("vector", lambda e: e.tensor_reduce(
                        rowred[0:E, :], cumT[0:E, :], axis=AX, op=ALU.max),
                        reads=[cumT], writes=[rowred])
                if pgate():
                    sc.op("vector", lambda e, k=k: e.tensor_max(
                        runk[k][0:E, :], runk[k][0:E, :], rowred[0:E, :]),
                        reads=[rowred], writes=[runk[k]])


    if MOE_STAGE < 16:
        return

    sc.op("tensor", lambda e: e.matmul(ptf[0:1, 0:E], runk[0][0:E, :],
                                       ct.identf[0:E, 0:E], start=True, stop=True),
          reads=[runk[0], ct.identf], writes=[ptf])
    sc.op("vector", lambda e: e.tensor_copy(cnt1row[0:1, 0:E], ptf[0:1, 0:E]),
          reads=[ptf], writes=[cnt1row])
    sc.op("tensor", lambda e: e.matmul(ptf[0:128, 0:E], ct.onesrow[:],
                                       cnt1row[0:1, 0:E], start=True, stop=True),
          reads=[cnt1row, ct.onesrow], writes=[ptf])
    sc.op("vector", lambda e: e.tensor_copy(cnt1bc[:, 0:E], ptf[0:128, 0:E]),
          reads=[ptf], writes=[cnt1bc])

    for i in range(NTILES):
        sc.op("vector", lambda e, i=i: e.tensor_scalar_min(
            adrF[:, 0:1], posA[0][:, i:i + 1], float(CAP - 1)),
            reads=[posA[0]], writes=[adrF])
        sc.op("vector", lambda e, i=i: e.scalar_tensor_tensor(
            adrF[:, 0:1], idxA[0][:, i:i + 1], float(CAP), adrF[:, 0:1],
            op0=ALU.mult, op1=ALU.add), reads=[idxA[0]], writes=[adrF])
        sc.op("vector", lambda e, i=i: e.tensor_scalar(
            eqt[:, 0:E], iotam[:, 0:E], idxA[1][:, i:i + 1], -64.0,
            op0=ALU.subtract, op1=ALU.is_equal),
            reads=[iotam, idxA[1]], writes=[eqt])
        sc.op("vector", lambda e: e.tensor_mul(eqt[:, 0:E], eqt[:, 0:E],
                                               cnt1bc[:, 0:E]),
              reads=[cnt1bc], writes=[eqt])
        sc.op("vector", lambda e: e.tensor_reduce(den[:], eqt[:, 0:E], axis=AX,
                                                  op=ALU.add),
              reads=[eqt], writes=[den])
        sc.op("vector", lambda e, i=i: e.tensor_add(
            adrF[:, 1:2], posA[1][:, i:i + 1], den[:]),
            reads=[posA[1], den], writes=[adrF])
        sc.op("vector", lambda e: e.tensor_scalar_min(
            adrF[:, 1:2], adrF[:, 1:2], float(CAP - 1)), writes=[adrF])
        sc.op("vector", lambda e, i=i: e.scalar_tensor_tensor(
            adrF[:, 1:2], idxA[1][:, i:i + 1], float(CAP), adrF[:, 1:2],
            op0=ALU.mult, op1=ALU.add), reads=[idxA[1]], writes=[adrF])
        for k in range(2):
            sc.op("vector", lambda e, i=i, k=k: e.tensor_copy(
                adrI[k][:, i:i + 1], adrF[:, k:k + 1]), reads=[adrF],
                writes=[adrI[k]])
        for g in range(NG):
            sc.op("tensor", lambda e, i=i, g=g: e.transpose(
                pt[0:128, (g % 4) * 128:(g % 4 + 1) * 128],
                xbf[:, g * HT + i * 128:g * HT + (i + 1) * 128], ct.identbf[:]),
                reads=[xbf, ct.identbf], writes=[pt])
            if g % 4 == 3:
                sc.op("vector", lambda e, g=g: e.tensor_copy(
                    x1tm[:, (g - 3) * 128:(g + 1) * 128], pt[0:128, 0:512]),
                    reads=[pt], writes=[x1tm])
        if not MOE_DEBUG:
            for k in range(2):
                sc.op("gpsimd", lambda e, i=i, k=k: e.indirect_dma_start(
                    out=xg_d.ap, out_offset=IndirectOffsetOnAxis(
                        ap=adrI[k][:, i:i + 1], axis=0),
                    in_=x1tm[:], in_offset=None),
                    reads=[adrI[k], x1tm], writes=[xg_d], dma=True)

    if MOE_STAGE < 4:
        return

    for e_i in range(E):
        for g in range(NG):
            stage_w(wgbf, slice(g * IH, (g + 1) * IH),
                    wg[e_i, g * 128:(g + 1) * 128, :], IH)
            stage_w(wubf, slice(g * IH, (g + 1) * IH),
                    wu[e_i, g * 128:(g + 1) * 128, :], IH)
        for ic in range(NIH):
            stage_w(wdbf, slice(ic * C, (ic + 1) * C),
                    wd[e_i, ic * 128:(ic + 1) * 128, :], C)
        for j in range(NST):
            row0 = e_i * CAP + j * 128
            xgtj = xgt[j % 2]
            xgfj = xgf[j % 2]
            sc.dma(xgtj[:], xg_d.ap[row0:row0 + 128, :],
                   reads=[xg_d], writes=[xgtj])
            for g in range(NG):
                sc.op("tensor", lambda e, xgtj=xgtj, g=g: e.transpose(
                    pt[0:128, (g % 4) * 128:(g % 4 + 1) * 128],
                    xgtj[:, g * 128:(g + 1) * 128], ct.identbf[:]),
                    reads=[xgtj, ct.identbf], writes=[pt])
                if g % 4 == 3:
                    sc.op("vector", lambda e, xgfj=xgfj, g=g: e.tensor_copy(
                        xgfj[:, (g - 3) * 128:(g + 1) * 128], pt[0:128, 0:512]),
                        reads=[pt], writes=[xgfj])
            def mmgu(e, j=j, xgfj=xgfj):
                for g in range(NG):
                    for si, (i0, i1) in enumerate(IHS):
                        e.matmul(pp[si][0:128, 0:i1 - i0],
                                 xgfj[:, g * 128:(g + 1) * 128],
                                 wgbf[:, g * IH + i0:g * IH + i1],
                                 start=(g == 0), stop=(g == NG - 1))
                        r = e.matmul(pp[3 + si][0:128, 0:i1 - i0],
                                     xgfj[:, g * 128:(g + 1) * 128],
                                     wubf[:, g * IH + i0:g * IH + i1],
                                     start=(g == 0), stop=(g == NG - 1))
                return r
            sc.op("tensor", mmgu, reads=[xgfj, wgbf, wubf],
                  writes=[pp[si] for si in range(len(IHS))] +
                         [pp[3 + si] for si in range(len(IHS))])
            for si, (i0, i1) in enumerate(IHS):
                sc.op("scalar", lambda e, si=si, i0=i0, i1=i1: e.activation(
                    hsil[:, i0:i1], pp[si][0:128, 0:i1 - i0], AF.Silu),
                    reads=[pp[si]], writes=[hsil])
                sc.op("vector", lambda e, si=si, i0=i0, i1=i1: e.tensor_mul(
                    htm[:, i0:i1], hsil[:, i0:i1], pp[3 + si][0:128, 0:i1 - i0]),
                    reads=[hsil, pp[3 + si]], writes=[htm])
            for ic in range(NIH):
                sc.op("tensor", lambda e, ic=ic: e.transpose(
                    pt[0:128, (ic % 4) * 128:(ic % 4 + 1) * 128],
                    htm[:, ic * 128:(ic + 1) * 128], ct.identbf[:]),
                    reads=[htm, ct.identbf], writes=[pt])
                if ic % 4 == 3 or ic == NIH - 1:
                    lo = (ic // 4) * 4
                    sc.op("vector", lambda e, ic=ic, lo=lo: e.tensor_copy(
                        hfm[:, lo * 128:(ic + 1) * 128],
                        pt[0:128, 0:(ic + 1 - lo) * 128]),
                        reads=[pt], writes=[hfm])
            def mmd(e, j=j):
                for ic in range(NIH):
                    e.matmul(pp[0][0:128, 0:512],
                             hfm[:, ic * 128:(ic + 1) * 128],
                             wdbf[:, ic * C:ic * C + 512],
                             start=(ic == 0), stop=(ic == NIH - 1))
                    r = e.matmul(pp[1][0:128, 0:512],
                                 hfm[:, ic * 128:(ic + 1) * 128],
                                 wdbf[:, ic * C + 512:(ic + 1) * C],
                                 start=(ic == 0), stop=(ic == NIH - 1))
                return r
            sc.op("tensor", mmd, reads=[hfm, wdbf], writes=[pp[0], pp[1]])
            for half in range(2):
                sc.op("vector", lambda e, half=half: e.tensor_copy(
                    ytm[:, half * 512:(half + 1) * 512],
                    pp[half][0:128, 0:512]), reads=[pp[half]], writes=[ytm])
            sc.dma(yb_d.ap[row0 + 0:row0 + 128, :] if False else
                   yb_d.ap[e_i * CAP + j * 128:e_i * CAP + j * 128 + 128, :],
                   ytm[:], reads=[ytm], writes=[yb_d])

    if MOE_STAGE < 5:
        return

    if shared:
        for g in range(NG):
            stage_w(wgsbf, slice(g * IH, (g + 1) * IH),
                    wgs[g * 128:(g + 1) * 128, :], IH)
            stage_w(wusbf, slice(g * IH, (g + 1) * IH),
                    wus[g * 128:(g + 1) * 128, :], IH)
        for ic in range(NIH):
            stage_w(wdsbf, slice(ic * C, (ic + 1) * C),
                    wds[ic * 128:(ic + 1) * 128, :], C)

    for tc in range(NTC):
        xx = xt[tc % 2]
        for g in range(NG):
            sc.dma(xx[:, g * TT:(g + 1) * TT],
                   x[g * 128:(g + 1) * 128, tc * TT:(tc + 1) * TT],
                   reads=[xB], writes=[xx])
        for ti in range(4):
            i = tc * 4 + ti
            if not MOE_DEBUG:
                for k, ybuf in ((0, y1), (1, y2)):
                    sc.op("gpsimd", lambda e, i=i, k=k, ybuf=ybuf:
                          e.indirect_dma_start(
                              out=ybuf[:], out_offset=None,
                              in_=yb_d.ap, in_offset=IndirectOffsetOnAxis(
                                  ap=adrI[k][:, i:i + 1], axis=0)),
                          reads=[yb_d, adrI[k]], writes=[ybuf], dma=True)
            sc.op("vector", lambda e, i=i: e.tensor_scalar_mul(
                ycomb[:], y1[:], wA[:, 2 * i:2 * i + 1]),
                reads=[y1, wA], writes=[ycomb])
            sc.op("vector", lambda e, i=i: e.scalar_tensor_tensor(
                ycomb[:], y2[:], wA[:, 2 * i + 1:2 * i + 2], ycomb[:],
                op0=ALU.mult, op1=ALU.add), reads=[y2, wA], writes=[ycomb])
            if shared:
                def mmgus(e, i=i):
                    for g in range(NG):
                        for si, (i0, i1) in enumerate(IHS):
                            e.matmul(pp[si][0:128, 0:i1 - i0],
                                     xbf[:, g * HT + i * 128:g * HT + (i + 1) * 128],
                                     wgsbf[:, g * IH + i0:g * IH + i1],
                                     start=(g == 0), stop=(g == NG - 1))
                            r = e.matmul(pp[3 + si][0:128, 0:i1 - i0],
                                         xbf[:, g * HT + i * 128:g * HT + (i + 1) * 128],
                                         wusbf[:, g * IH + i0:g * IH + i1],
                                         start=(g == 0), stop=(g == NG - 1))
                    return r
                sc.op("tensor", mmgus, reads=[xbf, wgsbf, wusbf],
                      writes=[pp[si] for si in range(len(IHS))] +
                             [pp[3 + si] for si in range(len(IHS))])
                for si, (i0, i1) in enumerate(IHS):
                    sc.op("scalar", lambda e, si=si, i0=i0, i1=i1: e.activation(
                        hsil[:, i0:i1], pp[si][0:128, 0:i1 - i0], AF.Silu),
                        reads=[pp[si]], writes=[hsil])
                    sc.op("vector", lambda e, si=si, i0=i0, i1=i1: e.tensor_mul(
                        htm[:, i0:i1], hsil[:, i0:i1], pp[3 + si][0:128, 0:i1 - i0]),
                        reads=[hsil, pp[3 + si]], writes=[htm])
                for ic in range(NIH):
                    sc.op("tensor", lambda e, ic=ic: e.transpose(
                        pt[0:128, (ic % 4) * 128:(ic % 4 + 1) * 128],
                        htm[:, ic * 128:(ic + 1) * 128], ct.identbf[:]),
                        reads=[htm, ct.identbf], writes=[pt])
                    if ic % 4 == 3 or ic == NIH - 1:
                        lo = (ic // 4) * 4
                        sc.op("vector", lambda e, ic=ic, lo=lo: e.tensor_copy(
                            hfm[:, lo * 128:(ic + 1) * 128],
                            pt[0:128, 0:(ic + 1 - lo) * 128]),
                            reads=[pt], writes=[hfm])

                def mmds(e):
                    for ic in range(NIH):
                        e.matmul(pp[0][0:128, 0:512],
                                 hfm[:, ic * 128:(ic + 1) * 128],
                                 wdsbf[:, ic * C:ic * C + 512],
                                 start=(ic == 0), stop=(ic == NIH - 1))
                        r = e.matmul(pp[1][0:128, 0:512],
                                     hfm[:, ic * 128:(ic + 1) * 128],
                                     wdsbf[:, ic * C + 512:(ic + 1) * C],
                                     start=(ic == 0), stop=(ic == NIH - 1))
                    return r
                sc.op("tensor", mmds, reads=[hfm, wdsbf], writes=[pp[0], pp[1]])
                for half in range(2):
                    sc.op("vector", lambda e, half=half: e.tensor_add(
                        ycomb[:, half * 512:(half + 1) * 512],
                        ycomb[:, half * 512:(half + 1) * 512],
                        pp[half][0:128, 0:512]), reads=[pp[half]], writes=[ycomb])
            sc.op("scalar", lambda e: e.copy(ycbf[:], ycomb[:]),
                  reads=[ycomb], writes=[ycbf])
            for g in range(NG):
                sc.op("tensor", lambda e, g=g: e.transpose(
                    pt[0:128, 0:128],
                    ycbf[:, g * 128:(g + 1) * 128], ct.identbf[:]),
                    reads=[ycbf, ct.identbf], writes=[pt])
                sc.op("vector", lambda e, g=g, ti=ti: e.tensor_copy(
                    yfm[:, g * TT + ti * 128:g * TT + ti * 128 + 128],
                    pt[0:128, 0:128]), reads=[pt], writes=[yfm])
        for g in range(NG):
            sl = slice(g * TT, (g + 1) * TT)
            sc.op("vector", lambda e, sl=sl, xx=xx: e.tensor_add(
                res[:, sl], res[:, sl], xx[:, sl]),
                reads=[xx], writes=[res])
        emit_ln(sc, ct, res, res, TT, lns, pp[0], pp[1], pp[2], pp[3])
        for g in range(NG):
            sc.dma(out_o[g * 128:(g + 1) * 128, tc * TT:(tc + 1) * TT],
                   res[:, g * TT:(g + 1) * TT], reads=[res], writes=[outB],
                   engine="gpsimd", noww=True)
        if "bndx" in EX and tc == NTC - 1:
            xbndt = sc.sb(f"{tag}_xbndt", [128, NG], F32)
            xbnds = sc.sb(f"{tag}_xbnds", [128, NG], F32)
            sc.op("vector", lambda e: e.tensor_copy(
                xbndt[:, 0:NG], res.ap[:, TT - 1::TT]), reads=[res],
                writes=[xbndt])
            for half, sv_ in ((0, EX["s0"]), (1, EX["s1"])):
                sc.op("vector", lambda e, sv_=sv_: e.tensor_scalar_mul(
                    xbnds[:], xbndt[:], sv_[:]), reads=[xbndt, sv_],
                    writes=[xbnds])
                sc.dma(EX["bndx"].ap[half, :].rearrange("(p g) -> p g", p=128),
                       xbnds[:], reads=[xbnds], writes=[EX["bndx"]], noww=True)


def _moe_external(nc, tag, E, IH, shared, with_x=True):
    EX = {}
    if with_x:
        EX["x"] = Buf(nc.dram_tensor(f"{tag}_x", [C, HT], F32, kind="ExternalInput").ap())
    EX["rw"] = Buf(nc.dram_tensor(f"{tag}_rw", [C, E], F32, kind="ExternalInput").ap())
    EX["wg"] = Buf(nc.dram_tensor(f"{tag}_wg", [E, C, IH], F32, kind="ExternalInput").ap())
    EX["wu"] = Buf(nc.dram_tensor(f"{tag}_wu", [E, C, IH], F32, kind="ExternalInput").ap())
    EX["wd"] = Buf(nc.dram_tensor(f"{tag}_wd", [E, IH, C], F32, kind="ExternalInput").ap())
    if shared:
        EX["wgs"] = Buf(nc.dram_tensor(f"{tag}_wgs", [C, IH], F32, kind="ExternalInput").ap())
        EX["wus"] = Buf(nc.dram_tensor(f"{tag}_wus", [C, IH], F32, kind="ExternalInput").ap())
        EX["wds"] = Buf(nc.dram_tensor(f"{tag}_wds", [IH, C], F32, kind="ExternalInput").ap())
    return EX


def build_moe_phase(tag, E, IH, CAP, renorm, shared, final_gamma=False):
    nc = bass.Bass("TRN2", target_bir_lowering=False)
    EX = _moe_external(nc, tag, E, IH, shared)
    EX["iotam64"] = Buf(nc.dram_tensor("c_iotam64", [128, 16], F32,
                                       kind="ExternalInput").ap())
    EX["o"] = Buf(nc.dram_tensor(f"{tag}_o", [C, HT], F32,
                                 kind="ExternalOutput").ap())
    co = declare_consts(nc)
    sc = Sched(nc)
    ct = CT(sc, co)
    emit_moe(nc, sc, ct, EX, tag, E, IH, CAP, renorm, shared)
    sc.emit_and_close()
    return nc


def emit_ln_1tok(sc, ct, x, out, pps, sq1, sv1, mub, rsb, gamma=None, beta=None):
    sc.op("tensor", lambda e: e.matmul(pps[0:1, 0:NG], ct.onescol[:], x[:, 0:NG],
                                       start=True, stop=True),
          reads=[x, ct.onescol], writes=[pps])
    sc.op("vector", lambda e: e.tensor_reduce(sv1[0:1, 0:1], pps[0:1, 0:NG],
                                              axis=AX, op=ALU.add),
          reads=[pps], writes=[sv1])
    sc.op("scalar", lambda e: e.activation(sq1[:, 0:NG], x[:, 0:NG], AF.Square),
          reads=[x], writes=[sq1])
    sc.op("tensor", lambda e: e.matmul(pps[0:1, 8:8 + NG], ct.onescol[:],
                                       sq1[:, 0:NG], start=True, stop=True),
          reads=[sq1, ct.onescol], writes=[pps])
    sc.op("vector", lambda e: e.tensor_reduce(sv1[0:1, 1:2], pps[0:1, 8:8 + NG],
                                              axis=AX, op=ALU.add),
          reads=[pps], writes=[sv1])
    sc.op("vector", lambda e: e.tensor_scalar_mul(sv1[0:1, 0:2], sv1[0:1, 0:2],
                                                  1.0 / C), writes=[sv1])
    sc.op("scalar", lambda e: e.activation(sv1[0:1, 2:3], sv1[0:1, 0:1],
                                           AF.Square), reads=[sv1], writes=[sv1])
    sc.op("vector", lambda e: e.tensor_scalar(
        sv1[0:1, 3:4], sv1[0:1, 1:2], sv1[0:1, 2:3], None, op0=ALU.subtract),
        writes=[sv1])
    sc.op("scalar", lambda e: e.activation(sv1[0:1, 3:4], sv1[0:1, 3:4], AF.Ln,
                                           bias=ct.keps[0:1, 0:1]),
          reads=[ct.keps], writes=[sv1])
    sc.op("scalar", lambda e: e.activation(sv1[0:1, 3:4], sv1[0:1, 3:4], AF.Exp,
                                           scale=-0.5), writes=[sv1])
    sc.op("tensor", lambda e: e.matmul(pps[0:128, 16:17], ct.onesrow[:],
                                       sv1[0:1, 0:1], start=True, stop=True),
          reads=[sv1, ct.onesrow], writes=[pps])
    sc.op("vector", lambda e: e.tensor_copy(mub[:], pps[0:128, 16:17]),
          reads=[pps], writes=[mub])
    sc.op("tensor", lambda e: e.matmul(pps[0:128, 17:18], ct.onesrow[:],
                                       sv1[0:1, 3:4], start=True, stop=True),
          reads=[sv1, ct.onesrow], writes=[pps])
    sc.op("vector", lambda e: e.tensor_copy(rsb[:], pps[0:128, 17:18]),
          reads=[pps], writes=[rsb])
    sc.op("vector", lambda e: e.tensor_scalar(
        out[:, 0:NG], x[:, 0:NG], mub[:], rsb[:], op0=ALU.subtract,
        op1=ALU.mult), reads=[x, mub, rsb], writes=[out])
    if gamma is not None:
        for g in range(NG):
            sc.op("vector", lambda e, g=g: e.tensor_scalar(
                out[:, g:g + 1], out[:, g:g + 1], gamma[:, g:g + 1],
                beta[:, g:g + 1], op0=ALU.mult, op1=ALU.add),
                reads=[gamma, beta], writes=[out])


def emit_phase3(nc, sc, ct, EX):
    xB = EX["x"]
    x = xB.ap
    pmv = EX["pm"].ap
    wk = EX["wk"].ap
    wv = EX["wv"].ap
    wr = EX["wr"].ap
    cns = {nm: EX["c_" + nm].ap for nm in
           ("l0g", "l0b", "l1g", "l1b", "tmk", "tmv", "tmr")}
    kslice = EX["kslice"]
    vslice = EX["vslice"]
    kvbufs = EX["kvbufs"]
    srB = EX["sr"]
    sro = srB.ap
    x2pB = EX["x2p"]
    x2po = x2pB.ap

    ctile = {}
    for nm in cns:
        ctile[nm] = sc.sb(f"c_{nm}", [128, NG], F32)
        sc.dma(ctile[nm][:], cns[nm], writes=[ctile[nm]])
    pmt = sc.sb("pmt", [1, 1], F32)
    sc.dma(pmt[:], pmv, writes=[pmt])
    xprevt = EX["xprevt"]

    wbf = {}
    wtmps3 = [sc.sb(f"wtmp{i}", [128, 1024], F32) for i in range(4)]
    _wr3 = [0]

    def stage3(dst_buf, dst_sl, src_ap):
        i = _wr3[0] % 4
        _wr3[0] += 1
        wt = wtmps3[i]
        deng = ("sync", "gpsimd", "scalar", "sync")[i]
        ceng = ("scalar", "vector", "gpsimd", "scalar")[i]
        sc.dma(wt[:], src_ap, writes=[wt], engine=deng)
        if ceng == "scalar":
            sc.op(ceng, lambda e, wt=wt, dst_sl=dst_sl: e.copy(
                dst_buf[:, dst_sl], wt[:]), reads=[wt], writes=[dst_buf])
        else:
            sc.op(ceng, lambda e, wt=wt, dst_sl=dst_sl: e.tensor_copy(
                dst_buf[:, dst_sl], wt[:]), reads=[wt], writes=[dst_buf])

    for nm, wap in (("k", wk), ("v", wv), ("r", wr)):
        wbf[nm] = sc.sb(f"w{nm}bf", [128, NG * C], BF16)
        for g in range(NG):
            stage3(wbf[nm], slice(g * C, (g + 1) * C),
                   wap[g * 128:(g + 1) * 128, :])

    xa = sc.sb("xa", [128, NG * TT], F32)
    xb = sc.sb("xb", [128, NG * TT], F32)
    x2pt = sc.sb("x2pt", [128, NG * TT], F32)
    htile = sc.sb("htile", [128, NG * TT], F32)
    hlast = sc.sb("hlast", [128, NG], F32)
    hprev = sc.sb("hprev", [128, NG], F32)
    sq1 = sc.sb("sq1", [128, NG], F32)
    sv1 = sc.sb("sv1", [1, 32], F32)
    mub = sc.sb("mub", [128, 1], F32)
    rsb = sc.sb("rsb", [128, 1], F32)
    pmb = sc.sb("pmb", [128, 1], F32)
    xmix = {nm: sc.sb(f"xmix{nm}", [128, NG * TT], BF16)
            for nm in ("k", "v", "r")}
    obuf = sc.sb("obuf", [128, TT], BF16)
    obuf2 = sc.sb("obuf2", [128, TT], BF16)
    srbndt = sc.sb("srbndt", [128, NG], F32)
    x2pbndt = sc.sb("x2pbndt", [128, NG], F32)
    lns = LNScratch(sc, "p3ln", TT)
    pp = [sc.ps(f"pp{i}", [128, TT]) for i in range(6)]

    emit_ln_1tok(sc, ct, xprevt, hprev, pp[4], sq1, sv1, mub, rsb,
                 gamma=ctile["l0g"], beta=ctile["l0b"])
    emit_ln_1tok(sc, ct, hprev, hprev, pp[4], sq1, sv1, mub, rsb,
                 gamma=ctile["l1g"], beta=ctile["l1b"])
    sc.op("tensor", lambda e: e.matmul(pp[4][0:128, 18:19], ct.onesrow[:],
                                       pmt[:], start=True, stop=True),
          reads=[pmt, ct.onesrow], writes=[pp[4]])
    sc.op("vector", lambda e: e.tensor_copy(pmb[:], pp[4][0:128, 18:19]),
          reads=[pp[4]], writes=[pmb])
    sc.op("vector", lambda e: e.tensor_scalar_mul(hprev[:], hprev[:], pmb[:]),
          reads=[pmb], writes=[hprev])

    xt = [xa, xb]
    for tc in range(NTC):
        xx = xt[tc % 2]
        for g in range(NG):
            sc.dma(xx[:, g * TT:(g + 1) * TT],
                   x[g * 128:(g + 1) * 128, tc * TT:(tc + 1) * TT],
                   reads=[xB], writes=[xx])
        emit_ln(sc, ct, xx, x2pt, TT, lns, pp[0], pp[1], pp[2], pp[3],
                gamma=ctile["l0g"], beta=ctile["l0b"])
        for g in range(NG):
            sc.dma(x2po[g * 128:(g + 1) * 128, tc * TT:(tc + 1) * TT],
                   x2pt[:, g * TT:(g + 1) * TT], reads=[x2pt], writes=[x2pB],
                   engine="gpsimd", noww=True)
        emit_ln(sc, ct, x2pt, htile, TT, lns, pp[0], pp[1], pp[2], pp[3],
                gamma=ctile["l1g"], beta=ctile["l1b"])
        for nm, cvec in (("k", "tmk"), ("v", "tmv"), ("r", "tmr")):
            dst = xmix[nm]
            for g in range(NG):
                sl0 = g * TT
                sc.op("vector", lambda e, sl0=sl0: e.tensor_sub(
                    x2pt[:, sl0 + 1:sl0 + TT], htile[:, sl0 + 1:sl0 + TT],
                    htile[:, sl0:sl0 + TT - 1]), reads=[htile], writes=[x2pt])
                sc.op("vector", lambda e, sl0=sl0, g=g, cvec=cvec, dst=dst:
                      e.scalar_tensor_tensor(
                          dst[:, sl0 + 1:sl0 + TT], x2pt[:, sl0 + 1:sl0 + TT],
                          ctile[cvec][:, g:g + 1], htile[:, sl0:sl0 + TT - 1],
                          op0=ALU.mult, op1=ALU.add),
                      reads=[x2pt, ctile[cvec], htile], writes=[dst])
                hl = hprev if tc == 0 else hlast
                sc.op("vector", lambda e, sl0=sl0, g=g, hl=hl: e.tensor_sub(
                    x2pt[:, sl0:sl0 + 1], htile[:, sl0:sl0 + 1], hl[:, g:g + 1]),
                    reads=[htile, hl], writes=[x2pt])
                sc.op("vector", lambda e, sl0=sl0, g=g, cvec=cvec, hl=hl,
                      dst=dst: e.scalar_tensor_tensor(
                          dst[:, sl0:sl0 + 1], x2pt[:, sl0:sl0 + 1],
                          ctile[cvec][:, g:g + 1], hl[:, g:g + 1],
                          op0=ALU.mult, op1=ALU.add),
                      reads=[x2pt, ctile[cvec], hl], writes=[dst])
        hl2 = hlast
        sc.op("vector", lambda e: e.tensor_copy(
            hl2[:, 0:NG], htile.ap[:, TT - 1::TT]), reads=[htile], writes=[hl2])
        for nm in ("k", "v", "r"):
            for oc in range(NG):
                pb = pp[oc % 2]

                def mmkv(e, nm=nm, oc=oc, pb=pb):
                    for g in range(NG):
                        r = e.matmul(pb[0:128, 0:TT],
                                     wbf[nm][:, g * C + oc * 128:g * C + (oc + 1) * 128],
                                     xmix[nm][:, g * TT:(g + 1) * TT],
                                     start=(g == 0), stop=(g == NG - 1))
                    return r
                sc.op("tensor", mmkv, reads=[wbf[nm], xmix[nm]], writes=[pb])
                if nm == "r":
                    sc.op("scalar", lambda e, pb=pb: e.activation(
                        obuf[:], pb[0:128, 0:TT], AF.Sigmoid),
                        reads=[pb], writes=[obuf])
                    if "bnd3" in EX and tc == NTC - 1:
                        sc.op("vector", lambda e, oc=oc: e.tensor_copy(
                            srbndt[:, oc:oc + 1], obuf[:, TT - 1:TT]),
                            reads=[obuf], writes=[srbndt])
                    dst = sro[oc * 128:(oc + 1) * 128, tc * TT:(tc + 1) * TT]
                    dstb = [srB]
                elif nm == "k":
                    sc.op("vector", lambda e, pb=pb: e.tensor_copy(
                        obuf[:], pb[0:128, 0:TT]), reads=[pb], writes=[obuf])
                    dst = kslice(oc, tc)
                    dst2 = EX["kslice2"](oc, tc) if "kv2" in EX else None
                    dstb = kvbufs
                else:
                    sc.op("vector", lambda e, pb=pb: e.tensor_copy(
                        obuf[:], pb[0:128, 0:TT]), reads=[pb], writes=[obuf])
                    dst = vslice(oc, tc)
                    dst2 = EX["vslice2"](oc, tc) if "kv2" in EX else None
                    dstb = kvbufs
                if nm != "r" and "kv2" in EX:
                    s0 = EX["s0"]
                    s1 = EX["s1"]
                    sc.op("vector", lambda e: e.tensor_scalar_mul(
                        obuf2[:], obuf[:], s1[:]), reads=[obuf, s1],
                        writes=[obuf2])
                    sc.op("vector", lambda e: e.tensor_scalar_mul(
                        obuf[:], obuf[:], s0[:]), reads=[s0], writes=[obuf])
                    sc.dma(dst2, obuf2[:], reads=[obuf2], writes=[kvbufs[0]],
                           noww=True)
                sc.dma(dst, obuf[:], reads=[obuf], writes=[dstb[0]], noww=True)
    if "bnd3" in EX:
        sc.op("vector", lambda e: e.tensor_copy(
            x2pbndt[:, 0:NG], x2pt.ap[:, TT - 1::TT]), reads=[x2pt],
            writes=[x2pbndt])
        bnd3 = EX["bnd3"]
        bsc = sc.sb("bnd3sc", [128, NG], F32)
        for half, sv_ in ((0, EX["s0"]), (1, EX["s1"])):
            sc.op("vector", lambda e, sv_=sv_: e.tensor_scalar_mul(
                bsc[:], x2pbndt[:], sv_[:]), reads=[x2pbndt, sv_], writes=[bsc])
            sc.dma(bnd3.ap[half * 2 + 0, :].rearrange("(p g) -> p g", p=128),
                   bsc[:], reads=[bsc], writes=[bnd3], noww=True)
            sc.op("vector", lambda e, sv_=sv_: e.tensor_scalar_mul(
                bsc[:], srbndt[:], sv_[:]), reads=[srbndt, sv_], writes=[bsc])
            sc.dma(bnd3.ap[half * 2 + 1, :].rearrange("(p g) -> p g", p=128),
                   bsc[:], reads=[bsc], writes=[bnd3], noww=True)


def build_phase3():
    nc = bass.Bass("TRN2", target_bir_lowering=False)
    EX = {}
    EX["x"] = Buf(nc.dram_tensor("p3_x", [C, HT], F32, kind="ExternalInput").ap())
    xprev = nc.dram_tensor("p3_xprev", [128, NG], F32, kind="ExternalInput").ap()
    EX["pm"] = Buf(nc.dram_tensor("p3_pm", [1, 1], F32, kind="ExternalInput").ap())
    EX["wk"] = Buf(nc.dram_tensor("p3_wk", [C, C], F32, kind="ExternalInput").ap())
    EX["wv"] = Buf(nc.dram_tensor("p3_wv", [C, C], F32, kind="ExternalInput").ap())
    EX["wr"] = Buf(nc.dram_tensor("p3_wr", [C, C], F32, kind="ExternalInput").ap())
    for nm in ("l0g", "l0b", "l1g", "l1b", "tmk", "tmv", "tmr"):
        EX["c_" + nm] = Buf(nc.dram_tensor(f"p3_{nm}", [128, NG], F32,
                                           kind="ExternalInput").ap())
    ko = nc.dram_tensor("p3_k", [C, HT], BF16, kind="ExternalOutput").ap()
    vo = nc.dram_tensor("p3_v", [C, HT], BF16, kind="ExternalOutput").ap()
    kvB = Buf(ko)
    EX["kslice"] = lambda oc, tc: ko[oc * 128:(oc + 1) * 128,
                                     tc * TT:(tc + 1) * TT]
    EX["vslice"] = lambda oc, tc: vo[oc * 128:(oc + 1) * 128,
                                     tc * TT:(tc + 1) * TT]
    EX["kvbufs"] = [kvB]
    EX["sr"] = Buf(nc.dram_tensor("p3_sr", [C, HT], BF16, kind="ExternalOutput").ap())
    EX["x2p"] = Buf(nc.dram_tensor("p3_x2p", [C, HT], F32, kind="ExternalOutput").ap())
    co = declare_consts(nc)
    sc = Sched(nc)
    ct = CT(sc, co)
    xprevt = sc.sb("xprevt", [128, NG], F32)
    sc.dma(xprevt[:], xprev, writes=[xprevt])
    EX["xprevt"] = xprevt
    emit_phase3(nc, sc, ct, EX)
    sc.emit_and_close()
    return nc


def emit_phase4(nc, sc, EX):
    NCH = 4
    k0 = EX["k0"]
    k1 = EX["k1"]
    v0 = EX["v0"]
    v1 = EX["v1"]
    kvrbufs = EX["kvrbufs"]
    td = EX["td"].ap
    tf = EX["tf"].ap
    att0 = EX["att0"]
    att1 = EX["att1"]
    attbufs = EX["attbufs"]

    tdt = sc.sb("tdt", [128, NCH], F32)
    tft = sc.sb("tft", [128, NCH], F32)
    sc.dma(tdt[:], td, writes=[tdt])
    sc.dma(tft[:], tf, writes=[tft])
    ewt = sc.sb("ewt", [128, NCH], F32)
    eut = sc.sb("eut", [128, NCH], F32)
    nkm = sc.sb("nkm", [128, 1], F32)
    kb = sc.sb("kb", [128, T], BF16)
    vb = sc.sb("vb", [128, T], BF16)
    ek = sc.sb("ek", [128, T], F32)
    t2 = sc.sb("t2", [128, T], F32)
    At = sc.sb("At", [128, T], F32)
    Bt = sc.sb("Bt", [128, T], F32)
    t5 = sc.sb("t5", [128, T], F32)
    attb = sc.sb("attb", [128, T], BF16)

    sc.op("scalar", lambda e: e.activation(ewt[:], tdt[:], AF.Exp),
          reads=[tdt], writes=[ewt])
    sc.op("scalar", lambda e: e.activation(ewt[:], ewt[:], AF.Exp, scale=-1.0),
          writes=[ewt])
    sc.op("scalar", lambda e: e.activation(eut[:], tft[:], AF.Exp),
          reads=[tft], writes=[eut])

    attbndt = sc.sb("attbndt", [128, NCH], BF16)
    attb2 = sc.sb("attb2", [128, T], BF16) if "att2" in EX else None
    blend = "k0b" in EX
    if blend:
        kb2 = sc.sb("kb2", [128, T], BF16)
        vb2 = sc.sb("vb2", [128, T], BF16)
        s0 = EX["s0"]
        s1 = EX["s1"]
    for gg in range(NCH):
        sc.dma(kb[:, 0:HT], k0[gg * 128:(gg + 1) * 128, :],
               reads=kvrbufs, writes=[kb])
        sc.dma(kb[:, HT:T], k1[gg * 128:(gg + 1) * 128, :],
               reads=kvrbufs, writes=[kb], noww=True)
        sc.dma(vb[:, 0:HT], v0[gg * 128:(gg + 1) * 128, :],
               reads=kvrbufs, writes=[vb])
        sc.dma(vb[:, HT:T], v1[gg * 128:(gg + 1) * 128, :],
               reads=kvrbufs, writes=[vb], noww=True)
        if blend:
            sc.dma(kb2[:, 0:HT], EX["k0b"][gg * 128:(gg + 1) * 128, :],
                   reads=kvrbufs, writes=[kb2])
            sc.dma(kb2[:, HT:T], EX["k1b"][gg * 128:(gg + 1) * 128, :],
                   reads=kvrbufs, writes=[kb2], noww=True)
            sc.dma(vb2[:, 0:HT], EX["v0b"][gg * 128:(gg + 1) * 128, :],
                   reads=kvrbufs, writes=[vb2])
            sc.dma(vb2[:, HT:T], EX["v1b"][gg * 128:(gg + 1) * 128, :],
                   reads=kvrbufs, writes=[vb2], noww=True)
            sc.op("vector", lambda e: e.tensor_scalar_mul(kb[:], kb[:], s0[:]),
                  reads=[s0], writes=[kb])
            sc.op("vector", lambda e: e.scalar_tensor_tensor(
                kb[:], kb2[:], s1[:], kb[:], op0=ALU.mult, op1=ALU.add),
                reads=[kb2, s1], writes=[kb])
            sc.op("vector", lambda e: e.tensor_scalar_mul(vb[:], vb[:], s0[:]),
                  reads=[s0], writes=[vb])
            sc.op("vector", lambda e: e.scalar_tensor_tensor(
                vb[:], vb2[:], s1[:], vb[:], op0=ALU.mult, op1=ALU.add),
                reads=[vb2, s1], writes=[vb])
        sc.op("vector", lambda e: e.reduce_max(nkm[:], kb[:], axis=AX,
                                               negate=True),
              reads=[kb], writes=[nkm])
        sc.op("scalar", lambda e: e.activation(ek[:], kb[:], AF.Exp,
                                               bias=nkm[:]),
              reads=[kb, nkm], writes=[ek])
        sc.op("vector", lambda e: e.tensor_mul(t2[:], ek[:], vb[:]),
              reads=[ek, vb], writes=[t2])
        sc.op("vector", lambda e, gg=gg: e.tensor_tensor_scan(
            At[:], ewt.ap[:, gg:gg + 1].to_broadcast((128, T)), t2[:], 0.0,
            op0=ALU.mult, op1=ALU.add), reads=[ewt, t2], writes=[At])
        sc.op("vector", lambda e, gg=gg: e.tensor_tensor_scan(
            Bt[:], ewt.ap[:, gg:gg + 1].to_broadcast((128, T)), ek[:], 0.0,
            op0=ALU.mult, op1=ALU.add), reads=[ewt, ek], writes=[Bt])
        sc.op("vector", lambda e, gg=gg: e.tensor_scalar_mul(
            t5[:], ek[:], eut[:, gg:gg + 1]), reads=[ek, eut], writes=[t5])
        sc.op("vector", lambda e: e.tensor_mul(t2[:], t5[:], vb[:]),
              reads=[t5, vb], writes=[t2])
        sc.op("vector", lambda e: e.tensor_add(t2[:, 1:T], t2[:, 1:T],
                                               At[:, 0:T - 1]),
              reads=[At], writes=[t2])
        sc.op("vector", lambda e: e.tensor_add(t5[:, 1:T], t5[:, 1:T],
                                               Bt[:, 0:T - 1]),
              reads=[Bt], writes=[t5])
        sc.op("scalar", lambda e: e.activation(t5[:], t5[:], AF.Ln),
              writes=[t5])
        sc.op("scalar", lambda e: e.activation(t5[:], t5[:], AF.Exp,
                                               scale=-1.0), writes=[t5])
        sc.op("vector", lambda e: e.tensor_mul(attb[:], t2[:], t5[:]),
              reads=[t2, t5], writes=[attb])
        if "attbnd" in EX:
            sc.op("vector", lambda e, gg=gg: e.tensor_copy(
                attbndt[:, gg:gg + 1], attb[:, HT - 1:HT]),
                reads=[attb], writes=[attbndt])
        if "att2" in EX:
            s0 = EX["s0"]
            s1 = EX["s1"]
            att0b, att1b = EX["att2"]
            sc.op("vector", lambda e: e.tensor_scalar_mul(
                attb2[:], attb[:], s1[:]), reads=[attb, s1], writes=[attb2])
            sc.op("vector", lambda e: e.tensor_scalar_mul(
                attb[:], attb[:], s0[:]), reads=[s0], writes=[attb])
            sc.dma(att0b[gg * 128:(gg + 1) * 128, :], attb2[:, 0:HT],
                   reads=[attb2], writes=[attbufs[0]], noww=True)
            sc.dma(att1b[gg * 128:(gg + 1) * 128, :], attb2[:, HT:T],
                   reads=[attb2], writes=[attbufs[0]], noww=True)
        sc.dma(att0[gg * 128:(gg + 1) * 128, :], attb[:, 0:HT], reads=[attb],
               writes=[attbufs[0]], noww=True)
        sc.dma(att1[gg * 128:(gg + 1) * 128, :], attb[:, HT:T], reads=[attb],
               writes=[attbufs[0]], noww=True)
    if "attbnd" in EX:
        bsc = sc.sb("bndAsc", [128, NCH], BF16)
        for ap_, sv_ in zip(EX["attbnd"], (EX.get("s0"), EX.get("s1"))):
            if sv_ is None:
                sc.dma(ap_, attbndt[:], reads=[attbndt], writes=[attbufs[0]],
                       noww=True)
            else:
                sc.op("vector", lambda e, sv_=sv_: e.tensor_scalar_mul(
                    bsc[:], attbndt[:], sv_[:]), reads=[attbndt, sv_],
                    writes=[bsc])
                sc.dma(ap_, bsc[:], reads=[bsc], writes=[attbufs[0]], noww=True)


def build_phase4():
    NCH = 4
    nc = bass.Bass("TRN2", target_bir_lowering=False)
    kin = nc.dram_tensor("p4_k", [512, T], BF16, kind="ExternalInput").ap()
    vin = nc.dram_tensor("p4_v", [512, T], BF16, kind="ExternalInput").ap()
    EX = {}
    EX["k0"] = kin[:, 0:HT]
    EX["k1"] = kin[:, HT:T]
    EX["v0"] = vin[:, 0:HT]
    EX["v1"] = vin[:, HT:T]
    EX["kvrbufs"] = []
    EX["td"] = Buf(nc.dram_tensor("p4_td", [128, NCH], F32, kind="ExternalInput").ap())
    EX["tf"] = Buf(nc.dram_tensor("p4_tf", [128, NCH], F32, kind="ExternalInput").ap())
    atto = nc.dram_tensor("p4_att", [512, T], BF16, kind="ExternalOutput").ap()
    EX["att0"] = atto[:, 0:HT]
    EX["att1"] = atto[:, HT:T]
    EX["attbufs"] = [Buf(atto)]
    sc = Sched(nc)
    emit_phase4(nc, sc, EX)
    sc.emit_and_close()
    return nc


def emit_phase5(nc, sc, ct, EX):
    H4 = 4096
    NHC = H4 // 128
    x2B = EX["x2"]
    x2 = x2B.ap
    x2pB = EX["x2p"]
    x2p = x2pB.ap
    srB = EX["sr"]
    srin = srB.ap
    attB = EX["att"]
    attin = attB.ap
    pmv = EX["pm"].ap
    wo = EX["wo"].ap
    wfr = EX["wfr"].ap
    wfk = EX["wfk"].ap
    wfv = EX["wfv"].ap
    cns = {nm: EX["c_" + nm].ap for nm in ("l2g", "l2b", "cmk", "cmr")}
    ooB = EX["o"]
    oo = ooB.ap
    x3_d = Buf(nc.dram_tensor("p5_x3d", [C, HT], F32).ap())
    kf_d = Buf(nc.dram_tensor("p5_kfd", [H4, HT], BF16).ap())
    sr2_d = Buf(nc.dram_tensor("p5_sr2d", [C, HT], BF16).ap())

    ctile = {}
    for nm in cns:
        ctile[nm] = sc.sb(f"c5_{nm}", [128, NG], F32)
        sc.dma(ctile[nm][:], cns[nm], writes=[ctile[nm]])
    pmt = sc.sb("pmt5", [1, 1], F32)
    sc.dma(pmt[:], pmv, writes=[pmt])
    apv = EX["apv"]
    spv = EX["spv"]
    xpv = EX["xpv"]

    wtmps5 = [sc.sb(f"wtmp{i}", [128, 1024], F32) for i in range(2)]
    wtmps5b = [Buf(wtmps5[0].ap[:, 0:512]), Buf(wtmps5[0].ap[:, 512:1024]),
               Buf(wtmps5[1].ap[:, 0:512]), Buf(wtmps5[1].ap[:, 512:1024])]

    _wr5 = [0]

    def stage5(dst_buf, dst_sl, src_ap):
        i = _wr5[0] % 2
        _wr5[0] += 1
        wt = wtmps5[i]
        deng = ("sync", "gpsimd")[i]
        ceng = ("scalar", "vector")[i]
        sc.dma(wt[:], src_ap, writes=[wt], engine=deng)
        if ceng == "scalar":
            sc.op(ceng, lambda e, wt=wt, dst_sl=dst_sl: e.copy(
                dst_buf[:, dst_sl], wt[:]), reads=[wt], writes=[dst_buf])
        else:
            sc.op(ceng, lambda e, wt=wt, dst_sl=dst_sl: e.tensor_copy(
                dst_buf[:, dst_sl], wt[:]), reads=[wt], writes=[dst_buf])

    wobf = sc.sb("wobf", [128, NG * C], BF16)
    wfrbf = sc.sb("wfrbf", [128, NG * C], BF16)
    wbig = sc.sb("wbig", [128, NG * H4], BF16)
    for g in range(NG):
        stage5(wobf, slice(g * C, (g + 1) * C), wo[g * 128:(g + 1) * 128, :])
        stage5(wfrbf, slice(g * C, (g + 1) * C), wfr[g * 128:(g + 1) * 128, :])
    for g in range(NG):
        for q in range(4):
            stage5(wbig, slice(g * H4 + q * C, g * H4 + (q + 1) * C),
                   wfk[g * 128:(g + 1) * 128, q * C:(q + 1) * C])

    xa = sc.sb("xa", [128, NG * TT], F32)
    xb = xa
    srt = sc.sb("srt", [128, NG * TT], BF16)
    attt = sc.sb("attt", [128, NG * TT], BF16)
    att2t = sc.sb("att2t", [128, TT], BF16) if "attin2" in EX else None
    x3t = sc.sb("x3t", [128, NG * TT], F32)
    h2t = sc.sb("h2t", [128, NG * TT], F32)
    xk2 = sc.sb("xk2", [128, NG * TT], BF16)
    xr2 = sc.sb("xr2", [128, NG * TT], BF16)
    h2last = sc.sb("h2last", [128, NG], F32)
    h2prev = sc.sb("h2prev", [128, NG], F32)
    sq1 = sc.sb("sq1", [128, NG], F32)
    sv1 = sc.sb("sv1", [1, 32], F32)
    mub = sc.sb("mub", [128, 1], F32)
    rsb = sc.sb("rsb", [128, 1], F32)
    pmb = sc.sb("pmb", [128, 1], F32)
    obuf = sc.sb("obuf", [128, TT], BF16)
    kft = sc.sb("kft", [128, TT], BF16)
    lns = LNScratch(sc, "p5ln", TT)
    pp = [sc.ps(f"pp{i}", [128, TT]) for i in range(6)]

    sc.op("tensor", lambda e: e.matmul(pp[4][0:128, 18:19], ct.onesrow[:],
                                       pmt[:], start=True, stop=True),
          reads=[pmt, ct.onesrow], writes=[pp[4]])
    sc.op("vector", lambda e: e.tensor_copy(pmb[:], pp[4][0:128, 18:19]),
          reads=[pp[4]], writes=[pmb])

    sap = sc.sb("sap", [128, NG], F32)
    sc.op("vector", lambda e: e.tensor_mul(sap[:], spv[:], apv[:]),
          reads=[spv, apv], writes=[sap])
    sapbf = sc.sb("sapbf", [128, NG], BF16)
    sc.op("vector", lambda e: e.tensor_copy(sapbf[:], sap[:]),
          reads=[sap], writes=[sapbf])

    def mmprev(e):
        for oc in range(NG):
            for g in range(NG):
                r = e.matmul(pp[5][0:128, oc:oc + 1],
                             wobf[:, g * C + oc * 128:g * C + (oc + 1) * 128],
                             sapbf[:, g:g + 1],
                             start=(g == 0), stop=(g == NG - 1))
        return r
    sc.op("tensor", mmprev, reads=[wobf, sapbf], writes=[pp[5]])
    sc.op("vector", lambda e: e.tensor_add(h2prev[:], pp[5][0:128, 0:NG],
                                           xpv[:]),
          reads=[pp[5], xpv], writes=[h2prev])
    emit_ln_1tok(sc, ct, h2prev, h2prev, pp[4], sq1, sv1, mub, rsb,
                 gamma=ctile["l2g"], beta=ctile["l2b"])
    sc.op("vector", lambda e: e.tensor_scalar_mul(h2prev[:], h2prev[:], pmb[:]),
          reads=[pmb], writes=[h2prev])

    xt = [xa, xb]
    for tc in range(NTC):
        xx = xt[tc % 2]
        for g in range(NG):
            sc.dma(xx[:, g * TT:(g + 1) * TT],
                   x2p[g * 128:(g + 1) * 128, tc * TT:(tc + 1) * TT],
                   reads=[x2pB], writes=[xx])
            sc.dma(srt[:, g * TT:(g + 1) * TT],
                   srin[g * 128:(g + 1) * 128, tc * TT:(tc + 1) * TT],
                   reads=[srB], writes=[srt])
            sc.dma(attt[:, g * TT:(g + 1) * TT],
                   attin[g * 128:(g + 1) * 128, tc * TT:(tc + 1) * TT],
                   reads=[attB], writes=[attt])
            if "attin2" in EX:
                sc.dma(att2t[:],
                       EX["attin2"][g * 128:(g + 1) * 128, tc * TT:(tc + 1) * TT],
                       reads=[attB], writes=[att2t])
                sc.op("vector", lambda e, g=g: e.tensor_scalar_mul(
                    attt[:, g * TT:(g + 1) * TT], attt[:, g * TT:(g + 1) * TT],
                    EX["s0"][:]), reads=[EX["s0"]], writes=[attt])
                sc.op("vector", lambda e, g=g: e.scalar_tensor_tensor(
                    attt[:, g * TT:(g + 1) * TT], att2t[:], EX["s1"][:],
                    attt[:, g * TT:(g + 1) * TT], op0=ALU.mult, op1=ALU.add),
                    reads=[att2t, EX["s1"]], writes=[attt])
        sratt = attt
        for g in range(NG):
            sc.op("vector", lambda e, g=g: e.tensor_mul(
                sratt[:, g * TT:(g + 1) * TT], srt[:, g * TT:(g + 1) * TT],
                attt[:, g * TT:(g + 1) * TT]), reads=[srt], writes=[attt])
        for oc in range(NG):
            pb = pp[oc % 2]

            def mmo(e, oc=oc, pb=pb):
                for g in range(NG):
                    r = e.matmul(pb[0:128, 0:TT],
                                 wobf[:, g * C + oc * 128:g * C + (oc + 1) * 128],
                                 sratt[:, g * TT:(g + 1) * TT],
                                 start=(g == 0), stop=(g == NG - 1))
                return r
            sc.op("tensor", mmo, reads=[wobf, sratt], writes=[pb])
            sc.op("vector", lambda e, oc=oc, pb=pb, xx=xx: e.tensor_add(
                x3t[:, oc * TT:(oc + 1) * TT], pb[0:128, 0:TT],
                xx[:, oc * TT:(oc + 1) * TT]), reads=[pb, xx], writes=[x3t])
        for g in range(NG):
            sc.dma(x3_d.ap[g * 128:(g + 1) * 128, tc * TT:(tc + 1) * TT],
                   x3t[:, g * TT:(g + 1) * TT], reads=[x3t], writes=[x3_d])
        emit_ln(sc, ct, x3t, h2t, TT, lns, pp[0], pp[1], pp[2], pp[3],
                gamma=ctile["l2g"], beta=ctile["l2b"])
        for dst, cvec in ((xk2, "cmk"), (xr2, "cmr")):
            for g in range(NG):
                sl0 = g * TT
                sc.op("vector", lambda e, sl0=sl0: e.tensor_sub(
                    x3t[:, sl0 + 1:sl0 + TT], h2t[:, sl0 + 1:sl0 + TT],
                    h2t[:, sl0:sl0 + TT - 1]), reads=[h2t], writes=[x3t])
                sc.op("vector", lambda e, sl0=sl0, g=g, dst=dst, cvec=cvec:
                      e.scalar_tensor_tensor(
                          dst[:, sl0 + 1:sl0 + TT], x3t[:, sl0 + 1:sl0 + TT],
                          ctile[cvec][:, g:g + 1], h2t[:, sl0:sl0 + TT - 1],
                          op0=ALU.mult, op1=ALU.add),
                      reads=[x3t, ctile[cvec], h2t], writes=[dst])
                hl = h2prev if tc == 0 else h2last
                sc.op("vector", lambda e, sl0=sl0, g=g, hl=hl: e.tensor_sub(
                    x3t[:, sl0:sl0 + 1], h2t[:, sl0:sl0 + 1], hl[:, g:g + 1]),
                    reads=[h2t, hl], writes=[x3t])
                sc.op("vector", lambda e, sl0=sl0, g=g, dst=dst, cvec=cvec,
                      hl=hl: e.scalar_tensor_tensor(
                          dst[:, sl0:sl0 + 1], x3t[:, sl0:sl0 + 1],
                          ctile[cvec][:, g:g + 1], hl[:, g:g + 1],
                          op0=ALU.mult, op1=ALU.add),
                      reads=[x3t, ctile[cvec], hl], writes=[dst])
        sc.op("vector", lambda e: e.tensor_copy(
            h2last[:, 0:NG], h2t.ap[:, TT - 1::TT]), reads=[h2t],
            writes=[h2last])
        for oc in range(NG):
            pb = pp[oc % 2]

            def mmr(e, oc=oc, pb=pb):
                for g in range(NG):
                    r = e.matmul(pb[0:128, 0:TT],
                                 wfrbf[:, g * C + oc * 128:g * C + (oc + 1) * 128],
                                 xr2[:, g * TT:(g + 1) * TT],
                                 start=(g == 0), stop=(g == NG - 1))
                return r
            sc.op("tensor", mmr, reads=[wfrbf, xr2], writes=[pb])
            sc.op("scalar", lambda e, pb=pb: e.activation(
                obuf[:], pb[0:128, 0:TT], AF.Sigmoid), reads=[pb],
                writes=[obuf])
            sc.dma(sr2_d.ap[oc * 128:(oc + 1) * 128, tc * TT:(tc + 1) * TT],
                   obuf[:], reads=[obuf], writes=[sr2_d])
        for oc in range(NHC):
            pb = pp[2 + (oc % 2)]

            def mmk(e, oc=oc, pb=pb):
                for g in range(NG):
                    r = e.matmul(pb[0:128, 0:TT],
                                 wbig[:, g * H4 + oc * 128:g * H4 + (oc + 1) * 128],
                                 xk2[:, g * TT:(g + 1) * TT],
                                 start=(g == 0), stop=(g == NG - 1))
                return r
            sc.op("tensor", mmk, reads=[wbig, xk2], writes=[pb])
            sc.op("vector", lambda e, pb=pb: e.tensor_scalar_max(
                pb[0:128, 0:TT], pb[0:128, 0:TT], 0.0), writes=[pb])
            sc.op("scalar", lambda e, pb=pb: e.activation(
                kft[:], pb[0:128, 0:TT], AF.Square), reads=[pb], writes=[kft])
            sc.dma(kf_d.ap[oc * 128:(oc + 1) * 128, tc * TT:(tc + 1) * TT],
                   kft[:], reads=[kft], writes=[kf_d])

    for hc in range(NHC):
        stage5(wbig, slice(hc * C, (hc + 1) * C),
               wfv[hc * 128:(hc + 1) * 128, :])
    _kfc0 = Buf(xk2.ap[:, 0:TT])
    _kfc0.writer = xk2.writer
    _kfc0.readers = list(xk2.readers)
    kfc = [_kfc0, Buf(xk2.ap[:, TT:2 * TT])]
    kfc[1].writer = xk2.writer
    kfc[1].readers = list(xk2.readers)
    for tc in range(NTC):
        xx = xt[tc % 2]
        for g in range(NG):
            sc.dma(srt[:, g * TT:(g + 1) * TT],
                   sr2_d.ap[g * 128:(g + 1) * 128, tc * TT:(tc + 1) * TT],
                   reads=[sr2_d], writes=[srt])
            sc.dma(xx[:, g * TT:(g + 1) * TT],
                   x2[g * 128:(g + 1) * 128, tc * TT:(tc + 1) * TT],
                   reads=[x2B], writes=[xx])
            sc.dma(x3t[:, g * TT:(g + 1) * TT],
                   x3_d.ap[g * 128:(g + 1) * 128, tc * TT:(tc + 1) * TT],
                   reads=[x3_d], writes=[x3t])
        for ohalf in range(2):
            for hc in range(NHC):
                kf_b = kfc[hc % 2]
                sc.dma(kf_b[:],
                       kf_d.ap[hc * 128:(hc + 1) * 128, tc * TT:(tc + 1) * TT],
                       reads=[kf_d], writes=[kf_b])

                def mmv(e, ohalf=ohalf, hc=hc, kf_b=kf_b):
                    for oi in range(4):
                        oc = ohalf * 4 + oi
                        r = e.matmul(pp[oi][0:128, 0:TT],
                                     wbig[:, hc * C + oc * 128:hc * C + (oc + 1) * 128],
                                     kf_b[:],
                                     start=(hc == 0), stop=(hc == NHC - 1))
                    return r
                sc.op("tensor", mmv, reads=[wbig, kf_b],
                      writes=[pp[0], pp[1], pp[2], pp[3]])
            for oi in range(4):
                oc = ohalf * 4 + oi
                sc.op("vector", lambda e, oc=oc, oi=oi: e.tensor_mul(
                    h2t[:, oc * TT:(oc + 1) * TT], pp[oi][0:128, 0:TT],
                    srt[:, oc * TT:(oc + 1) * TT]), reads=[pp[oi], srt],
                    writes=[h2t])
                sc.op("vector", lambda e, oc=oc, xx=xx: e.tensor_add(
                    h2t[:, oc * TT:(oc + 1) * TT], h2t[:, oc * TT:(oc + 1) * TT],
                    xx[:, oc * TT:(oc + 1) * TT]), reads=[xx], writes=[h2t])
                sc.op("vector", lambda e, oc=oc: e.tensor_add(
                    h2t[:, oc * TT:(oc + 1) * TT], h2t[:, oc * TT:(oc + 1) * TT],
                    x3t[:, oc * TT:(oc + 1) * TT]), reads=[x3t], writes=[h2t])
        emit_ln(sc, ct, h2t, h2t, TT, lns, pp[0], pp[1], pp[2], pp[3])
        for g in range(NG):
            sc.dma(oo[g * 128:(g + 1) * 128, tc * TT:(tc + 1) * TT],
                   h2t[:, g * TT:(g + 1) * TT], reads=[h2t], writes=[ooB],
                   engine="gpsimd", noww=True)


def build_phase5():
    nc = bass.Bass("TRN2", target_bir_lowering=False)
    H4 = 4096
    EX = {}
    EX["x2"] = Buf(nc.dram_tensor("p5_x2", [C, HT], F32, kind="ExternalInput").ap())
    EX["x2p"] = Buf(nc.dram_tensor("p5_x2p", [C, HT], F32, kind="ExternalInput").ap())
    EX["sr"] = Buf(nc.dram_tensor("p5_sr", [C, HT], BF16, kind="ExternalInput").ap())
    EX["att"] = Buf(nc.dram_tensor("p5_att", [C, HT], BF16, kind="ExternalInput").ap())
    attprev = nc.dram_tensor("p5_attprev", [128, NG], BF16, kind="ExternalInput").ap()
    srprev = nc.dram_tensor("p5_srprev", [128, NG], BF16, kind="ExternalInput").ap()
    x2pprev = nc.dram_tensor("p5_x2pprev", [128, NG], F32, kind="ExternalInput").ap()
    EX["pm"] = Buf(nc.dram_tensor("p5_pm", [1, 1], F32, kind="ExternalInput").ap())
    EX["wo"] = Buf(nc.dram_tensor("p5_wo", [C, C], F32, kind="ExternalInput").ap())
    EX["wfr"] = Buf(nc.dram_tensor("p5_wfr", [C, C], F32, kind="ExternalInput").ap())
    EX["wfk"] = Buf(nc.dram_tensor("p5_wfk", [C, H4], F32, kind="ExternalInput").ap())
    EX["wfv"] = Buf(nc.dram_tensor("p5_wfv", [H4, C], F32, kind="ExternalInput").ap())
    for nm in ("l2g", "l2b", "cmk", "cmr"):
        EX["c_" + nm] = Buf(nc.dram_tensor(f"p5_{nm}", [128, NG], F32,
                                           kind="ExternalInput").ap())
    EX["o"] = Buf(nc.dram_tensor("p5_o", [C, HT], F32, kind="ExternalOutput").ap())
    co = declare_consts(nc)
    sc = Sched(nc)
    ct = CT(sc, co)
    apvb = sc.sb("apvb", [128, NG], BF16)
    spvb = sc.sb("spvb", [128, NG], BF16)
    apv = sc.sb("apv", [128, NG], F32)
    spv = sc.sb("spv", [128, NG], F32)
    xpv = sc.sb("xpv", [128, NG], F32)
    sc.dma(apvb[:], attprev, writes=[apvb])
    sc.dma(spvb[:], srprev, writes=[spvb])
    sc.dma(xpv[:], x2pprev, writes=[xpv])
    sc.op("vector", lambda e: e.tensor_copy(apv[:], apvb[:]),
          reads=[apvb], writes=[apv])
    sc.op("vector", lambda e: e.tensor_copy(spv[:], spvb[:]),
          reads=[spvb], writes=[spv])
    EX["apv"] = apv
    EX["spv"] = spv
    EX["xpv"] = xpv
    emit_phase5(nc, sc, ct, EX)
    sc.emit_and_close()
    return nc


PAIRS = [[0, 1], [2, 3], [4, 5], [6, 7]]


FUSED_STAGE = 9


def build_fused():
    nc = bass.Bass("TRN2", target_bir_lowering=False)
    co = declare_consts(nc)
    iotam64 = Buf(nc.dram_tensor("c_iotam64", [128, 16], F32,
                                 kind="ExternalInput").ap())
    ch0 = nc.dram_tensor("c_h0", [128, 1], F32, kind="ExternalInput").ap()
    ch1 = nc.dram_tensor("c_h1", [128, 1], F32, kind="ExternalInput").ap()

    x1_d = Buf(nc.dram_tensor("f_x1", [C, HT], F32).ap())
    x2_d = Buf(nc.dram_tensor("f_x2", [C, HT], F32).ap())
    x2p_d = Buf(nc.dram_tensor("f_x2p", [C, HT], F32).ap())
    sr_d = Buf(nc.dram_tensor("f_sr", [C, HT], BF16).ap())
    x4_d = Buf(nc.dram_tensor("f_x4", [C, HT], F32).ap())
    bnd2s = Buf(nc.dram_tensor("f_bnd2s", [2, C], F32).ap())
    bnd2r = Buf(nc.dram_tensor("f_bnd2r", [2, C], F32).ap())
    kvs = Buf(nc.dram_tensor("f_kvs", [2 * 2048, HT], BF16).ap())
    kvr = Buf(nc.dram_tensor("f_kvr", [2 * 2048, HT], BF16).ap())
    bnd3s = Buf(nc.dram_tensor("f_bnd3s", [4, C], F32).ap())
    bnd3r = Buf(nc.dram_tensor("f_bnd3r", [4, C], F32).ap())
    atts = Buf(nc.dram_tensor("f_atts", [C, T], BF16).ap())
    attr_ = Buf(nc.dram_tensor("f_attr", [C, T], BF16).ap())
    bndAs = Buf(nc.dram_tensor("f_bndAs", [2, 512], BF16).ap())
    bndAr = Buf(nc.dram_tensor("f_bndAr", [2, 512], BF16).ap())

    p6o = Buf(nc.dram_tensor("p6_o", [C, HT], F32, kind="ExternalOutput").ap())
    sc = Sched(nc)
    ct = CT(sc, co)
    s0t = sc.sb("s0t", [128, 1], F32)
    s1t = sc.sb("s1t", [128, 1], F32)
    sc.dma(s0t[:], ch0, writes=[s0t])
    sc.dma(s1t[:], ch1, writes=[s1t])

    def allred(snd, rcv, dt):
        sc.op("gpsimd", lambda e: e.collective_compute(
            "AllReduce", ALU.add, replica_groups=PAIRS,
            ins=[snd.ap.opt()], outs=[rcv.ap.opt()]),
            reads=[snd], writes=[rcv], dma=True, dmabuf=rcv, inc_amt=1)

    sc.prefix = "f1_"
    sc.push_scope()
    E1 = {}
    E1["x"] = Buf(nc.dram_tensor("p1_x", [C, T], F32, kind="ExternalInput").ap())
    E1["xh"] = Buf(nc.dram_tensor("p1_xh", [C, HT], F32, kind="ExternalInput").ap())
    E1["xrev"] = Buf(nc.dram_tensor("p1_xrev", [C, HT], F32, kind="ExternalInput").ap())
    E1["rw"] = Buf(nc.dram_tensor("p1_rw", [128, NG], F32, kind="ExternalInput").ap())
    E1["rb"] = Buf(nc.dram_tensor("p1_rb", [128, 1], F32, kind="ExternalInput").ap())
    E1["hsel"] = Buf(nc.dram_tensor("p1_hsel", [1, 2], F32, kind="ExternalInput").ap())
    E1["x1"] = x1_d
    emit_phase1(nc, sc, ct, E1)
    sc.pop_scope()
    if FUSED_STAGE == 15:
        allred(bnd2s, bnd2r, F32)
        import os
        lim = os.environ.get("SCHED_LIMIT")
        if lim:
            n = int(lim)
            last = sc.ops[-1]
            sc.ops = sc.ops[:n]
            sc.ops.append(last)
            tot = {}
            for (e, fn, waits, inc) in sc.ops:
                s, amt = inc
                tot[s["name"]] = tot.get(s["name"], 0) + amt
            for s in sc.sems.values():
                s["tot"] = tot.get(s["name"], 0)
            e, fn, waits, inc = sc.ops[-1]
            waits = [(s, min(v, tot.get(s["name"], 0))) for (s, v) in waits]
            sc.ops[-1] = (e, fn, waits, inc)
        sc.emit_and_close()
        return nc
    if FUSED_STAGE < 2:
        sc.emit_and_close()
        return nc

    sc.prefix = "f2_"
    sc.push_scope()
    EX2 = _moe_external(nc, "p2", 16, 1024, False, with_x=False)
    EX2["x"] = x1_d
    EX2["o"] = x2_d
    EX2["iotam64"] = iotam64
    EX2["bndx"] = bnd2s
    EX2["s0"] = s0t
    EX2["s1"] = s1t
    emit_moe(nc, sc, ct, EX2, "p2", 16, 1024, 384, False, False)
    sc.pop_scope()
    if FUSED_STAGE == 25:
        sc.emit_and_close()
        return nc
    allred(bnd2s, bnd2r, F32)
    if FUSED_STAGE < 3:
        sc.emit_and_close()
        return nc

    sc.prefix = "f3_"
    sc.push_scope()
    EX3 = {}
    EX3["x"] = x2_d
    EX3["pm"] = Buf(nc.dram_tensor("p3_pm", [1, 1], F32, kind="ExternalInput").ap())
    EX3["wk"] = Buf(nc.dram_tensor("p3_wk", [C, C], F32, kind="ExternalInput").ap())
    EX3["wv"] = Buf(nc.dram_tensor("p3_wv", [C, C], F32, kind="ExternalInput").ap())
    EX3["wr"] = Buf(nc.dram_tensor("p3_wr", [C, C], F32, kind="ExternalInput").ap())
    for nm in ("l0g", "l0b", "l1g", "l1b", "tmk", "tmv", "tmr"):
        EX3["c_" + nm] = Buf(nc.dram_tensor(f"p3_{nm}", [128, NG], F32,
                                            kind="ExternalInput").ap())
    xprevt = sc.sb("xprevt", [128, NG], F32)
    sc.dma(xprevt[:], bnd2r.ap[0, :].rearrange("(p g) -> p g", p=128),
           reads=[bnd2r], writes=[xprevt])
    EX3["xprevt"] = xprevt
    kvs_ap = kvs.ap

    def ksl(reg):
        def f(oc, tc):
            if oc < 4:
                r = reg * 2048 + oc * 128
            else:
                r = reg * 2048 + 1024 + (oc - 4) * 128
            return kvs_ap[r:r + 128, tc * TT:(tc + 1) * TT]
        return f

    def vsl(reg):
        def f(oc, tc):
            if oc < 4:
                r = reg * 2048 + 512 + oc * 128
            else:
                r = reg * 2048 + 1536 + (oc - 4) * 128
            return kvs_ap[r:r + 128, tc * TT:(tc + 1) * TT]
        return f
    EX3["kslice"] = ksl(0)
    EX3["vslice"] = vsl(0)
    EX3["kslice2"] = ksl(1)
    EX3["vslice2"] = vsl(1)
    EX3["kv2"] = True
    EX3["s0"] = s0t
    EX3["s1"] = s1t
    EX3["kvbufs"] = [kvs]
    EX3["sr"] = sr_d
    EX3["x2p"] = x2p_d
    EX3["bnd3"] = bnd3s
    emit_phase3(nc, sc, ct, EX3)
    sc.pop_scope()
    allred(kvs, kvr, BF16)
    allred(bnd3s, bnd3r, F32)
    if FUSED_STAGE < 4:
        sc.emit_and_close()
        return nc

    sc.prefix = "f4_"
    sc.push_scope()
    EX4 = {}
    kvr_ap = kvr.ap
    EX4["k0"] = kvr_ap[0:512, :]
    EX4["v0"] = kvr_ap[512:1024, :]
    EX4["k1"] = kvr_ap[2048:2560, :]
    EX4["v1"] = kvr_ap[2560:3072, :]
    EX4["k0b"] = kvr_ap[1024:1536, :]
    EX4["v0b"] = kvr_ap[1536:2048, :]
    EX4["k1b"] = kvr_ap[3072:3584, :]
    EX4["v1b"] = kvr_ap[3584:4096, :]
    EX4["kvrbufs"] = [kvr]
    EX4["s0"] = s0t
    EX4["s1"] = s1t
    EX4["td"] = Buf(nc.dram_tensor("p4_td", [128, 4], F32, kind="ExternalInput").ap())
    EX4["tf"] = Buf(nc.dram_tensor("p4_tf", [128, 4], F32, kind="ExternalInput").ap())
    EX4["att0"] = atts.ap[0:512, 0:HT]
    EX4["att1"] = atts.ap[0:512, HT:T]
    EX4["att2"] = (atts.ap[512:1024, 0:HT], atts.ap[512:1024, HT:T])
    EX4["attbufs"] = [atts]
    EX4["attbnd"] = [bndAs.ap[0, :].rearrange("(p g) -> p g", p=128),
                     bndAs.ap[1, :].rearrange("(p g) -> p g", p=128)]
    emit_phase4(nc, sc, EX4)
    sc.pop_scope()
    allred(atts, attr_, BF16)
    allred(bndAs, bndAr, BF16)
    if FUSED_STAGE < 5:
        sc.emit_and_close()
        return nc

    sc.prefix = "f5_"
    sc.push_scope()
    EX5 = {}
    EX5["x2"] = x2_d
    EX5["x2p"] = x2p_d
    EX5["sr"] = sr_d
    EX5["att"] = attr_
    EX5["attin2"] = attr_.ap[:, HT:T]
    EX5["s0"] = s0t
    EX5["s1"] = s1t
    EX5["pm"] = Buf(nc.dram_tensor("p5_pm", [1, 1], F32, kind="ExternalInput").ap())
    EX5["wo"] = Buf(nc.dram_tensor("p5_wo", [C, C], F32, kind="ExternalInput").ap())
    EX5["wfr"] = Buf(nc.dram_tensor("p5_wfr", [C, C], F32, kind="ExternalInput").ap())
    EX5["wfk"] = Buf(nc.dram_tensor("p5_wfk", [C, 4096], F32, kind="ExternalInput").ap())
    EX5["wfv"] = Buf(nc.dram_tensor("p5_wfv", [4096, C], F32, kind="ExternalInput").ap())
    for nm in ("l2g", "l2b", "cmk", "cmr"):
        EX5["c_" + nm] = Buf(nc.dram_tensor(f"p5_{nm}", [128, NG], F32,
                                            kind="ExternalInput").ap())
    EX5["o"] = x4_d
    apvb = sc.sb("apvb", [128, NG], BF16)
    apv = sc.sb("apv", [128, NG], F32)
    spv = sc.sb("spv", [128, NG], F32)
    xpv = sc.sb("xpv", [128, NG], F32)
    sc.dma(apvb[:, 0:4], bndAr.ap[0, :].rearrange("(p g) -> p g", p=128),
           reads=[bndAr], writes=[apvb])
    sc.dma(apvb[:, 4:8], bndAr.ap[1, :].rearrange("(p g) -> p g", p=128),
           reads=[bndAr], writes=[apvb], noww=True)
    sc.op("vector", lambda e: e.tensor_copy(apv[:], apvb[:]),
          reads=[apvb], writes=[apv])
    sc.dma(xpv[:], bnd3r.ap[0, :].rearrange("(p g) -> p g", p=128),
           reads=[bnd3r], writes=[xpv])
    sc.dma(spv[:], bnd3r.ap[1, :].rearrange("(p g) -> p g", p=128),
           reads=[bnd3r], writes=[spv])
    EX5["apv"] = apv
    EX5["spv"] = spv
    EX5["xpv"] = xpv
    emit_phase5(nc, sc, ct, EX5)
    sc.pop_scope()
    if FUSED_STAGE < 6:
        sc.emit_and_close()
        return nc

    sc.prefix = "f6_"
    sc.push_scope()
    EX6 = _moe_external(nc, "p6", 8, 1408, True, with_x=False)
    EX6["x"] = x4_d
    EX6["iotam64"] = iotam64
    EX6["o"] = p6o
    emit_moe(nc, sc, ct, EX6, "p6", 8, 1408, 768, True, True)
    sc.pop_scope()

    sc.emit_and_close()
    return nc


def run_fused(inputs):
    co = _get_consts()
    global _FUSED_NC
    if _FUSED_NC is None:
        _FUSED_NC = build_fused()
    nc = _FUSED_NC
    ins = {k: np.asarray(v) for k, v in inputs.items()}
    f32 = np.float32
    xf = host_fm(ins["L0_attn_x"])
    xrevf = xf[:, :, ::-1]
    rw_vec = np.asarray(ins["L0_attn_router_w"])[0]
    rb = np.asarray(ins["L0_attn_router_b"])[0]
    wkT = np.ascontiguousarray(np.asarray(ins["L1_attn_Wk"], f32).T)
    wvT = np.ascontiguousarray(np.asarray(ins["L1_attn_Wv"], f32).T)
    wrT = np.ascontiguousarray(np.asarray(ins["L1_attn_Wr"], f32).T)
    woT = np.ascontiguousarray(np.asarray(ins["L1_attn_Wo"], f32).T)
    wfrT = np.ascontiguousarray(np.asarray(ins["L1_attn_Wfr"], f32).T)
    wfkT = np.ascontiguousarray(np.asarray(ins["L1_attn_Wfk"], f32).T)
    wfvT = np.ascontiguousarray(np.asarray(ins["L1_attn_Wfv"], f32).T)
    td = np.asarray(ins["L1_attn_time_decay"], f32)
    tf = np.asarray(ins["L1_attn_time_first"], f32)
    in_maps = []
    for core in range(NCORES):
        b, h = core // 2, core % 2
        m = dict(co)
        m["p1_x"] = np.ascontiguousarray(xf[b])
        m["p1_xh"] = np.ascontiguousarray(xf[b, :, h * HT:(h + 1) * HT])
        m["p1_xrev"] = np.ascontiguousarray(xrevf[b, :, h * HT:(h + 1) * HT])
        m["p1_rw"] = host_vec8(rw_vec)
        m["p1_rb"] = np.full((128, 1), float(rb), f32)
        m["p1_hsel"] = np.array([[1.0 - h, float(h)]], f32)
        m["c_h0"] = np.full((128, 1), 1.0 - h, f32)
        m["c_h1"] = np.full((128, 1), float(h), f32)
        m["p2_rw"] = np.ascontiguousarray(np.asarray(ins["L0_ffn_router_w"], f32))
        m["p2_wg"] = np.ascontiguousarray(np.asarray(ins["L0_ffn_gate_w"], f32))
        m["p2_wu"] = np.ascontiguousarray(np.asarray(ins["L0_ffn_up_w"], f32))
        m["p2_wd"] = np.ascontiguousarray(np.asarray(ins["L0_ffn_down_w"], f32))
        m["p3_pm"] = np.array([[float(h)]], f32)
        m["p3_wk"] = wkT
        m["p3_wv"] = wvT
        m["p3_wr"] = wrT
        m["p3_l0g"] = host_vec8(ins["L1_attn_ln0_w"])
        m["p3_l0b"] = host_vec8(ins["L1_attn_ln0_b"])
        m["p3_l1g"] = host_vec8(ins["L1_attn_ln1_w"])
        m["p3_l1b"] = host_vec8(ins["L1_attn_ln1_b"])
        m["p3_tmk"] = host_vec8(ins["L1_attn_tmk"])
        m["p3_tmv"] = host_vec8(ins["L1_attn_tmv"])
        m["p3_tmr"] = host_vec8(ins["L1_attn_tmr"])
        sl = slice(h * 512, (h + 1) * 512)
        m["p4_td"] = np.ascontiguousarray(td[sl].reshape(4, 128).T)
        m["p4_tf"] = np.ascontiguousarray(tf[sl].reshape(4, 128).T)
        m["p5_pm"] = np.array([[float(h)]], f32)
        m["p5_wo"] = woT
        m["p5_wfr"] = wfrT
        m["p5_wfk"] = wfkT
        m["p5_wfv"] = wfvT
        m["p5_l2g"] = host_vec8(ins["L1_attn_ln2_w"])
        m["p5_l2b"] = host_vec8(ins["L1_attn_ln2_b"])
        m["p5_cmk"] = host_vec8(ins["L1_attn_cmk"])
        m["p5_cmr"] = host_vec8(ins["L1_attn_cmr"])
        m["p6_rw"] = np.ascontiguousarray(np.asarray(ins["L1_ffn_w_router"], f32).T)
        m["p6_wg"] = np.ascontiguousarray(np.asarray(ins["L1_ffn_wg"], f32))
        m["p6_wu"] = np.ascontiguousarray(np.asarray(ins["L1_ffn_wu"], f32))
        m["p6_wd"] = np.ascontiguousarray(np.asarray(ins["L1_ffn_wd"], f32))
        m["p6_wgs"] = np.ascontiguousarray(np.asarray(ins["L1_ffn_wg_s"], f32))
        m["p6_wus"] = np.ascontiguousarray(np.asarray(ins["L1_ffn_wu_s"], f32))
        m["p6_wds"] = np.ascontiguousarray(np.asarray(ins["L1_ffn_wd_s"], f32))
        in_maps.append(m)
    res = run_bass_kernel_spmd(nc, in_maps, core_ids=list(range(NCORES)))
    out = np.zeros((B, C, T), np.float32)
    for core in range(NCORES):
        b, h = core // 2, core % 2
        out[b, :, h * HT:(h + 1) * HT] = res.results[core]["p6_o"]
    return out


_FUSED_NC = None


_CONSTS = None


def _get_consts():
    global _CONSTS
    if _CONSTS is None:
        _CONSTS = consts_host()
    return _CONSTS


def run_phase1(xf, rw_vec, rb):
    co = _get_consts()
    nc = build_phase1()
    xrevf = xf[:, :, ::-1]
    in_maps = []
    for core in range(NCORES):
        b, h = core // 2, core % 2
        m = dict(co)
        m["p1_x"] = np.ascontiguousarray(xf[b])
        m["p1_xh"] = np.ascontiguousarray(xf[b, :, h * HT:(h + 1) * HT])
        m["p1_xrev"] = np.ascontiguousarray(xrevf[b, :, h * HT:(h + 1) * HT])
        m["p1_rw"] = host_vec8(rw_vec)
        m["p1_rb"] = np.full((128, 1), float(rb), np.float32)
        m["p1_hsel"] = np.array([[1.0 - h, float(h)]], np.float32)
        in_maps.append(m)
    res = run_bass_kernel_spmd(nc, in_maps, core_ids=list(range(NCORES)))
    x1 = np.zeros((B, C, T), np.float32)
    for core in range(NCORES):
        b, h = core // 2, core % 2
        x1[b, :, h * HT:(h + 1) * HT] = res.results[core]["p1_x1"]
    return x1


_MOE_NC = {}


def run_moe_phase(tag, E, IH, CAP, renorm, xin, rw, wg, wu, wd,
                  wgs=None, wus=None, wds=None):
    co = _get_consts()
    key = (tag, E, IH, CAP, renorm, wgs is not None)
    if key not in _MOE_NC:
        _MOE_NC[key] = build_moe_phase(tag, E, IH, CAP, renorm, wgs is not None)
    nc = _MOE_NC[key]
    rw = np.ascontiguousarray(np.asarray(rw, np.float32))
    wg = np.ascontiguousarray(np.asarray(wg, np.float32))
    wu = np.ascontiguousarray(np.asarray(wu, np.float32))
    wd = np.ascontiguousarray(np.asarray(wd, np.float32))
    in_maps = []
    for core in range(NCORES):
        b, h = core // 2, core % 2
        m = dict(co)
        m[f"{tag}_x"] = np.ascontiguousarray(xin[b, :, h * HT:(h + 1) * HT])
        m[f"{tag}_rw"] = rw
        m[f"{tag}_wg"] = wg
        m[f"{tag}_wu"] = wu
        m[f"{tag}_wd"] = wd
        if wgs is not None:
            m[f"{tag}_wgs"] = np.ascontiguousarray(np.asarray(wgs, np.float32))
            m[f"{tag}_wus"] = np.ascontiguousarray(np.asarray(wus, np.float32))
            m[f"{tag}_wds"] = np.ascontiguousarray(np.asarray(wds, np.float32))
        in_maps.append(m)
    res = run_bass_kernel_spmd(nc, in_maps, core_ids=list(range(NCORES)))
    out = np.zeros((B, C, T), np.float32)
    for core in range(NCORES):
        b, h = core // 2, core % 2
        out[b, :, h * HT:(h + 1) * HT] = res.results[core][f"{tag}_o"]
    return out


_P3_NC = None
_P4_NC = None
_P5_NC = None


def _slice_prev(arr_fm, b, h):
    if h == 0:
        return np.zeros((128, NG), np.float32)
    return host_vec8(arr_fm[b, :, h * HT - 1])


def run_phase3(x2f, ins):
    global _P3_NC
    if _P3_NC is None:
        _P3_NC = build_phase3()
    co = _get_consts()
    f32 = np.float32
    wkT = np.ascontiguousarray(np.asarray(ins["L1_attn_Wk"], f32).T)
    wvT = np.ascontiguousarray(np.asarray(ins["L1_attn_Wv"], f32).T)
    wrT = np.ascontiguousarray(np.asarray(ins["L1_attn_Wr"], f32).T)
    in_maps = []
    for core in range(NCORES):
        b, h = core // 2, core % 2
        m = dict(co)
        m["p3_x"] = np.ascontiguousarray(x2f[b, :, h * HT:(h + 1) * HT])
        m["p3_xprev"] = _slice_prev(x2f, b, h)
        m["p3_pm"] = np.array([[float(h)]], f32)
        m["p3_wk"] = wkT
        m["p3_wv"] = wvT
        m["p3_wr"] = wrT
        m["p3_l0g"] = host_vec8(ins["L1_attn_ln0_w"])
        m["p3_l0b"] = host_vec8(ins["L1_attn_ln0_b"])
        m["p3_l1g"] = host_vec8(ins["L1_attn_ln1_w"])
        m["p3_l1b"] = host_vec8(ins["L1_attn_ln1_b"])
        m["p3_tmk"] = host_vec8(ins["L1_attn_tmk"])
        m["p3_tmv"] = host_vec8(ins["L1_attn_tmv"])
        m["p3_tmr"] = host_vec8(ins["L1_attn_tmr"])
        in_maps.append(m)
    res = run_bass_kernel_spmd(nc=_P3_NC, in_maps=in_maps,
                               core_ids=list(range(NCORES)))
    kf = np.zeros((B, C, T), ml_dtypes.bfloat16)
    vf = np.zeros((B, C, T), ml_dtypes.bfloat16)
    srf = np.zeros((B, C, T), ml_dtypes.bfloat16)
    x2pf = np.zeros((B, C, T), np.float32)
    for core in range(NCORES):
        b, h = core // 2, core % 2
        sl = slice(h * HT, (h + 1) * HT)
        kf[b, :, sl] = res.results[core]["p3_k"]
        vf[b, :, sl] = res.results[core]["p3_v"]
        srf[b, :, sl] = res.results[core]["p3_sr"]
        x2pf[b, :, sl] = res.results[core]["p3_x2p"]
    return kf, vf, srf, x2pf


def run_phase4(kf, vf, ins):
    global _P4_NC
    if _P4_NC is None:
        _P4_NC = build_phase4()
    f32 = np.float32
    td = np.asarray(ins["L1_attn_time_decay"], f32)
    tf = np.asarray(ins["L1_attn_time_first"], f32)
    in_maps = []
    for core in range(NCORES):
        b, ch = core // 2, core % 2
        sl = slice(ch * 512, (ch + 1) * 512)
        m = {}
        m["p4_k"] = np.ascontiguousarray(kf[b, sl, :])
        m["p4_v"] = np.ascontiguousarray(vf[b, sl, :])
        m["p4_td"] = np.ascontiguousarray(td[sl].reshape(4, 128).T)
        m["p4_tf"] = np.ascontiguousarray(tf[sl].reshape(4, 128).T)
        in_maps.append(m)
    res = run_bass_kernel_spmd(nc=_P4_NC, in_maps=in_maps,
                               core_ids=list(range(NCORES)))
    att = np.zeros((B, C, T), ml_dtypes.bfloat16)
    for core in range(NCORES):
        b, ch = core // 2, core % 2
        att[b, ch * 512:(ch + 1) * 512, :] = res.results[core]["p4_att"]
    return att


def run_phase5(x2f, x2pf, srf, att, ins):
    global _P5_NC
    if _P5_NC is None:
        _P5_NC = build_phase5()
    co = _get_consts()
    f32 = np.float32
    woT = np.ascontiguousarray(np.asarray(ins["L1_attn_Wo"], f32).T)
    wfrT = np.ascontiguousarray(np.asarray(ins["L1_attn_Wfr"], f32).T)
    wfkT = np.ascontiguousarray(np.asarray(ins["L1_attn_Wfk"], f32).T)
    wfvT = np.ascontiguousarray(np.asarray(ins["L1_attn_Wfv"], f32).T)
    srf32 = np.asarray(srf, f32)
    att32 = np.asarray(att, f32)
    in_maps = []
    for core in range(NCORES):
        b, h = core // 2, core % 2
        sl = slice(h * HT, (h + 1) * HT)
        m = dict(co)
        m["p5_x2"] = np.ascontiguousarray(x2f[b, :, sl])
        m["p5_x2p"] = np.ascontiguousarray(x2pf[b, :, sl])
        m["p5_sr"] = np.ascontiguousarray(srf[b, :, sl])
        m["p5_att"] = np.ascontiguousarray(att[b, :, sl])
        m["p5_attprev"] = _slice_prev(att32, b, h)
        m["p5_srprev"] = _slice_prev(srf32, b, h)
        m["p5_x2pprev"] = _slice_prev(x2pf, b, h)
        m["p5_pm"] = np.array([[float(h)]], f32)
        m["p5_wo"] = woT
        m["p5_wfr"] = wfrT
        m["p5_wfk"] = wfkT
        m["p5_wfv"] = wfvT
        m["p5_l2g"] = host_vec8(ins["L1_attn_ln2_w"])
        m["p5_l2b"] = host_vec8(ins["L1_attn_ln2_b"])
        m["p5_cmk"] = host_vec8(ins["L1_attn_cmk"])
        m["p5_cmr"] = host_vec8(ins["L1_attn_cmr"])
        in_maps.append(m)
    res = run_bass_kernel_spmd(nc=_P5_NC, in_maps=in_maps,
                               core_ids=list(range(NCORES)))
    x4 = np.zeros((B, C, T), np.float32)
    for core in range(NCORES):
        b, h = core // 2, core % 2
        x4[b, :, h * HT:(h + 1) * HT] = res.results[core]["p5_o"]
    return x4


FUSED = True


def kernel(**inputs):
    _ = (inputs["L0_attn_wq"], inputs["L0_attn_wk"], inputs["L0_attn_wv"])
    if FUSED:
        out = run_fused(inputs)
        return np.ascontiguousarray(np.swapaxes(out, -1, -2))
    xf = host_fm(inputs["L0_attn_x"])
    x1 = run_phase1(xf, np.asarray(inputs["L0_attn_router_w"])[0],
                    np.asarray(inputs["L0_attn_router_b"])[0])
    x2 = run_moe_phase("p2", 16, 1024, 384, False, x1,
                       inputs["L0_ffn_router_w"], inputs["L0_ffn_gate_w"],
                       inputs["L0_ffn_up_w"], inputs["L0_ffn_down_w"])
    kf, vf, srf, x2pf = run_phase3(x2, inputs)
    att = run_phase4(kf, vf, inputs)
    x4 = run_phase5(x2, x2pf, srf, att, inputs)
    x5 = run_moe_phase("p6", 8, 1408, 768, True, x4,
                       np.asarray(inputs["L1_ffn_w_router"], np.float32).T,
                       inputs["L1_ffn_wg"], inputs["L1_ffn_wu"],
                       inputs["L1_ffn_wd"], wgs=inputs["L1_ffn_wg_s"],
                       wus=inputs["L1_ffn_wu_s"], wds=inputs["L1_ffn_wd_s"])
    return np.ascontiguousarray(np.swapaxes(x5, -1, -2))
